# version 16
# baseline (speedup 1.0000x reference)
import math
from contextlib import ExitStack

import numpy as np
import ml_dtypes

import concourse.bass as bass
import concourse.mybir as mybir
from concourse.bass_utils import run_bass_kernel_spmd
from concourse.aot_env import allow_ici_all_to_all

F32 = mybir.dt.float32
BF16 = mybir.dt.bfloat16
AF = mybir.ActivationFunctionType
ALU = mybir.AluOpType
AX = mybir.AxisListType

ENGINES = ("sync", "tensor", "vector", "scalar", "gpsimd")


def _free_val_regs(nc, eng, n0):
    import bass_rust as _br
    freed = set()
    for nm in nc.inst_names()[n0:]:
        ins = nc.get_inst_by_name(nm)
        for arg in list(ins.ins) + list(ins.outs):
            rr = getattr(arg, "regref", None)
            if rr and "_val_" in rr and rr not in freed:
                nc.free_register(_br.RegisterHandle(name=rr, engine=ins.engine))
                freed.add(rr)
    eng.get_value_cache().barrier()
NEG_BIG = -1.0e30


class CFG:
    def __init__(self, D=2048, SEQ=4096, HC=8, F=1408, FS=2816, n_cores=8):
        self.D = D
        self.SEQ = SEQ
        self.HC = HC
        self.F = F
        self.FS = FS
        self.E = 8
        self.n_cores = n_cores
        self.OWN = SEQ // 2
        self.DC = D // 128
        self.QKW = HC * 128
        self.TT = min(256, self.SEQ)
        self.QT = min(512, self.SEQ)
        self.NKC = SEQ // 128
        self.NQT = SEQ // self.QT
        self.VR = self.QT // 128
        self.FC = F // 128
        self.FSC = FS // 128
        self.MT = min(512, self.OWN)
        self.NMT = self.OWN // self.MT
        self.TH = 2 if self.OWN > 1024 else 1
        self.MH = self.OWN // self.TH
        self.MQ = min(512, self.MH)
        self.NMQ = self.MH // self.MQ
        self.QKT = min(512, SEQ)
        self.NQK = SEQ // self.QKT
        self.NT2 = SEQ // self.QT
        self.SPARSE = True
        self.NTILE_R = self.OWN // 128 + 8
        self.NSLOT_R = self.NTILE_R * 128
        assert D % 128 == 0 and SEQ % self.QT == 0 and self.OWN % 128 == 0
        assert D == 2 * HC * 128, "heads*head_dim must equal d_model"
        assert F % 128 == 0 and FS % 128 == 0
        assert self.OWN % self.TT == 0 and SEQ % self.TT == 0
        assert self.MH % 128 == 0


def _fc_groups(n, gmax=6):
    out, i = [], 0
    while i < n:
        g = min(gmax, n - i)
        out.append(list(range(i, i + g)))
        i += g
    return out


class Ring:

    def __init__(self, slots):
        self.s = list(slots)
        self.tok = [None] * len(self.s)
        self.k = 0

    def get(self):
        i = self.k % len(self.s)
        self.k += 1
        return i, self.s[i], self.tok[i]

    def set_tok(self, i, tok):
        self.tok[i] = tok


def build(cfg: CFG):
    nc = bass.Bass("TRN2", target_bir_lowering=False)
    c = cfg
    DC, HC, SEQ, OWN, TT, QT = c.DC, c.HC, c.SEQ, c.OWN, c.TT, c.QT
    INV_SQRT_HD = 1.0 / math.sqrt(128.0)

    d_xT = nc.dram_tensor("xT", [c.D, SEQ], F32, kind="ExternalInput")
    d_xownT = nc.dram_tensor("xownT", [c.D, OWN], F32, kind="ExternalInput")
    d_wqkvT = nc.dram_tensor("wqkvT", [c.D, 3 * c.QKW], F32, kind="ExternalInput")
    d_wprojT = nc.dram_tensor("wprojT", [c.D, c.D], F32, kind="ExternalInput")
    d_ropeC = nc.dram_tensor("ropeC", [128, SEQ], F32, kind="ExternalInput")
    d_ropeS = nc.dram_tensor("ropeS", [128, SEQ], F32, kind="ExternalInput")
    d_mask = nc.dram_tensor("maskdiag", [128, c.VR * QT], F32, kind="ExternalInput")
    d_gwT = nc.dram_tensor("gwT", [c.D, c.E], F32, kind="ExternalInput")
    d_wgT = nc.dram_tensor("wgT", [c.E, c.D, c.F], F32, kind="ExternalInput")
    d_wuT = nc.dram_tensor("wuT", [c.E, c.D, c.F], F32, kind="ExternalInput")
    d_wdT = nc.dram_tensor("wdT", [c.E, c.F, c.D], F32, kind="ExternalInput")
    d_swgT = nc.dram_tensor("swgT", [c.D, c.FS], F32, kind="ExternalInput")
    d_swuT = nc.dram_tensor("swuT", [c.D, c.FS], F32, kind="ExternalInput")
    d_swdT = nc.dram_tensor("swdT", [c.FS, c.D], F32, kind="ExternalInput")
    d_onesbf = nc.dram_tensor("onesbf", [128, 128], BF16, kind="ExternalInput")
    d_onesf = nc.dram_tensor("onesf", [1, 128], F32, kind="ExternalInput")
    d_constf = nc.dram_tensor("constf", [128, 8], F32, kind="ExternalInput")
    d_idf = nc.dram_tensor("idf", [128, 128], F32, kind="ExternalInput")
    d_out = nc.dram_tensor("outT", [c.D, OWN], F32, kind="ExternalOutput")

    d_yidx = nc.dram_tensor("yidx", [128, DC], mybir.dt.int32, kind="ExternalInput")
    d_ltriT = nc.dram_tensor("ltriT", [128, 128], BF16, kind="ExternalInput")
    d_iotaT = nc.dram_tensor("iotaT", [1, c.NTILE_R], F32, kind="ExternalInput")
    d_iotaP = nc.dram_tensor("iotaP", [128, 1], F32, kind="ExternalInput")
    d_idbf = nc.dram_tensor("idbf", [128, 128], BF16, kind="ExternalInput")
    d_wgbf = nc.dram_tensor("wgbf", [c.E * 128, (c.D // 128) * c.F], BF16, kind="Internal")
    d_wubf = nc.dram_tensor("wubf", [c.E * 128, (c.D // 128) * c.F], BF16, kind="Internal")
    d_wdbf = nc.dram_tensor("wdbf", [c.E * 128, (c.F // 128) * c.D], BF16, kind="Internal")
    d_h2fm = nc.dram_tensor("h2fm", [DC, 128, OWN], BF16, kind="Internal")
    d_h2tm = nc.dram_tensor("h2tm", [OWN, c.D], BF16, kind="Internal")
    d_idxf = [nc.dram_tensor(f"idxf{r}", [c.NSLOT_R, 1], F32, kind="Internal")
              for r in range(2)]
    d_cwsf = [nc.dram_tensor(f"cwsf{r}", [c.NSLOT_R, 1], F32, kind="Internal")
              for r in range(2)]
    d_y1 = nc.dram_tensor("yr1", [OWN, c.D], BF16, kind="Internal")
    d_y2 = nc.dram_tensor("yr2", [OWN, c.D], BF16, kind="Internal")
    d_qkv = nc.dram_tensor("qkv_spill", [3 * HC, 128, SEQ], BF16, kind="Internal")
    d_x2 = nc.dram_tensor("x2_spill", [DC, 128, OWN], F32, kind="Internal")

    sched = {e: [] for e in ENGINES}
    ctr = {}
    sems = {}

    def add(e, f):
        sched[e].append(f)

    def inc(name, n=1):
        getsem(name)
        ctr[name] = ctr.get(name, 0) + n
        return ctr[name]

    def cur(name):
        return ctr.get(name, 0)

    def wait(e, sem, tgt):
        if tgt is None or tgt <= 0:
            return
        getsem(sem)
        def f(eng, sem=sem, tgt=tgt):
            eng.wait_ge(sems[sem], tgt)
        add(e, f)

    def wait_tok(e, tok):
        if tok is not None:
            wait(e, tok[0], tok[1])

    def dma(qeng, dst, src, sem, **kw):
        getsem(sem)
        tgt = inc(sem, 16)
        def f(eng, dst=dst, src=src, sem=sem, kw=kw):
            eng.dma_start(out=dst, in_=src, **kw).then_inc(sems[sem], 16)
        add(qeng, f)
        return tgt

    class DmaRing:

        def __init__(self, name, slots):
            self.name = name
            self.s = list(slots)
            self.tok = [None] * len(self.s)
            self.k = 0
            for i in range(len(self.s)):
                getsem(f"{name}{i}")

        def get(self):
            i = self.k % len(self.s)
            self.k += 1
            return i, self.s[i], self.tok[i]

        def sem(self, i):
            return f"{self.name}{i}"

        def set_tok(self, i, tok):
            self.tok[i] = tok

    bar_n = [0]
    bar_snap = [{}]

    def barrier(exclude=()):
        getsem("bar")
        bar_n[0] += 1
        tgt = 5 * bar_n[0]
        snap = dict(ctr)
        changed = [(s, v) for s, v in snap.items()
                   if s != "bar" and s not in exclude
                   and v != bar_snap[0].get(s, 0)]
        for s in exclude:
            if s in snap:
                snap[s] = bar_snap[0].get(s, 0)
        bar_snap[0] = snap
        for e in ENGINES:
            for s, v in changed:
                wait(e, s, v)
            def f(eng, tgt=tgt):
                eng.sem_inc(sems["bar"], 1)
                eng.wait_ge(sems["bar"], tgt)
            add(e, f)

    st = ExitStack()

    def getsem(name):
        if name not in sems:
            sems[name] = st.enter_context(nc.semaphore(name))
        return name

    with st, allow_ici_all_to_all():

        sb_ones_bf = st.enter_context(nc.sbuf_tensor("ones_bf", [128, 128], BF16))
        sb_ones_f = st.enter_context(nc.sbuf_tensor("ones_f", [1, 128], F32))
        sb_cst = st.enter_context(nc.sbuf_tensor("cstf", [128, 8], F32))
        dma("sync", sb_ones_bf[:, :], d_onesbf[:, :], "const")
        dma("sync", sb_ones_f[:, :], d_onesf[:, :], "const")
        dma("sync", sb_cst[:, :], d_constf[:, :], "const")
        CONST_VALS = [0.0, 1.0, 1e-5, 1.0 / 128.0, INV_SQRT_HD]
        for ci, val in enumerate(CONST_VALS):
            nc.const_aps.register(F32, val, sb_cst[:, ci:ci + 1])
        const_base = cur("const")
        for e in ("tensor", "vector", "scalar"):
            wait(e, "const", const_base)

        _apply_tmp = [st.enter_context(nc.sbuf_tensor(f"lnt{p}", [128, TT], F32))
                      for p in range(2)]

        pump_pairs = []
        if c.SPARSE:
            for e in range(c.E):
                for dc2 in range(DC):
                    pump_pairs.append((d_wgT.ap()[e][dc2 * 128:(dc2 + 1) * 128, :],
                                       d_wgbf.ap()[e * 128:(e + 1) * 128,
                                                   dc2 * c.F:(dc2 + 1) * c.F], c.F))
                    pump_pairs.append((d_wuT.ap()[e][dc2 * 128:(dc2 + 1) * 128, :],
                                       d_wubf.ap()[e * 128:(e + 1) * 128,
                                                   dc2 * c.F:(dc2 + 1) * c.F], c.F))
                for fc in range(c.FC):
                    pump_pairs.append((d_wdT.ap()[e][fc * 128:(fc + 1) * 128, :],
                                       d_wdbf.ap()[e * 128:(e + 1) * 128,
                                                   fc * c.D:(fc + 1) * c.D], c.D))
            ph_pump = ExitStack()
            pcb = DmaRing("pcb", [ph_pump.enter_context(nc.sbuf_tensor(f"pcb{r}", [128, 2048], BF16))
                                  for r in range(2)])
        pump_i = [0]

        def pump(n):
            if not c.SPARSE:
                return
            for _ in range(n):
                if pump_i[0] >= len(pump_pairs):
                    return
                srcap, dstap, wdt = pump_pairs[pump_i[0]]
                pump_i[0] += 1
                i, buf, tok = pcb.get()
                wait_tok("gpsimd", tok)
                dma("gpsimd", buf[:, 0:wdt], srcap, pcb.sem(i))
                wait("gpsimd", pcb.sem(i), cur(pcb.sem(i)))
                sp = dma("gpsimd", dstap, buf[:, 0:wdt], f"ost{i}")
                pcb.set_tok(i, (f"ost{i}", sp))

        def pump_done_waits(e):
            if c.SPARSE:
                assert pump_i[0] >= len(pump_pairs), "precast pump unfinished"
                wait(e, "ost0", cur("ost0"))
                wait(e, "ost1", cur("ost1"))

        def emit_ln(px, NT, src_ap, out_T, ph):
            TTL = TT
            xbuf = [[ph.enter_context(nc.sbuf_tensor(f"{px}xb{p}_{i}", [128, TTL], F32))
                     for i in range(DC)] for p in range(2)]
            xbf = [ph.enter_context(nc.sbuf_tensor(f"{px}xbf{p}", [128, TTL], BF16)) for p in range(2)]
            xsq = [ph.enter_context(nc.sbuf_tensor(f"{px}xsq{p}", [128, TTL], BF16)) for p in range(2)]
            sml = ph.enter_context(nc.sbuf_tensor(f"{px}sml", [1, 12 * TTL], F32))
            ps_sm = [ph.enter_context(nc.psum_tensor(f"{px}psm{p}", [1, TTL], F32)) for p in range(2)]
            ps_sq = [ph.enter_context(nc.psum_tensor(f"{px}psq{p}", [1, TTL], F32)) for p in range(2)]
            ps_bc = [ph.enter_context(nc.psum_tensor(f"{px}bc{p}", [128, 2 * TTL], F32)) for p in range(2)]
            s_x, s_pre, s_stat, s_sml, s_bc, s_apl = ("ln_x", "ln_pre",
                "ln_stat", "ln_sml", "ln_bc", "ln_apl")
            b_stat, b_sml, b_apl = cur(s_stat), cur(s_sml), cur(s_apl)
            invD = 1.0 / c.D
            for t in range(NT):
                p = t % 2
                s_xp = f"{s_x}{p}"
                if t >= 2:
                    wait("sync", s_apl, b_apl + (t - 1) * DC)
                for i in range(DC):
                    dma("sync", xbuf[p][i][:, :], src_ap(i, t), s_xp)
                ld = cur(s_xp)
                wait("scalar", s_xp, ld)
                for i in range(DC):
                    k = t * DC + i
                    if k >= 2:
                        wait("scalar", s_stat, b_stat + k - 1)
                    sc = inc("sch", 1)
                    def f(eng, i=i, p=p, k=k, sc=sc):
                        eng.copy(out=xbf[k % 2][:, :], in_=xbuf[p][i][:, :]).then_inc(sems["sch"], 1)
                        eng.wait_ge(sems["sch"], sc)
                        a = eng.activation(out=xsq[k % 2][:, :], in_=xbuf[p][i][:, :],
                                           func=AF.Square)
                        a.then_inc(sems[s_pre], 1)
                    add("scalar", f)
                    inc(s_pre, 1)
                    wait("tensor", s_pre, cur(s_pre))
                    if i == 0 and t >= 2:
                        wait("tensor", s_sml, b_sml + 3 * (t - 2) + 1)
                    def f(eng, i=i, p=p, k=k):
                        eng.matmul(ps_sm[p][0:1, :], sb_ones_bf[:, 0:1], xbf[k % 2][:, :],
                                   start=(i == 0), stop=(i == DC - 1))
                        mm = eng.matmul(ps_sq[p][0:1, :], sb_ones_bf[:, 0:1], xsq[k % 2][:, :],
                                        start=(i == 0), stop=(i == DC - 1))
                        mm.then_inc(sems[s_stat], 1)
                    add("tensor", f)
                    inc(s_stat, 1)
                o = 6 * p
                S_mean = sml[0:1, (o + 0) * TTL:(o + 1) * TTL]
                S_msq = sml[0:1, (o + 1) * TTL:(o + 2) * TTL]
                S_var = sml[0:1, (o + 2) * TTL:(o + 3) * TTL]
                S_sd = sml[0:1, (o + 3) * TTL:(o + 4) * TTL]
                S_rstd = sml[0:1, (o + 4) * TTL:(o + 5) * TTL]
                S_mrs = sml[0:1, (o + 5) * TTL:(o + 6) * TTL]
                wait("vector", s_stat, b_stat + (t + 1) * DC)
                c1 = inc("vch", 2)
                def f(eng, p=p, S_mean=S_mean, S_msq=S_msq, S_var=S_var, c1=c1):
                    eng.tensor_scalar_mul(S_mean, ps_sm[p][0:1, :], invD).then_inc(sems["vch"], 1)
                    eng.tensor_scalar_mul(S_var, ps_sq[p][0:1, :], invD).then_inc(sems["vch"], 1)
                    eng.wait_ge(sems["vch"], c1)
                    eng.tensor_mul(S_msq, S_mean, S_mean).then_inc(sems["vch"], 1)
                    eng.wait_ge(sems["vch"], c1 + 1)
                    i0 = eng.tensor_sub(S_var, S_var, S_msq)
                    i0.then_inc(sems[s_sml], 1)
                add("vector", f)
                inc("vch", 1)
                v1 = inc(s_sml, 1)
                wait("scalar", s_sml, v1)
                def f(eng, S_sd=S_sd, S_var=S_var):
                    a = eng.activation(out=S_sd, in_=S_var, func=AF.Sqrt, bias=1e-5)
                    a.then_inc(sems[s_sml], 1)
                add("scalar", f)
                v2 = inc(s_sml, 1)
                wait("vector", s_sml, v2)
                c2 = inc("vch", 2)
                def f(eng, S_sd=S_sd, S_rstd=S_rstd, S_mrs=S_mrs, S_mean=S_mean, c2=c2):
                    eng.reciprocal(S_rstd, S_sd).then_inc(sems["vch"], 1)
                    eng.wait_ge(sems["vch"], c2 - 1)
                    eng.tensor_mul(S_mrs, S_mean, S_rstd).then_inc(sems["vch"], 1)
                    eng.wait_ge(sems["vch"], c2)
                    i0 = eng.tensor_scalar_mul(S_mrs, S_mrs, -1.0)
                    i0.then_inc(sems[s_sml], 1)
                add("vector", f)
                v3 = inc(s_sml, 1)
                wait("tensor", s_sml, v3)
                if t >= 2:
                    wait("tensor", s_apl, b_apl + (t - 1) * DC)
                def f(eng, p=p, S_rstd=S_rstd, S_mrs=S_mrs):
                    eng.matmul(ps_bc[p][:, 0:TTL], sb_ones_f[0:1, :], S_rstd,
                               start=True, stop=True)
                    mm = eng.matmul(ps_bc[p][:, TTL:2 * TTL], sb_ones_f[0:1, :], S_mrs,
                                    start=True, stop=True)
                    mm.then_inc(sems[s_bc], 1)
                add("tensor", f)
                btgt = inc(s_bc, 1)
                wait("vector", s_bc, btgt)
                for i in range(DC):
                    apl_pre = cur(s_apl)
                    mc = inc("vch", 1)
                    def f(eng, i=i, p=p, t=t, apl_pre=apl_pre, mc=mc):
                        if apl_pre:
                            eng.wait_ge(sems[s_apl], apl_pre)
                        eng.tensor_mul(_apply_tmp[p][:, :], xbuf[p][i][:, :],
                                       ps_bc[p][:, 0:TTL]).then_inc(sems["vch"], 1)
                        eng.wait_ge(sems["vch"], mc)
                        i1 = eng.tensor_add(out_T[i][:, t * TTL:(t + 1) * TTL],
                                            _apply_tmp[p][:, :], ps_bc[p][:, TTL:2 * TTL])
                        i1.then_inc(sems[s_apl], 1)
                    add("vector", f)
                    inc(s_apl, 1)

        ph_a = ExitStack()
        sb_xnT = [ph_a.enter_context(nc.sbuf_tensor(f"xnT{i}", [128, SEQ], BF16))
                  for i in range(DC)]

        with ExitStack() as ph:
            emit_ln("ln1", SEQ // TT,
                    lambda i, t: d_xT[i * 128:(i + 1) * 128, t * TT:(t + 1) * TT],
                    sb_xnT, ph)
            barrier()

        with ExitStack() as ph:
            sb_ropeC = ph.enter_context(nc.sbuf_tensor("ropeCs", [128, SEQ], F32))
            sb_ropeS = ph.enter_context(nc.sbuf_tensor("ropeSs", [128, SEQ], F32))
            dma("sync", sb_ropeC[:, :], d_ropeC[:, :], "ropeld")
            dma("sync", sb_ropeS[:, :], d_ropeS[:, :], "ropeld")
            rope_tgt = cur("ropeld")
            wsb = DmaRing("wsb", [ph.enter_context(nc.sbuf_tensor(f"wqkv{r}", [128, DC * 128], BF16))
                                  for r in range(4)])
            stg = Ring([ph.enter_context(nc.sbuf_tensor(f"qstg{r}", [128, QT], BF16))
                        for r in range(4)])
            rt1 = [ph.enter_context(nc.sbuf_tensor(f"ropet1{p}", [128, QT], F32)) for p in range(2)]
            rt2 = [ph.enter_context(nc.sbuf_tensor(f"ropet2{p}", [128, QT], F32)) for p in range(2)]
            ps_qk = Ring([ph.enter_context(nc.psum_tensor(f"psqk{r}", [128, c.QKT], F32))
                          for r in range(2)])
            wq3 = d_wqkvT.ap().rearrange("(i p) f -> p i f", p=128)
            wait("vector", "ropeld", rope_tgt)

            NQK = c.NQK
            HW = c.QKT // QT
            order = []
            for h in range(HC):
                order += [(0, h), (1, h), (2, h)]
            for sidx, (kind, h) in enumerate(order):
                col0 = kind * c.QKW + h * 128
                wi, wbuf, wtok = wsb.get()
                wait_tok("gpsimd", wtok)
                dma("gpsimd", wbuf[:, :].rearrange("p (i f) -> p i f", f=128),
                    wq3[:, :, col0:col0 + 128], wsb.sem(wi))
                wld = cur(wsb.sem(wi))
                wsb.set_tok(wi, ("qk_mm", (sidx + 1) * NQK))
                for tt in range(NQK):
                    pi, psb, ptok = ps_qk.get()
                    if tt == 0:
                        wait("tensor", wsb.sem(wi), wld)
                    wait_tok("tensor", ptok)
                    def f(eng, wbuf=wbuf, psb=psb, tt=tt):
                        for i in range(DC):
                            mm = eng.matmul(psb[:, :], wbuf[:, i * 128:(i + 1) * 128],
                                            sb_xnT[i][:, tt * c.QKT:(tt + 1) * c.QKT],
                                            start=(i == 0), stop=(i == DC - 1))
                        mm.then_inc(sems["qk_mm"], 1)
                    add("tensor", f)
                    mmt = inc("qk_mm", 1)
                    ptok_new = None
                    for hw in range(HW):
                        gcol = tt * c.QKT + hw * QT
                        gi = (sidx * NQK + tt) * HW + hw
                        si, sbuf_, stok = stg.get()
                        if kind in (0, 1):
                            if hw == 0:
                                wait("vector", "qk_mm", mmt)
                            wait_tok("vector", stok)
                            p2 = gi % 2
                            wait("vector", "qk_stage", cur("qk_stage"))
                            vc = inc("vch", 2)
                            cur_evac = inc("qk_evac", 1)
                            def f(eng, psb=psb, p2=p2, hw=hw, gcol=gcol, sbuf_=sbuf_,
                                  vc=vc, cur_evac=cur_evac):
                                hs = hw * QT
                                eng.tensor_mul(rt1[p2][:, :], psb[:, hs:hs + QT],
                                               sb_ropeC[:, gcol:gcol + QT]).then_inc(sems["vch"], 1)
                                eng.tensor_mul(rt2[p2][0:64, :], psb[64:128, hs:hs + QT],
                                               sb_ropeS[0:64, gcol:gcol + QT]).then_inc(sems["vch"], 1)
                                eng.wait_ge(sems["vch"], vc)
                                i2 = eng.tensor_mul(rt2[p2][64:128, :], psb[0:64, hs:hs + QT],
                                                    sb_ropeS[64:128, gcol:gcol + QT])
                                i2.then_inc(sems["qk_evac"], 1)
                                eng.wait_ge(sems["qk_evac"], cur_evac)
                                i3 = eng.tensor_add(sbuf_[:, :], rt1[p2][:, :], rt2[p2][:, :])
                                i3.then_inc(sems["qk_stage"], 1)
                            add("vector", f)
                            ptok_new = ("qk_evac", cur_evac)
                            stok_sem, stgt = "qk_stage", inc("qk_stage", 1)
                        else:
                            if hw == 0:
                                wait("scalar", "qk_mm", mmt)
                            wait_tok("scalar", stok)
                            def f(eng, psb=psb, sbuf_=sbuf_, hw=hw):
                                a = eng.copy(out=sbuf_[:, :], in_=psb[:, hw * QT:(hw + 1) * QT])
                                a.then_inc(sems["qk_evacS"], 1)
                            add("scalar", f)
                            ptok_new = ("qk_evacS", inc("qk_evacS", 1))
                            stok_sem, stgt = "qk_evacS", cur("qk_evacS")
                        wait("sync", stok_sem, stgt)
                        sp = dma("sync", d_qkv[sidx, :, gcol:gcol + QT], sbuf_[:, :],
                                 f"qsp{si}")
                        stg.set_tok(si, (f"qsp{si}", sp))
                    ps_qk.set_tok(pi, ptok_new)
                pump(5)
            barrier()
        ph_a.close()

        ph_y = ExitStack()
        sb_yT = [ph_y.enter_context(nc.sbuf_tensor(f"yT{i}", [128, SEQ], BF16))
                 for i in range(HC)]
        sb_yidx = ph_y.enter_context(nc.sbuf_tensor("yidxsb", [128, DC], mybir.dt.int32))

        if HC >= 4:
            hgroups = [list(range(0, HC // 2)),
                       list(range(HC // 2, 3 * HC // 4)),
                       list(range(3 * HC // 4, HC))]
        elif HC >= 2:
            hgroups = [list(range(0, HC // 2)), list(range(HC // 2, HC))]
        else:
            hgroups = [list(range(HC))]
        d_cins, d_couts = [], []
        for g, hg in enumerate(hgroups):
            sz = c.n_cores * len(hg) * 128
            d_cins.append(nc.dram_tensor(f"cc_in{g}", [sz, OWN], BF16, kind="Internal"))
            d_couts.append(nc.dram_tensor(f"cc_out{g}", [sz, OWN], BF16, kind="Internal"))

        def yf_ap(i):
            p = i // HC
            return sb_yT[i % HC][:, p * OWN:(p + 1) * OWN]

        gat_tot = []

        def emit_exchange_cc(g):
            hg = hgroups[g]
            nhg = len(hg)
            wait("gpsimd", "a_y", (hg[-1] + 1) * c.NQT)
            for d in range(c.n_cores):
                rho = d % 2
                for jj, i in enumerate(hg):
                    dma("gpsimd",
                        d_cins[g][(d * nhg + jj) * 128:(d * nhg + jj + 1) * 128, :],
                        sb_yT[i][:, rho * OWN:(rho + 1) * OWN], "cc_wr")
            wait("gpsimd", "cc_wr", cur("cc_wr"))
            def f(eng, g=g):
                cc = eng.collective_compute(
                    "AllToAll", ALU.bypass,
                    replica_groups=[list(range(c.n_cores))],
                    ins=[d_cins[g].ap().opt()], outs=[d_couts[g].ap().opt()])
                cc.then_inc(sems["cc_done"])
            add("gpsimd", f)
            inc("cc_done", 1)

        def emit_gather(g):
            hg = hgroups[g]
            sem = f"ygat{g % 2}"
            getsem(sem)
            wait("gpsimd", "cc_done", g + 1)
            wait("gpsimd", "yixld", cur("yixld"))
            chunks = [p * HC + lh for p in range(2) for lh in hg]
            for i in chunks:
                dst = yf_ap(i)
                inc(sem, 16)
                def f(eng, dst=dst, i=i, g=g, sem=sem):
                    n0 = nc.n_instructions()
                    dd = eng.indirect_dma_start(
                        out=dst, out_offset=None,
                        in_=d_couts[g].ap(),
                        in_offset=bass.IndirectOffsetOnAxis(
                            ap=sb_yidx[:, i:i + 1], axis=0))
                    dd.then_inc(sems[sem], 16)
                    _free_val_regs(nc, eng, n0)
                add("gpsimd", f)
            gat_tot.append((sem, cur(sem)))

        with ExitStack() as ph:
            sb_mask = ph.enter_context(nc.sbuf_tensor("maskd", [128, c.VR * QT], F32))
            dma("sync", sb_mask[:, :], d_mask[:, :], "mskld")
            dma("sync", sb_yidx[:, :], d_yidx[:, :], "yixld")
            wait("vector", "mskld", cur("mskld"))
            for r in range(4):
                wait("sync", f"qsp{r}", cur(f"qsp{r}"))

            hb = DmaRing("ahl", [
                {"q": ph.enter_context(nc.sbuf_tensor(f"qh{r}", [128, SEQ], BF16)),
                 "k": ph.enter_context(nc.sbuf_tensor(f"kh{r}", [128, SEQ], BF16)),
                 "v": ph.enter_context(nc.sbuf_tensor(f"vh{r}", [128, SEQ], BF16))}
                for r in range(2)])
            nsq = Ring([ph.enter_context(nc.sbuf_tensor(f"nsq{r}", [128, QT], BF16))
                        for r in range(2)])
            nrow = ph.enter_context(nc.sbuf_tensor("nrow", [1, 2 * c.NT2], F32))
            nsml = ph.enter_context(nc.sbuf_tensor("nsml", [1, 8], F32))
            biasA = [ph.enter_context(nc.sbuf_tensor(f"biasA{r}", [128, 1], F32))
                     for r in range(2)]
            ptb = Ring([ph.enter_context(nc.sbuf_tensor(f"pt{r}", [128, QT], BF16))
                        for r in range(6)])
            smb = Ring([ph.enter_context(nc.sbuf_tensor(f"smb{r}", [128, QT], F32))
                        for r in range(5)])
            rbs = ph.enter_context(nc.sbuf_tensor("rbs", [128, QT], F32))
            rcp = ph.enter_context(nc.sbuf_tensor("rcp", [1, QT], F32))
            ps_s = Ring([ph.enter_context(nc.psum_tensor(f"pss{r}", [128, QT], F32))
                         for r in range(4)])
            ps_o = Ring([ph.enter_context(nc.psum_tensor(f"pso{r}", [128, QT], F32))
                         for r in range(2)])
            ps_sum = ph.enter_context(nc.psum_tensor("pssum", [1, QT], F32))
            ps_rb = ph.enter_context(nc.psum_tensor("psrb", [128, QT], F32))

            ps_sum_tok = None
            ps_rb_tok = None
            rbs_tok = None
            rcp_tok = None
            exps_at_head_end = []

            for h in range(HC):
                bi, bufs, btok = hb.get()
                hsem = hb.sem(bi)
                wait_tok("sync", btok)
                dma("sync", bufs["q"][:, :], d_qkv[3 * h + 0, :, :], hsem)
                dma("sync", bufs["k"][:, :], d_qkv[3 * h + 1, :, :], hsem)
                for kc in range(c.NKC):
                    dma("sync", bufs["v"][:, kc * 128:(kc + 1) * 128],
                        d_qkv[3 * h + 2, :, kc * 128:(kc + 1) * 128], hsem,
                        transpose=True)
                ldt = cur(hsem)

                wait("scalar", hsem, ldt)
                wait("vector", "a_nrd", cur("a_nrd"))
                for pidx, part in enumerate(("q", "k")):
                    for j in range(c.NT2):
                        g = pidx * c.NT2 + j
                        qi, qb, qtok = nsq.get()
                        wait_tok("scalar", qtok)
                        def f(eng, qb=qb, part=part, j=j, bufs=bufs):
                            a = eng.activation(out=qb[:, :],
                                               in_=bufs[part][:, j * QT:(j + 1) * QT],
                                               func=AF.Square)
                            a.then_inc(sems["a_nrm"], 1)
                        add("scalar", f)
                        nt = inc("a_nrm", 1)
                        nsq.set_tok(qi, ("a_nmm", cur("a_nmm") + 1))
                        if g % 2 == 0:
                            nb, ntok = ps_sum, ps_sum_tok
                        else:
                            nb, ntok = ps_rb, ps_rb_tok
                        wait("tensor", "a_nrm", nt)
                        wait_tok("tensor", ntok)
                        def f(eng, nb=nb, qb=qb):
                            mm = eng.matmul(nb[0:1, :], sb_ones_bf[:, 0:1], qb[:, :],
                                            start=True, stop=True)
                            mm.then_inc(sems["a_nmm"], 1)
                        add("tensor", f)
                        nmm = inc("a_nmm", 1)
                        wait("vector", "a_nmm", nmm)
                        def f(eng, nb=nb, g=g):
                            r = eng.reduce_max(nrow[0:1, g:g + 1], nb[0:1, :], axis=AX.X)
                            r.then_inc(sems["a_nrd"], 1)
                        add("vector", f)
                        ntok2 = ("a_nrd", inc("a_nrd", 1))
                        if g % 2 == 0:
                            ps_sum_tok = ntok2
                        else:
                            ps_rb_tok = ntok2
                wait("vector", "a_nrd", cur("a_nrd"))
                wait("vector", "a_nrm", cur("a_nrm"))
                vc = inc("vch", 2)
                def f(eng, vc=vc):
                    eng.reduce_max(nsml[0:1, 0:1], nrow[0:1, 0:c.NT2], axis=AX.X).then_inc(sems["vch"], 1)
                    eng.reduce_max(nsml[0:1, 1:2], nrow[0:1, c.NT2:2 * c.NT2], axis=AX.X).then_inc(sems["vch"], 1)
                    eng.wait_ge(sems["vch"], vc)
                    i0 = eng.tensor_mul(nsml[0:1, 2:3], nsml[0:1, 0:1], nsml[0:1, 1:2])
                    i0.then_inc(sems["a_nrd"], 1)
                add("vector", f)
                nprod = inc("a_nrd", 1)
                wait("scalar", "a_nrd", nprod)
                def f(eng):
                    a = eng.activation(out=nsml[0:1, 3:4], in_=nsml[0:1, 2:3],
                                       func=AF.Sqrt, scale=1.0 / 128.0)
                    a.then_inc(sems["a_nrm"], 1)
                add("scalar", f)
                nb_t = inc("a_nrm", 1)
                wait("vector", "a_nrm", nb_t)
                wait("vector", "a_nmm", cur("a_nmm"))
                def f(eng):
                    i0 = eng.tensor_scalar_mul(nsml[0:1, 4:5], nsml[0:1, 3:4], -1.0)
                    i0.then_inc(sems["a_nrd"], 1)
                add("vector", f)
                negb = inc("a_nrd", 1)
                wait("tensor", "a_nrd", negb)
                wait_tok("tensor", ps_rb_tok)
                def f(eng):
                    mm = eng.matmul(ps_rb[:, 0:1], sb_ones_f[0:1, :], nsml[0:1, 4:5],
                                    start=True, stop=True)
                    mm.then_inc(sems["a_nmm"], 1)
                add("tensor", f)
                bmm = inc("a_nmm", 1)
                wait("vector", "a_nmm", bmm)
                if h >= 2:
                    wait("vector", "a_exp", exps_at_head_end[h - 2])
                bA = biasA[h % 2]
                def f(eng, bA=bA):
                    i0 = eng.tensor_copy(bA[:, :], ps_rb[:, 0:1])
                    i0.then_inc(sems["a_bias"], 1)
                add("vector", f)
                ps_rb_tok = ("a_bias", inc("a_bias", 1))
                wait("scalar", "a_bias", cur("a_bias"))

                for qt in range(c.NQT):
                    kcs = list(range(0, (qt + 1) * c.VR))
                    nkc = len(kcs)
                    SKEW = min(5, nkc)
                    kstate = {}
                    octx = {}

                    def emitA(ki, qt=qt, kcs=kcs, bufs=bufs, kstate=kstate):
                        kc = kcs[ki]
                        diag_v = kc - qt * c.VR
                        si, ssb, stok2 = ps_s.get()
                        wait_tok("tensor", stok2)
                        def f(eng, ssb=ssb, kc=kc, qt=qt, bufs=bufs):
                            mm = eng.matmul(ssb[:, :], bufs["k"][:, kc * 128:(kc + 1) * 128],
                                            bufs["q"][:, qt * QT:(qt + 1) * QT],
                                            start=True, stop=True)
                            mm.then_inc(sems["a_smm"], 1)
                        add("tensor", f)
                        smm = inc("a_smm", 1)
                        pti, ptbuf, pttok = ptb.get()
                        bA = biasA[h % 2]
                        if diag_v >= 0:
                            mi, msb, mtok = smb.get()
                            wait("vector", "a_smm", smm)
                            wait_tok("vector", mtok)
                            def f(eng, msb=msb, ssb=ssb, diag_v=diag_v):
                                i0 = eng.tensor_add(msb[:, :], ssb[:, :],
                                                    sb_mask[:, diag_v * QT:(diag_v + 1) * QT])
                                i0.then_inc(sems["a_msk"], 1)
                            add("vector", f)
                            mk = inc("a_msk", 1)
                            ps_s.set_tok(si, ("a_msk", mk))
                            wait("scalar", "a_msk", mk)
                            wait_tok("scalar", pttok)
                            def f(eng, msb=msb, ptbuf=ptbuf, bA=bA):
                                a = eng.activation(out=ptbuf[:, :], in_=msb[:, :],
                                                   func=AF.Exp, bias=bA[:, 0:1],
                                                   scale=INV_SQRT_HD)
                                a.then_inc(sems["a_exp"], 1)
                            add("scalar", f)
                            ex = inc("a_exp", 1)
                            smb.set_tok(mi, ("a_exp", ex))
                        else:
                            wait("scalar", "a_smm", smm)
                            wait_tok("scalar", pttok)
                            def f(eng, ssb=ssb, ptbuf=ptbuf, bA=bA):
                                a = eng.activation(out=ptbuf[:, :], in_=ssb[:, :],
                                                   func=AF.Exp, bias=bA[:, 0:1],
                                                   scale=INV_SQRT_HD)
                                a.then_inc(sems["a_exp"], 1)
                            add("scalar", f)
                            ex = inc("a_exp", 1)
                            ps_s.set_tok(si, ("a_exp", ex))
                        kstate[ki] = (ex, pti, ptbuf)

                    def emitB(ki, qt=qt, kcs=kcs, bufs=bufs, kstate=kstate, octx=octx):
                        nonlocal ps_sum_tok
                        kc = kcs[ki]
                        nkc = len(kcs)
                        ex, pti, ptbuf = kstate.pop(ki)
                        if ki == 0:
                            oi, osb, otok = ps_o.get()
                            wait_tok("tensor", otok)
                            wait_tok("tensor", ps_sum_tok)
                            octx["o"] = (oi, osb)
                        oi, osb = octx["o"]
                        wait("tensor", "a_exp", ex)
                        def f(eng, osb=osb, ptbuf=ptbuf, kc=kc, ki=ki, nkc=nkc, bufs=bufs):
                            eng.matmul(osb[:, :], bufs["v"][:, kc * 128:(kc + 1) * 128],
                                       ptbuf[:, :], start=(ki == 0), stop=(ki == nkc - 1))
                            mm = eng.matmul(ps_sum[0:1, :], sb_ones_bf[:, 0:1], ptbuf[:, :],
                                            start=(ki == 0), stop=(ki == nkc - 1))
                            mm.then_inc(sems["a_pv"], 1)
                        add("tensor", f)
                        pv = inc("a_pv", 1)
                        ptb.set_tok(pti, ("a_pv", pv))
                        return pv

                    for ki in range(SKEW):
                        emitA(ki)
                    for ki in range(nkc):
                        if ki + SKEW < nkc:
                            emitA(ki + SKEW)
                        pv = emitB(ki)
                    oi, osb = octx["o"]
                    wait("vector", "a_pv", pv)
                    wait_tok("vector", rcp_tok)
                    def f(eng):
                        i0 = eng.reciprocal(rcp[0:1, :], ps_sum[0:1, :])
                        i0.then_inc(sems["a_rcp"], 1)
                    add("vector", f)
                    ps_sum_tok = ("a_rcp", inc("a_rcp", 1))
                    wait("tensor", "a_rcp", ps_sum_tok[1])
                    wait_tok("tensor", ps_rb_tok)
                    def f(eng):
                        mm = eng.matmul(ps_rb[:, :], sb_ones_f[0:1, :], rcp[0:1, :],
                                        start=True, stop=True)
                        mm.then_inc(sems["a_rbc"], 1)
                    add("tensor", f)
                    rbc = inc("a_rbc", 1)
                    rcp_tok = ("a_rbc", rbc)
                    wait("scalar", "a_rbc", rbc)
                    wait_tok("scalar", rbs_tok)
                    def f(eng):
                        a = eng.copy(out=rbs[:, :], in_=ps_rb[:, :])
                        a.then_inc(sems["a_rbe"], 1)
                    add("scalar", f)
                    rbe = inc("a_rbe", 1)
                    ps_rb_tok = ("a_rbe", rbe)
                    wait("vector", "a_rbe", rbe)
                    def f(eng, h=h, qt=qt, osb=osb):
                        i0 = eng.scalar_tensor_tensor(sb_yT[h][:, qt * QT:(qt + 1) * QT],
                                                      osb[:, :], 1.0, rbs[:, :],
                                                      ALU.mult, ALU.mult)
                        i0.then_inc(sems["a_y"], 1)
                    add("vector", f)
                    ytgt = inc("a_y", 1)
                    ps_o.set_tok(oi, ("a_y", ytgt))
                    rbs_tok = ("a_y", ytgt)
                hb.set_tok(bi, ("a_y", cur("a_y")))
                exps_at_head_end.append(cur("a_exp"))
                for g2 in range(len(hgroups) - 1):
                    if h == hgroups[g2][-1]:
                        emit_exchange_cc(g2)
                        emit_gather(g2)
                pump(20)

            emit_exchange_cc(len(hgroups) - 1)
            barrier(exclude=("cc_done",))

        with ExitStack() as ph:
            emit_gather(len(hgroups) - 1)
            wpj = DmaRing("wpj", [ph.enter_context(nc.sbuf_tensor(f"wpj{r}", [128, DC * 128], BF16))
                                  for r in range(2)])
            xob = DmaRing("pxo", [ph.enter_context(nc.sbuf_tensor(f"xob{r}", [128, OWN], F32))
                                  for r in range(2)])
            stgp = Ring([ph.enter_context(nc.sbuf_tensor(f"pstg{r}", [128, c.MT], F32))
                         for r in range(4)])
            ps_p = Ring([ph.enter_context(nc.psum_tensor(f"psp{r}", [128, c.MT], F32))
                         for r in range(2)])
            wpr = d_wprojT.ap().rearrange("(i p) f -> p i f", p=128)
            csets = []
            for g, hg in enumerate(hgroups):
                csets.append([p * HC + lh for p in range(2) for lh in hg])
            for o in range(DC):
                wi, wbuf, wtok = wpj.get()
                wait_tok("gpsimd", wtok)
                dma("gpsimd", wbuf[:, :].rearrange("p (i f) -> p i f", f=128),
                    wpr[:, :, o * 128:(o + 1) * 128], wpj.sem(wi))
                wld = cur(wpj.sem(wi))
                pump(6)
                wpj.set_tok(wi, ("pj_mm", (o + 1) * c.NMT))
                xi, xbuf2, xtok = xob.get()
                wait_tok("sync", xtok)
                dma("sync", xbuf2[:, :], d_xownT[o * 128:(o + 1) * 128, :], xob.sem(xi))
                xld = cur(xob.sem(xi))
                xob.set_tok(xi, ("pj_ev", (o + 1) * c.NMT))
                for mt in range(c.NMT):
                    pi, psb, ptok = ps_p.get()
                    wait_tok("tensor", ptok)
                    if mt == 0:
                        wait("tensor", wpj.sem(wi), wld)
                    nci = 0
                    for g, iset in enumerate(csets):
                        wait("tensor", gat_tot[g][0], gat_tot[g][1])
                        first, last = nci == 0, (nci + len(iset) == DC)
                        def f(eng, o=o, mt=mt, psb=psb, wbuf=wbuf, iset=list(iset),
                              first=first, last=last):
                            for j, i in enumerate(iset):
                                mm = eng.matmul(psb[:, :], wbuf[:, i * 128:(i + 1) * 128],
                                                yf_ap(i)[:, mt * c.MT:(mt + 1) * c.MT],
                                                start=(first and j == 0),
                                                stop=(last and j == len(iset) - 1))
                            if last:
                                mm.then_inc(sems["pj_mm"], 1)
                        add("tensor", f)
                        nci += len(iset)
                    mmt = inc("pj_mm", 1)
                    si, sstg, stok2 = stgp.get()
                    wait("vector", "pj_mm", mmt)
                    if mt == 0:
                        wait("vector", xob.sem(xi), xld)
                    wait_tok("vector", stok2)
                    def f(eng, psb=psb, sstg=sstg, xbuf2=xbuf2, mt=mt):
                        i0 = eng.tensor_add(sstg[:, :], psb[:, :],
                                            xbuf2[:, mt * c.MT:(mt + 1) * c.MT])
                        i0.then_inc(sems["pj_ev"], 1)
                    add("vector", f)
                    ps_p.set_tok(pi, ("pj_ev", inc("pj_ev", 1)))
                    wait("sync", "pj_ev", cur("pj_ev"))
                    sp = dma("sync", d_x2[o, :, mt * c.MT:(mt + 1) * c.MT], sstg[:, :],
                             f"psp{si}")
                    stgp.set_tok(si, (f"psp{si}", sp))
            barrier()
        ph_y.close()

        if c.SPARSE:
            ph_pump.close()

        ph_b = ExitStack()
        ph_h2 = ExitStack()
        NTK = OWN // 128
        sb_cwtm = (None if c.SPARSE else
                   ph_b.enter_context(nc.sbuf_tensor("cwtm", [128, NTK * c.E], F32)))
        sb_etile_i = ph_b.enter_context(nc.sbuf_tensor("etileI", [1, 2 * c.NTILE_R],
                                                       mybir.dt.int32))
        sb_idxw_i = ph_b.enter_context(nc.sbuf_tensor("idxwI", [128, 2 * c.NTILE_R],
                                                      mybir.dt.int32))
        _h2scope = ph_h2 if c.SPARSE else ph_b
        sb_h2 = [_h2scope.enter_context(nc.sbuf_tensor(f"h2T{i}", [128, OWN], BF16))
                 for i in range(DC)]
        with ExitStack() as ph:
            for r in range(4):
                wait("sync", f"psp{r}", cur(f"psp{r}"))
            emit_ln("ln2", OWN // TT,
                    lambda i, t: d_x2[i, :, t * TT:(t + 1) * TT],
                    sb_h2, ph)
            barrier()

        with ExitStack() as ph:
            gwsb = ph.enter_context(nc.sbuf_tensor("gwsb", [128, DC * c.E], BF16))
            dma("gpsimd", gwsb[:, :].rearrange("p (i e) -> p i e", e=c.E),
                d_gwT.ap().rearrange("(i p) e -> p i e", p=128), "gwl")
            wait("tensor", "gwl", cur("gwl"))
            e8 = Ring([ph.enter_context(nc.sbuf_tensor(f"e8_{r}", [128, c.E], F32))
                       for r in range(2)])
            rsm = ph.enter_context(nc.sbuf_tensor("rsm", [128, 8], F32))
            cwtk = ph.enter_context(nc.sbuf_tensor("cwtk", [128, c.E], F32))
            ps_lg = Ring([ph.enter_context(nc.psum_tensor(f"pslg{r}", [128, c.E], F32))
                          for r in range(2)])
            if c.SPARSE:
                msk8 = ph.enter_context(nc.sbuf_tensor("msk8", [128, c.E], F32))
                sb_H = [ph.enter_context(nc.sbuf_tensor(f"sbH{r}", [128, NTK * c.E], BF16))
                        for r in range(2)]
                sb_w12 = ph.enter_context(nc.sbuf_tensor("sbw12", [128, 2 * NTK], F32))
                sb_cnt = ph.enter_context(nc.sbuf_tensor("sbcnt", [8, 2 * NTK], F32))
                ps_cnt = ph.enter_context(nc.psum_tensor("pscnt", [8, 2], F32))
                wait("sync", "ln_apl", cur("ln_apl"))
                for i in range(DC):
                    dma("sync", d_h2fm[i, :, :], sb_h2[i][:, :], "h2sp")
                sb_idb5 = ph.enter_context(nc.sbuf_tensor("idb5", [128, 128], BF16))
                dma("sync", sb_idb5[:, :], d_idbf[:, :], "mskld")
                idb5_t = cur("mskld")
                hstg = Ring([ph.enter_context(nc.sbuf_tensor(f"hst{r}", [128, c.D], BF16))
                             for r in range(2)])
                ps_ht = ph.enter_context(nc.psum_tensor("psht", [128, 128], BF16))
                wait("tensor", "mskld", idb5_t)
                wait("tensor", "ln_apl", cur("ln_apl"))
                ps_ht_tok = [None]
            for tk in range(NTK):
                if c.SPARSE:
                    tb = tk
                    i7, hbuf, htok = hstg.get()
                    for ic in range(DC):
                        s7 = 0
                        psh = ps_ht[:, :]
                        wait_tok("tensor", ps_ht_tok[s7])
                        def f(eng, psh=psh, ic=ic, tb=tb):
                            mm = eng.transpose(psh, sb_h2[ic][:, tb * 128:(tb + 1) * 128],
                                               sb_idb5[:, :])
                            mm.then_inc(sems["m_tp"], 1)
                        add("tensor", f)
                        t7 = inc("m_tp", 1)
                        wait("vector", "m_tp", t7)
                        if ic == 0:
                            wait_tok("vector", htok)
                        def f(eng, psh=psh, hbuf=hbuf, ic=ic):
                            i0 = eng.tensor_copy(hbuf[:, ic * 128:(ic + 1) * 128], psh)
                            i0.then_inc(sems["m_evS"], 1)
                        add("vector", f)
                        ps_ht_tok[s7] = ("m_evS", inc("m_evS", 1))
                    wait("sync", "m_evS", cur("m_evS"))
                    sp7 = dma("sync", d_h2tm[tb * 128:(tb + 1) * 128, :], hbuf[:, :],
                              "h2sp")
                    hstg.set_tok(i7, ("h2sp", sp7))
                li, lgb, ltok = ps_lg.get()
                wait_tok("tensor", ltok)
                def f(eng, lgb=lgb, tk=tk, h2L=tuple(sb_h2)):
                    for i in range(DC):
                        mm = eng.matmul(lgb[:, :], h2L[i][:, tk * 128:(tk + 1) * 128],
                                        gwsb[:, i * c.E:(i + 1) * c.E],
                                        start=(i == 0), stop=(i == DC - 1))
                    mm.then_inc(sems["rt_mm"], 1)
                add("tensor", f)
                mmt = inc("rt_mm", 1)
                wait("vector", "rt_mm", mmt)
                wait("vector", "rt_exp", cur("rt_exp"))
                vg = inc("vch", 1)
                def f(eng, lgb=lgb, vg=vg):
                    eng.reduce_max(rsm[:, 0:1], lgb[:, :], axis=AX.X).then_inc(sems["vch"], 1)
                    eng.wait_ge(sems["vch"], vg)
                    i0 = eng.tensor_scalar_mul(rsm[:, 1:2], rsm[:, 0:1], -1.0)
                    i0.then_inc(sems["rt_cw"], 1)
                add("vector", f)
                v1 = inc("rt_cw", 1)
                wait("scalar", "rt_cw", v1)
                ei, e8b, etok = e8.get()
                wait_tok("scalar", etok)
                def f(eng, lgb=lgb, e8b=e8b):
                    a = eng.activation(out=e8b[:, :], in_=lgb[:, :], func=AF.Exp,
                                       bias=rsm[:, 1:2], scale=1.0)
                    a.then_inc(sems["rt_exp"], 1)
                add("scalar", f)
                ex = inc("rt_exp", 1)
                ps_lg.set_tok(li, ("rt_exp", ex))
                wait("vector", "rt_exp", ex)
                wait("vector", "rt_cw", cur("rt_cw"))
                if not c.SPARSE:
                    vg0 = cur("vch")
                    inc("vch", 6)
                    def f(eng, e8b=e8b, tk=tk, vg0=vg0):
                        ops = [
                            lambda: eng.tensor_scalar(cwtk[:, :], e8b[:, :], 1.0, None, ALU.is_equal),
                            lambda: eng.scalar_tensor_tensor(cwtk[:, :], cwtk[:, :], -2.0, e8b[:, :],
                                                             ALU.mult, ALU.add),
                            lambda: eng.reduce_max(rsm[:, 2:3], cwtk[:, :], axis=AX.X),
                            lambda: eng.tensor_scalar(rsm[:, 3:4], rsm[:, 2:3], 1.0, None, ALU.add),
                            lambda: eng.reciprocal(rsm[:, 4:5], rsm[:, 3:4]),
                            lambda: eng.tensor_scalar(cwtk[:, :], e8b[:, :], rsm[:, 2:3], None, ALU.is_ge),
                        ]
                        for j, op in enumerate(ops):
                            op().then_inc(sems["vch"], 1)
                            eng.wait_ge(sems["vch"], vg0 + j + 1)
                        i0 = eng.scalar_tensor_tensor(sb_cwtm[:, tk * c.E:(tk + 1) * c.E],
                                                      e8b[:, :], rsm[:, 4:5],
                                                      cwtk[:, :], ALU.mult, ALU.mult)
                        i0.then_inc(sems["rt_cw"], 1)
                    add("vector", f)
                    e8.set_tok(ei, ("rt_cw", inc("rt_cw", 1)))
                else:
                    vg0 = cur("vch")
                    inc("vch", 8)
                    H1s = sb_H[0][:, tk * c.E:(tk + 1) * c.E]
                    H2s = sb_H[1][:, tk * c.E:(tk + 1) * c.E]
                    def f(eng, e8b=e8b, tk=tk, vg0=vg0, H1s=H1s, H2s=H2s):
                        ops = [
                            lambda: eng.tensor_scalar(cwtk[:, :], e8b[:, :], 1.0, None, ALU.is_equal),
                            lambda: eng.tensor_copy(H1s, cwtk[:, :]),
                            lambda: eng.scalar_tensor_tensor(msk8[:, :], cwtk[:, :], -2.0, e8b[:, :],
                                                             ALU.mult, ALU.add),
                            lambda: eng.reduce_max(rsm[:, 2:3], msk8[:, :], axis=AX.X),
                            lambda: eng.tensor_scalar(rsm[:, 3:4], rsm[:, 2:3], 1.0, None, ALU.add),
                            lambda: eng.reciprocal(sb_w12[:, tk:tk + 1], rsm[:, 3:4]),
                            lambda: eng.tensor_scalar(msk8[:, :], e8b[:, :], rsm[:, 2:3], None, ALU.is_ge),
                            lambda: eng.tensor_sub(msk8[:, :], msk8[:, :], cwtk[:, :]),
                        ]
                        for j, op in enumerate(ops):
                            op().then_inc(sems["vch"], 1)
                            eng.wait_ge(sems["vch"], vg0 + j + 1)
                        eng.tensor_copy(H2s, msk8[:, :]).then_inc(sems["vch"], 1)
                        eng.wait_ge(sems["vch"], vg0 + 9)
                        i0 = eng.tensor_mul(sb_w12[:, NTK + tk:NTK + tk + 1],
                                            rsm[:, 2:3], sb_w12[:, tk:tk + 1])
                        i0.then_inc(sems["rt_cw"], 1)
                    add("vector", f)
                    inc("vch", 1)
                    e8.set_tok(ei, ("rt_cw", inc("rt_cw", 1)))
                    rtc = cur("rt_cw")
                    for r in range(2):
                        Hs = sb_H[r][:, tk * c.E:(tk + 1) * c.E]
                        wait("tensor", "rt_cw", rtc)
                        if tk > 0 or r > 0:
                            wait("tensor", "rt_cn", cur("rt_cn"))
                        def f(eng, Hs=Hs, r=r):
                            mm = eng.matmul(ps_cnt[:, r:r + 1], Hs, sb_ones_bf[:, 0:1],
                                            start=True, stop=True)
                            mm.then_inc(sems["rt_mm"], 1)
                        add("tensor", f)
                        mmc = inc("rt_mm", 1)
                        wait("vector", "rt_mm", mmc)
                        def f(eng, r=r, tk=tk):
                            i0 = eng.tensor_copy(sb_cnt[:, r * NTK + tk:r * NTK + tk + 1],
                                                 ps_cnt[:, r:r + 1])
                            i0.then_inc(sems["rt_cn"], 1)
                        add("vector", f)
                        inc("rt_cn", 1)
            rt_done = cur("rt_cw")

            if c.SPARSE:
                sb_ltri = ph.enter_context(nc.sbuf_tensor("ltrisb", [128, 128], BF16))
                sb_iotaT = ph.enter_context(nc.sbuf_tensor("iotaTsb", [1, c.NTILE_R], F32))
                sb_iotaP = ph.enter_context(nc.sbuf_tensor("iotaPsb", [128, 1], F32))
                sb_idf5 = ph.enter_context(nc.sbuf_tensor("idf5", [128, 128], F32))
                dma("sync", sb_ltri[:, :], d_ltriT[:, :], "mskld")
                dma("sync", sb_iotaT[:, :], d_iotaT[:, :], "mskld")
                dma("sync", sb_iotaP[:, :], d_iotaP[:, :], "mskld")
                dma("sync", sb_idf5[:, :], d_idf[:, :], "mskld")
                aux5 = cur("mskld")
                sb_one8 = ph.enter_context(nc.sbuf_tensor("one8", [8, max(NTK, 8)], F32))
                sb_cb = ph.enter_context(nc.sbuf_tensor("sbcb", [8, 2 * NTK], F32))
                sb_sml5 = ph.enter_context(nc.sbuf_tensor("sml5", [8, 8], F32))
                sb_prow = ph.enter_context(nc.sbuf_tensor("prow", [1, 32], F32))
                sb_padb = ph.enter_context(nc.sbuf_tensor("padb", [128, 16], F32))
                sb_etf = ph.enter_context(nc.sbuf_tensor("etf", [1, 2 * c.NTILE_R], F32))
                sb_selb = ph.enter_context(nc.sbuf_tensor("selb", [128, 8], F32))
                sb_t8 = ph.enter_context(nc.sbuf_tensor("t8", [128, 8], F32))
                sb_cbr = ph.enter_context(nc.sbuf_tensor("cbr", [1, 8], F32))
                slt = Ring([ph.enter_context(nc.sbuf_tensor(f"slt{r}", [128, 1], mybir.dt.int32))
                            for r in range(2)])
                tkid = Ring([ph.enter_context(nc.sbuf_tensor(f"tkid{r}", [128, 1], F32))
                             for r in range(2)])
                cwv = Ring([ph.enter_context(nc.sbuf_tensor(f"cwv{r}", [128, 1], F32))
                            for r in range(2)])
                pfo = ph.enter_context(nc.sbuf_tensor("pfo", [128, c.NTILE_R], F32))
                ps_P = Ring([ph.enter_context(nc.psum_tensor(f"psP{r}", [128, 8], F32))
                             for r in range(1)])
                ps_sc = ph.enter_context(nc.psum_tensor("pssc", [1, 8], F32))
                ps_pb = ph.enter_context(nc.psum_tensor("pspb", [128, 8], F32))

                def f(eng):
                    eng.memset(sb_one8[:, :], 1.0).then_inc(sems["pfv"], 1)
                    eng.memset(pfo[:, :], float(OWN)).then_inc(sems["pfv"], 1)
                add("vector", f)
                inc("pfv", 2)
                wait("sync", "pfv", cur("pfv"))
                for r in range(2):
                    dma("sync", d_idxf[r].ap().rearrange("(p a) one -> p (a one)", p=128),
                        pfo[:, :], "ropeld")
                wait("vector", "ropeld", cur("ropeld"))
                def f(eng):
                    eng.memset(pfo[:, :], 0.0).then_inc(sems["pfv"], 1)
                add("vector", f)
                inc("pfv", 1)
                wait("sync", "pfv", cur("pfv"))
                for r in range(2):
                    dma("sync", d_cwsf[r].ap().rearrange("(p a) one -> p (a one)", p=128),
                        pfo[:, :], "ropeld")
                pfl_t = cur("ropeld")

                ps_sc_tok = None
                ps_pb_tok = None
                for r in range(2):
                    cnt_s = sb_cnt[:, r * NTK:(r + 1) * NTK]
                    cb_s = sb_cb[:, r * NTK:(r + 1) * NTK]
                    wait("vector", "rt_cn", cur("rt_cn"))
                    NPK = NTK
                    vg = cur("vch")
                    inc("vch", 3 + NPK)
                    def f(eng, cnt_s=cnt_s, cb_s=cb_s, r=r, vg=vg, NPK=NPK):
                        eng.tensor_tensor_scan(cb_s, cnt_s, sb_one8[:, 0:NTK], 0.0,
                                               ALU.add, ALU.mult).then_inc(sems["vch"], 1)
                        eng.wait_ge(sems["vch"], vg + 1)
                        eng.tensor_sub(cb_s, cb_s, cnt_s).then_inc(sems["vch"], 1)
                        eng.reduce_sum(sb_sml5[:, 4 * r:4 * r + 1], cnt_s, axis=AX.X).then_inc(sems["vch"], 1)
                        eng.wait_ge(sems["vch"], vg + 3)
                        nslot = sb_sml5[:, 4 * r:4 * r + 1]
                        acc = sb_sml5[:, 4 * r + 1:4 * r + 2]
                        w = vg + 3
                        eng.tensor_scalar(acc, nslot, 0.0, None, ALU.is_gt).then_inc(sems["vch"], 1)
                        w += 1
                        eng.wait_ge(sems["vch"], w)
                        for k in range(1, NPK):
                            eng.scalar_tensor_tensor(acc, nslot, float(128 * k), acc,
                                                     ALU.is_gt, ALU.add).then_inc(sems["vch"], 1)
                            w += 1
                            eng.wait_ge(sems["vch"], w)
                        i0 = eng.tensor_scalar(sb_sml5[:, 4 * r + 3:4 * r + 4],
                                               acc, 128.0, None, ALU.mult)
                        i0.then_inc(sems["rt_cn"], 1)
                    add("vector", f)
                    pz = inc("rt_cn", 1)
                    wait("tensor", "rt_cn", pz)
                    wait("tensor", "mskld", aux5)
                    wait_tok("tensor", ps_sc_tok)
                    def f(eng, r=r):
                        mm = eng.transpose(ps_sc[0:1, :], sb_sml5[:, 4 * r + 3:4 * r + 4],
                                           sb_idf5[0:8, 0:8])
                        mm.then_inc(sems["rt_mm"], 1)
                    add("tensor", f)
                    tmm = inc("rt_mm", 1)
                    wait("vector", "rt_mm", tmm)
                    vg = cur("vch")
                    inc("vch", 2)
                    def f(eng, r=r, vg=vg):
                        eng.tensor_copy(sb_prow[0:1, 16 * r:16 * r + 8], ps_sc[0:1, :]).then_inc(sems["vch"], 1)
                        eng.wait_ge(sems["vch"], vg + 1)
                        eng.tensor_tensor_scan(sb_prow[0:1, 16 * r + 8:16 * r + 16],
                                               sb_prow[0:1, 16 * r:16 * r + 8],
                                               sb_one8[0:1, 0:8], 0.0,
                                               ALU.add, ALU.mult).then_inc(sems["vch"], 1)
                        eng.wait_ge(sems["vch"], vg + 2)
                        i0 = eng.tensor_sub(sb_prow[0:1, 16 * r:16 * r + 8],
                                            sb_prow[0:1, 16 * r + 8:16 * r + 16],
                                            sb_prow[0:1, 16 * r:16 * r + 8])
                        i0.then_inc(sems["rt_cn"], 1)
                    add("vector", f)
                    ps_sc_tok = ("rt_cn", inc("rt_cn", 1))
                    prow_t = ps_sc_tok[1]
                    wait("tensor", "rt_cn", prow_t)
                    wait_tok("tensor", ps_pb_tok)
                    def f(eng, r=r):
                        mm = eng.matmul(ps_pb[:, :], sb_ones_f[0:1, :],
                                        sb_prow[0:1, 16 * r:16 * r + 8],
                                        start=True, stop=True)
                        mm.then_inc(sems["rt_mm"], 1)
                    add("tensor", f)
                    bmm = inc("rt_mm", 1)
                    wait("vector", "rt_mm", bmm)
                    def f(eng, r=r):
                        i0 = eng.tensor_copy(sb_padb[:, 8 * r:8 * r + 8], ps_pb[:, :])
                        i0.then_inc(sems["rt_cn"], 1)
                    add("vector", f)
                    ps_pb_tok = ("rt_cn", inc("rt_cn", 1))
                    wait("vector", "rt_cn", cur("rt_cn"))
                    vg = cur("vch")
                    inc("vch", 9)
                    def f(eng, r=r, vg=vg):
                        eng.tensor_scalar(sb_etf[0:1, r * c.NTILE_R:(r + 1) * c.NTILE_R],
                                          sb_iotaT[0:1, :], sb_prow[0:1, 16 * r:16 * r + 1],
                                          None, ALU.is_ge).then_inc(sems["vch"], 1)
                        for e2 in range(1, 8):
                            eng.wait_ge(sems["vch"], vg + e2)
                            eng.scalar_tensor_tensor(
                                sb_etf[0:1, r * c.NTILE_R:(r + 1) * c.NTILE_R],
                                sb_iotaT[0:1, :], sb_prow[0:1, 16 * r + e2:16 * r + e2 + 1],
                                sb_etf[0:1, r * c.NTILE_R:(r + 1) * c.NTILE_R],
                                ALU.is_ge, ALU.add).then_inc(sems["vch"], 1)
                        eng.wait_ge(sems["vch"], vg + 8)
                        eng.tensor_scalar(sb_etf[0:1, r * c.NTILE_R:(r + 1) * c.NTILE_R],
                                          sb_etf[0:1, r * c.NTILE_R:(r + 1) * c.NTILE_R],
                                          -1.0, None, ALU.add).then_inc(sems["vch"], 1)
                        eng.wait_ge(sems["vch"], vg + 9)
                        i0 = eng.tensor_copy(sb_etile_i[0:1, r * c.NTILE_R:(r + 1) * c.NTILE_R],
                                             sb_etf[0:1, r * c.NTILE_R:(r + 1) * c.NTILE_R])
                        i0.then_inc(sems["cmpz"], 1)
                    add("vector", f)
                    inc("cmpz", 1)

                    for tk in range(NTK):
                        Hs = sb_H[r][:, tk * c.E:(tk + 1) * c.E]
                        pi5, Pps, Ptok = ps_P.get()
                        wait_tok("tensor", Ptok)
                        wait("tensor", "rt_cw", rt_done)
                        def f(eng, Pps=Pps, Hs=Hs):
                            mm = eng.matmul(Pps[:, :], sb_ltri[:, :], Hs,
                                            start=True, stop=True)
                            mm.then_inc(sems["rt_mm"], 1)
                        add("tensor", f)
                        pmm = inc("rt_mm", 1)
                        wait("tensor", "rt_cn", cur("rt_cn"))
                        wait_tok("tensor", ps_sc_tok)
                        def f(eng, r=r, tk=tk):
                            mm = eng.transpose(ps_sc[0:1, :],
                                               sb_cb[:, r * NTK + tk:r * NTK + tk + 1],
                                               sb_idf5[0:8, 0:8])
                            mm.then_inc(sems["rt_mm"], 1)
                        add("tensor", f)
                        cmm = inc("rt_mm", 1)
                        wait("vector", "rt_mm", cmm)
                        def f(eng):
                            i0 = eng.tensor_copy(sb_cbr[0:1, :], ps_sc[0:1, :])
                            i0.then_inc(sems["rt_cn"], 1)
                        add("vector", f)
                        ps_sc_tok = ("rt_cn", inc("rt_cn", 1))
                        wait("tensor", "rt_cn", ps_sc_tok[1])
                        wait_tok("tensor", ps_pb_tok)
                        def f(eng):
                            mm = eng.matmul(ps_pb[:, :], sb_ones_f[0:1, :], sb_cbr[0:1, :],
                                            start=True, stop=True)
                            mm.then_inc(sems["rt_mm"], 1)
                        add("tensor", f)
                        bmm2 = inc("rt_mm", 1)
                        si5, slti, stok5 = slt.get()
                        ti5, tkb, ttok5 = tkid.get()
                        ci5, cwb5, ctok5 = cwv.get()
                        wait("vector", "rt_mm", max(pmm, bmm2))
                        wait_tok("vector", stok5)
                        wait_tok("vector", ttok5)
                        wait_tok("vector", ctok5)
                        vg = cur("vch")
                        inc("vch", 5)
                        wsrc = sb_w12[:, (r * NTK) + tk:(r * NTK) + tk + 1]
                        def f(eng, r=r, tk=tk, Pps=Pps, Hs=Hs, slti=slti, tkb=tkb,
                              cwb5=cwb5, vg=vg, wsrc=wsrc):
                            eng.tensor_add(sb_selb[:, :], ps_pb[:, :],
                                           sb_padb[:, 8 * r:8 * r + 8]).then_inc(sems["vch"], 1)
                            eng.wait_ge(sems["vch"], vg + 1)
                            eng.tensor_add(sb_t8[:, :], Pps[:, :], sb_selb[:, :]).then_inc(sems["vch"], 1)
                            eng.wait_ge(sems["vch"], vg + 2)
                            eng.tensor_mul(sb_t8[:, :], sb_t8[:, :], Hs).then_inc(sems["vch"], 1)
                            eng.wait_ge(sems["vch"], vg + 3)
                            eng.reduce_sum(sb_t8[:, 0:1], sb_t8[:, :], axis=AX.X).then_inc(sems["vch"], 1)
                            eng.wait_ge(sems["vch"], vg + 4)
                            eng.tensor_copy(slti[:, :], sb_t8[:, 0:1]).then_inc(sems["vch"], 1)
                            eng.tensor_scalar(tkb[:, :], sb_iotaP[:, :], float(tk * 128),
                                              None, ALU.add)
                            i0 = eng.tensor_copy(cwb5[:, :], wsrc)
                            i0.then_inc(sems["cmpv"], 1)
                        add("vector", f)
                        cmv = inc("cmpv", 1)
                        ps_P.set_tok(pi5, ("cmpv", cmv))
                        ps_pb_tok = ("cmpv", cmv)
                        wait("gpsimd", "cmpv", cmv)
                        wait("gpsimd", "ropeld", pfl_t)
                        ssem = f"sct{si5}"
                        getsem(ssem)
                        def f(eng, r=r, slti=slti, tkb=tkb, cwb5=cwb5, ssem=ssem):
                            n0 = nc.n_instructions()
                            d0 = eng.indirect_dma_start(
                                out=d_idxf[r].ap(),
                                out_offset=bass.IndirectOffsetOnAxis(slti[:, 0:1], axis=0),
                                in_=tkb[:, :], in_offset=None,
                                bounds_check=c.NSLOT_R - 1, oob_is_err=False)
                            d0.then_inc(sems[ssem], 16)
                            d1 = eng.indirect_dma_start(
                                out=d_cwsf[r].ap(),
                                out_offset=bass.IndirectOffsetOnAxis(slti[:, 0:1], axis=0),
                                in_=cwb5[:, :], in_offset=None,
                                bounds_check=c.NSLOT_R - 1, oob_is_err=False)
                            d1.then_inc(sems[ssem], 16)
                            _free_val_regs(nc, eng, n0)
                        add("gpsimd", f)
                        sc5 = inc(ssem, 32)
                        slt.set_tok(si5, (ssem, sc5))
                        tkid.set_tok(ti5, (ssem, sc5))
                        cwv.set_tok(ci5, (ssem, sc5))

                NT_ALL = 2 * c.NTILE_R
                sb_chg = ph.enter_context(nc.sbuf_tensor("chg", [1, NT_ALL], F32))
                sb_wi1 = ph.enter_context(nc.sbuf_tensor("wi1", [128, NT_ALL], F32))
                ps_wi = ph.enter_context(nc.psum_tensor("pswi", [128, NT_ALL], F32))
                wait("vector", "cmpz", cur("cmpz"))
                vg = cur("vch")
                inc("vch", 3)
                def f(eng, vg=vg):
                    eng.memset(sb_chg[0:1, :], 1.0).then_inc(sems["vch"], 1)
                    eng.wait_ge(sems["vch"], vg + 1)
                    eng.tensor_tensor(sb_chg[0:1, 1:NT_ALL], sb_etf[0:1, 1:NT_ALL],
                                      sb_etf[0:1, 0:NT_ALL - 1],
                                      ALU.not_equal).then_inc(sems["vch"], 1)
                    eng.wait_ge(sems["vch"], vg + 2)
                    eng.tensor_scalar(sb_chg[0:1, :], sb_chg[0:1, :],
                                      float(-c.E * 128), None,
                                      ALU.mult).then_inc(sems["vch"], 1)
                    eng.wait_ge(sems["vch"], vg + 3)
                    i0 = eng.scalar_tensor_tensor(sb_chg[0:1, :], sb_etf[0:1, :],
                                                  128.0, sb_chg[0:1, :], ALU.mult, ALU.add)
                    i0.then_inc(sems["cmpz"], 1)
                add("vector", f)
                chg1 = inc("cmpz", 1)
                wait("vector", "cmpz", chg1)
                def f(eng):
                    i0 = eng.tensor_scalar(sb_chg[0:1, :], sb_chg[0:1, :],
                                           float(c.E * 128), None, ALU.add)
                    i0.then_inc(sems["cmpz"], 1)
                add("vector", f)
                chg2 = inc("cmpz", 1)
                wait("tensor", "cmpz", chg2)
                wait_tok("tensor", ps_pb_tok)
                def f(eng):
                    mm = eng.matmul(ps_wi[:, :], sb_ones_f[0:1, :], sb_chg[0:1, :],
                                    start=True, stop=True)
                    mm.then_inc(sems["rt_mm"], 1)
                add("tensor", f)
                wmm = inc("rt_mm", 1)
                wait("vector", "rt_mm", wmm)
                vg2 = cur("vch")
                inc("vch", 1)
                def f(eng, vg2=vg2):
                    eng.tensor_scalar(sb_wi1[:, :], ps_wi[:, :], sb_iotaP[:, 0:1],
                                      None, ALU.add).then_inc(sems["vch"], 1)
                    eng.wait_ge(sems["vch"], vg2 + 1)
                    i0 = eng.tensor_copy(sb_idxw_i[:, :], sb_wi1[:, :])
                    i0.then_inc(sems["cmpz"], 1)
                add("vector", f)
                inc("cmpz", 1)

            barrier()

        if c.SPARSE:
            ph_h2.close()
            with ExitStack() as ph:
                NT_ALL = 2 * c.NTILE_R
                FSL = [(fs * 512, min(512, c.F - fs * 512))
                       for fs in range((c.F + 511) // 512)]
                sb_idbf = ph.enter_context(nc.sbuf_tensor("idbfsb", [128, 128], BF16))
                sb_idf6 = ph.enter_context(nc.sbuf_tensor("idf6", [128, 128], F32))
                dma("sync", sb_idbf[:, :], d_idbf[:, :], "yixld")
                dma("sync", sb_idf6[:, :], d_idf[:, :], "yixld")
                ps_A = ph.enter_context(nc.psum_tensor("psA6", [128, c.F], F32))
                ps_B = ph.enter_context(nc.psum_tensor("psB6", [128, c.F], F32))
                ps_i = ps_A
                sb_ir = [ph.enter_context(nc.sbuf_tensor(f"idxraw{r}", [c.NTILE_R, 128], F32))
                         for r in range(2)]
                sb_cr = [ph.enter_context(nc.sbuf_tensor(f"cwsraw{r}", [c.NTILE_R, 128], F32))
                         for r in range(2)]
                sb_idx = ph.enter_context(nc.sbuf_tensor("sbidx", [128, NT_ALL], mybir.dt.int32))
                sb_cws = ph.enter_context(nc.sbuf_tensor("sbcws", [128, NT_ALL], F32))
                wait("sync", "sct0", cur("sct0"))
                wait("sync", "sct1", cur("sct1"))
                for r in range(2):
                    dma("sync", sb_ir[r][:, :],
                        d_idxf[r].ap().rearrange("(j p) one -> j (p one)", p=128), "yixld")
                    dma("sync", sb_cr[r][:, :],
                        d_cwsf[r].ap().rearrange("(j p) one -> j (p one)", p=128), "yixld")
                aux6 = cur("yixld")
                ps_x = Ring([ph.enter_context(nc.psum_tensor(f"psx{r}", [128, 128], BF16))
                             for r in range(2)])
                wait("tensor", "yixld", aux6)
                NTr = c.NTILE_R
                def f(eng):
                    eng.transpose(ps_i[:, 0:NTr], sb_ir[0][:, :], sb_idf6[0:NTr, 0:NTr])
                    eng.transpose(ps_i[:, NTr:NT_ALL], sb_ir[1][:, :],
                                  sb_idf6[0:NTr, 0:NTr]).then_inc(sems["m_tp"], 1)
                add("tensor", f)
                t1 = inc("m_tp", 1)
                wait("vector", "m_tp", t1)
                def f(eng):
                    i0 = eng.tensor_copy(sb_idx[:, :], ps_i[:, 0:NT_ALL])
                    i0.then_inc(sems["sidx"], 1)
                add("vector", f)
                inc("sidx", 1)
                wait("tensor", "sidx", cur("sidx"))
                def f(eng):
                    eng.transpose(ps_i[:, 0:NTr], sb_cr[0][:, :], sb_idf6[0:NTr, 0:NTr])
                    eng.transpose(ps_i[:, NTr:NT_ALL], sb_cr[1][:, :],
                                  sb_idf6[0:NTr, 0:NTr]).then_inc(sems["m_tp"], 1)
                add("tensor", f)
                t2 = inc("m_tp", 1)
                wait("vector", "m_tp", t2)
                def f(eng):
                    i0 = eng.tensor_copy(sb_cws[:, :], ps_i[:, 0:NT_ALL])
                    i0.then_inc(sems["sidx"], 1)
                add("vector", f)
                inc("sidx", 1)

                wgsb = ph.enter_context(nc.sbuf_tensor("wgsb6", [128, DC * c.F], BF16))
                wusb = ph.enter_context(nc.sbuf_tensor("wusb6", [128, DC * c.F], BF16))
                wdsb = ph.enter_context(nc.sbuf_tensor("wdsb6", [128, c.FC * c.D], BF16))
                xg = DmaRing("xg", [ph.enter_context(nc.sbuf_tensor(f"xg{r}", [128, c.D], BF16))
                                    for r in range(2)])
                XT = Ring([ph.enter_context(nc.sbuf_tensor(f"XT{r}", [128, DC * 128], BF16))
                           for r in range(2)])
                sgb = Ring([ph.enter_context(nc.sbuf_tensor(f"sgb6{r}", [128, c.F], BF16))
                            for r in range(1)])
                ub = Ring([ph.enter_context(nc.sbuf_tensor(f"ub6{r}", [128, c.F], BF16))
                           for r in range(1)])
                gq = Ring([ph.enter_context(nc.sbuf_tensor(f"gq6{r}", [128, c.F], BF16))
                           for r in range(1)])
                gT = Ring([ph.enter_context(nc.sbuf_tensor(f"gT6{r}", [128, c.FC * 128], BF16))
                           for r in range(2)])
                eob = Ring([ph.enter_context(nc.sbuf_tensor(f"eob{r}", [128, c.D], BF16))
                            for r in range(2)])
                pump_done_waits("sync")
                wait("sync", "cmpz", cur("cmpz"))
                wait("gpsimd", "sidx", cur("sidx"))
                wait("scalar", "sidx", cur("sidx"))
                wait("vector", "sidx", cur("sidx"))
                psA_tok = None
                psB_tok = None
                if True:
                    for j in range(NT_ALL):
                        rblk = 0 if j < c.NTILE_R else 1
                        d_yr = d_y1 if rblk == 0 else d_y2
                        ga_pre, up_pre, dn_pre = cur("m_ga"), cur("m_up"), cur("m_dn")
                        wait("gpsimd", "m_ga", ga_pre)
                        wait("gpsimd", "m_up", up_pre)
                        wait("gpsimd", "m_dn", dn_pre)
                        def f(eng, j=j):
                            n0 = nc.n_instructions()
                            eng.indirect_dma_start(
                                out=wgsb[:, :], out_offset=None,
                                in_=d_wgbf.ap(),
                                in_offset=bass.IndirectOffsetOnAxis(sb_idxw_i[:, j:j + 1], axis=0),
                                bounds_check=c.E * 128 - 1,
                                oob_is_err=False).then_inc(sems["swg"], 16)
                            eng.indirect_dma_start(
                                out=wusb[:, :], out_offset=None,
                                in_=d_wubf.ap(),
                                in_offset=bass.IndirectOffsetOnAxis(sb_idxw_i[:, j:j + 1], axis=0),
                                bounds_check=c.E * 128 - 1,
                                oob_is_err=False).then_inc(sems["swu"], 16)
                            eng.indirect_dma_start(
                                out=wdsb[:, :], out_offset=None,
                                in_=d_wdbf.ap(),
                                in_offset=bass.IndirectOffsetOnAxis(sb_idxw_i[:, j:j + 1], axis=0),
                                bounds_check=c.E * 128 - 1,
                                oob_is_err=False).then_inc(sems["swd"], 16)
                            _free_val_regs(nc, eng, n0)
                        add("gpsimd", f)
                        inc("swg", 16); inc("swu", 16); inc("swd", 16)
                        gi6, xbuf6, xtok6 = xg.get()
                        wait_tok("gpsimd", xtok6)
                        def f(eng, xbuf6=xbuf6, j=j, gs=xg.sem(gi6)):
                            n0 = nc.n_instructions()
                            dd = eng.indirect_dma_start(
                                out=xbuf6[:, :], out_offset=None,
                                in_=d_h2tm.ap(),
                                in_offset=bass.IndirectOffsetOnAxis(sb_idx[:, j:j + 1], axis=0),
                                bounds_check=OWN - 1, oob_is_err=False)
                            dd.then_inc(sems[gs], 16)
                            _free_val_regs(nc, eng, n0)
                        add("gpsimd", f)
                        xld = inc(xg.sem(gi6), 16)
                        xi6, XTb, XTtok = XT.get()
                        wait("tensor", xg.sem(gi6), xld)
                        wait_tok("tensor", XTtok)
                        for ic in range(DC):
                            pi6, psx, pxtok = ps_x.get()
                            wait_tok("tensor", pxtok)
                            def f(eng, psx=psx, xbuf6=xbuf6, ic=ic):
                                mm = eng.transpose(psx[:, :], xbuf6[:, ic * 128:(ic + 1) * 128],
                                                   sb_idbf[:, :])
                                mm.then_inc(sems["m_tp"], 1)
                            add("tensor", f)
                            tmm6 = inc("m_tp", 1)
                            wait("vector", "m_tp", tmm6)
                            def f(eng, psx=psx, XTb=XTb, ic=ic):
                                i0 = eng.tensor_copy(XTb[:, ic * 128:(ic + 1) * 128], psx[:, :])
                                i0.then_inc(sems["m_cb"], 1)
                            add("vector", f)
                            ps_x.set_tok(pi6, ("m_cb", inc("m_cb", 1)))
                        xev = cur("m_cb")
                        xg.set_tok(gi6, ("m_tp", cur("m_tp")))
                        if j == 0:
                            wait("tensor", "sidx", cur("sidx"))
                        wait("tensor", "swg", 16 * (j + 1))
                        wait("tensor", "m_cb", xev)
                        wait_tok("tensor", psA_tok)
                        def f(eng, XTb=XTb):
                            for ic in range(DC):
                                for fs0, fn in FSL:
                                    mm = eng.matmul(ps_A[:, fs0:fs0 + fn],
                                                    XTb[:, ic * 128:(ic + 1) * 128],
                                                    wgsb[:, ic * c.F + fs0:ic * c.F + fs0 + fn],
                                                    start=(ic == 0), stop=(ic == DC - 1))
                            mm.then_inc(sems["m_ga"], 1)
                        add("tensor", f)
                        gat = inc("m_ga", 1)
                        si6, sgb6, sgtok6 = sgst_ = sgb.get()
                        wait("scalar", "m_ga", gat)
                        wait_tok("scalar", sgtok6)
                        def f(eng, sgb6=sgb6):
                            a = eng.activation(out=sgb6[:, :], in_=ps_A[:, :], func=AF.Silu)
                            a.then_inc(sems["m_evS"], 1)
                        add("scalar", f)
                        evs6 = inc("m_evS", 1)
                        wait("tensor", "swu", 16 * (j + 1))
                        wait_tok("tensor", psB_tok)
                        def f(eng, XTb=XTb):
                            for ic in range(DC):
                                for fs0, fn in FSL:
                                    mm = eng.matmul(ps_B[:, fs0:fs0 + fn],
                                                    XTb[:, ic * 128:(ic + 1) * 128],
                                                    wusb[:, ic * c.F + fs0:ic * c.F + fs0 + fn],
                                                    start=(ic == 0), stop=(ic == DC - 1))
                            mm.then_inc(sems["m_up"], 1)
                        add("tensor", f)
                        upt = inc("m_up", 1)
                        XT.set_tok(xi6, ("m_up", upt))
                        ui6, ub6, ubtok6 = ub.get()
                        wait("vector", "m_up", upt)
                        wait_tok("vector", ubtok6)
                        def f(eng, ub6=ub6, j=j):
                            i0 = eng.tensor_scalar(ub6[:, :], ps_B[:, :],
                                                   sb_cws[:, j:j + 1], None, ALU.mult)
                            i0.then_inc(sems["m_evV"], 1)
                        add("vector", f)
                        evv6 = inc("m_evV", 1)
                        qi6, gq6, gqtok6 = gq.get()
                        wait("vector", "m_evS", evs6)
                        wait("vector", "m_evV", evv6)
                        wait_tok("vector", gqtok6)
                        def f(eng, gq6=gq6, sgb6=sgb6, ub6=ub6):
                            i0 = eng.tensor_mul(gq6[:, :], sgb6[:, :], ub6[:, :])
                            i0.then_inc(sems["m_gp"], 1)
                        add("vector", f)
                        gp6 = inc("m_gp", 1)
                        sgb.set_tok(si6, ("m_gp", gp6))
                        ub.set_tok(ui6, ("m_gp", gp6))
                        gi7, gTb, gTtok = gT.get()
                        wait("tensor", "m_gp", gp6)
                        wait_tok("tensor", gTtok)
                        for fc in range(c.FC):
                            pi7, psx, pxtok = ps_x.get()
                            wait_tok("tensor", pxtok)
                            def f(eng, psx=psx, gq6=gq6, fc=fc):
                                mm = eng.transpose(psx[:, :], gq6[:, fc * 128:(fc + 1) * 128],
                                                   sb_idbf[:, :])
                                mm.then_inc(sems["m_tp"], 1)
                            add("tensor", f)
                            tmm7 = inc("m_tp", 1)
                            wait("vector", "m_tp", tmm7)
                            def f(eng, psx=psx, gTb=gTb, fc=fc):
                                i0 = eng.tensor_copy(gTb[:, fc * 128:(fc + 1) * 128], psx[:, :])
                                i0.then_inc(sems["m_cbe"], 1)
                            add("vector", f)
                            ps_x.set_tok(pi7, ("m_cbe", inc("m_cbe", 1)))
                        gev = cur("m_cbe")
                        gq.set_tok(qi6, ("m_tp", cur("m_tp")))
                        ei6, eo6, eotok6 = eob.get()
                        DH = c.D // 2
                        DNS = [(nn * 512, min(512, DH - nn * 512))
                               for nn in range((DH + 511) // 512)]
                        for hhf, pst in ((0, ps_A), (1, ps_B)):
                            wait("tensor", "m_cbe", gev)
                            wait("tensor", "swd", 16 * (j + 1))
                            if hhf == 0:
                                wait("tensor", "m_evS", evs6)
                            else:
                                wait("tensor", "m_evV", evv6)
                            def f(eng, gTb=gTb, pst=pst, hhf=hhf, DH=DH, DNS=DNS):
                                for fc in range(c.FC):
                                    for n0, nw in DNS:
                                        mm = eng.matmul(
                                            pst[:, n0:n0 + nw],
                                            gTb[:, fc * 128:(fc + 1) * 128],
                                            wdsb[:, fc * c.D + hhf * DH + n0:
                                                 fc * c.D + hhf * DH + n0 + nw],
                                            start=(fc == 0), stop=(fc == c.FC - 1))
                                mm.then_inc(sems["m_dn"], 1)
                            add("tensor", f)
                            dnt = inc("m_dn", 1)
                            wait("vector", "m_dn", dnt)
                            if hhf == 0:
                                wait_tok("vector", eotok6)
                            def f(eng, eo6=eo6, pst=pst, hhf=hhf, DH=DH):
                                i0 = eng.tensor_copy(eo6[:, hhf * DH:(hhf + 1) * DH],
                                                     pst[:, 0:DH])
                                i0.then_inc(sems["m_acc"], 1)
                            add("vector", f)
                            acc6 = inc("m_acc", 1)
                            if hhf == 0:
                                psA_tok = ("m_acc", acc6)
                            else:
                                psB_tok = ("m_acc", acc6)
                        gT.set_tok(gi7, ("m_dn", cur("m_dn")))
                        wait("gpsimd", "m_acc", cur("m_acc"))
                        def f(eng, eo6=eo6, j=j, d_yr=d_yr, es=f"eos{ei6}"):
                            n0 = nc.n_instructions()
                            dd = eng.indirect_dma_start(
                                out=d_yr.ap(),
                                out_offset=bass.IndirectOffsetOnAxis(sb_idx[:, j:j + 1], axis=0),
                                in_=eo6[:, :], in_offset=None,
                                bounds_check=OWN - 1, oob_is_err=False)
                            dd.then_inc(sems[es], 16)
                            _free_val_regs(nc, eng, n0)
                        add("gpsimd", f)
                        eob.set_tok(ei6, (f"eos{ei6}", inc(f"eos{ei6}", 16)))
                barrier()

        with ExitStack() as ph:
            GMAX = 6
            MH, MQ, NMQ = c.MH, c.MQ, c.NMQ
            if c.SPARSE:
                sb_h2 = [ph.enter_context(nc.sbuf_tensor(f"h2Tb{i}", [128, OWN], BF16))
                         for i in range(DC)]
                for i in range(DC):
                    dma("sync", sb_h2[i][:, :], d_h2fm[i, :, :], "h2sp")
                wait("tensor", "h2sp", cur("h2sp"))
                wait("vector", "h2sp", cur("h2sp"))
            sb_idf = ph.enter_context(nc.sbuf_tensor("idfsb", [128, 128], F32))
            dma("sync", sb_idf[:, :], d_idf[:, :], "idfld")
            wait("tensor", "idfld", cur("idfld"))
            sb_yacc = [ph.enter_context(nc.sbuf_tensor(f"yac{i}", [128, MH], BF16))
                       for i in range(DC)]
            wg_r = DmaRing("wpj", [ph.enter_context(nc.sbuf_tensor(f"mwg{r}", [128, DC * 128], BF16))
                                   for r in range(2)])
            wu_r = DmaRing("mwu", [ph.enter_context(nc.sbuf_tensor(f"mwu{r}", [128, DC * 128], BF16))
                                   for r in range(2)])
            wd_r = DmaRing("mwd", [ph.enter_context(nc.sbuf_tensor(f"mwd{r}", [128, DC * 128], BF16))
                                   for r in range(GMAX)])
            gbuf = [ph.enter_context(nc.sbuf_tensor(f"gbf{j}", [128, MH], BF16))
                    for j in range(GMAX)]
            sgst = Ring([ph.enter_context(nc.sbuf_tensor(f"sgs{r}", [128, MH], BF16))
                         for r in range(2)])
            usst = Ring([ph.enter_context(nc.sbuf_tensor(f"uss{r}", [128, MH], BF16))
                         for r in range(2)])
            cwrow = ph.enter_context(nc.sbuf_tensor("cwrow", [1, MH], F32))
            cwb = ph.enter_context(nc.sbuf_tensor("cwbb", [128, MH], F32))
            xo2 = DmaRing("ox", [ph.enter_context(nc.sbuf_tensor(f"xo2{r}", [128, MH], F32))
                                 for r in range(2)])
            yrb = DmaRing("ahl", [ph.enter_context(nc.sbuf_tensor(f"yrb{r}", [128, MH], BF16))
                                  for r in range(2)])
            ost = Ring([ph.enter_context(nc.sbuf_tensor(f"ost{r}", [128, MH], F32))
                        for r in range(2)])
            getsem("ost0"); getsem("ost1")
            ps_m = Ring([ph.enter_context(nc.psum_tensor(f"psm{r}", [128, MH], F32))
                         for r in range(4)])

            def stream_w(ring, src_ap, three_d):
                i, buf, tok = ring.get()
                wait_tok("gpsimd", tok)
                if three_d:
                    dma("gpsimd", buf[:, :].rearrange("p (i f) -> p i f", f=128),
                        src_ap, ring.sem(i))
                else:
                    dma("gpsimd", buf[:, :], src_ap, ring.sem(i))
                return i, buf, (ring.sem(i), cur(ring.sem(i)))

            units = ([] if c.SPARSE else [("e", e) for e in range(c.E)]) + [("s", None)]
            gbuf_tok = [None] * GMAX
            cwb_tok = None
            cwrow_tok = None

            for th in range(c.TH):
                t0 = th * MH
                wait("vector", "o_ev", th * DC)
                for i in range(DC):
                    def f(eng, i=i):
                        eng.memset(sb_yacc[i][:, :], 0.0).then_inc(sems["m_clr"], 1)
                    add("vector", f)
                    inc("m_clr", 1)
                last_acc = [("m_clr", cur("m_clr"))] * DC
                for ukind, ue in units:
                    nfc = c.FC if ukind == "e" else c.FSC
                    if ukind == "e":
                        w_g3 = d_wgT.ap()[ue].rearrange("(i p) f -> p i f", p=128)
                        w_u3 = d_wuT.ap()[ue].rearrange("(i p) f -> p i f", p=128)
                        d_wd2 = d_wdT.ap()[ue]
                    else:
                        w_g3 = d_swgT.ap().rearrange("(i p) f -> p i f", p=128)
                        w_u3 = d_swuT.ap().rearrange("(i p) f -> p i f", p=128)
                        d_wd2 = d_swdT.ap()

                    if ukind == "e":
                        pi, psb, ptok = ps_m.get()
                        wait_tok("tensor", ptok)
                        wait("tensor", "rt_cw", rt_done)
                        ntk_h = MH // 128
                        for jtk in range(ntk_h):
                            tk = th * ntk_h + jtk
                            def f(eng, tk=tk, jtk=jtk, psb=psb, ue=ue):
                                mm = eng.transpose(psb[0:1, jtk * 128:(jtk + 1) * 128],
                                                   sb_cwtm[:, tk * c.E + ue:tk * c.E + ue + 1],
                                                   sb_idf[:, :])
                                mm.then_inc(sems["m_tp"], 1)
                            add("tensor", f)
                            inc("m_tp", 1)
                        wait("vector", "m_tp", cur("m_tp"))
                        wait_tok("vector", cwrow_tok)
                        def f(eng, psb=psb):
                            i0 = eng.tensor_copy(cwrow[0:1, :], psb[0:1, 0:MH])
                            i0.then_inc(sems["m_cb"], 1)
                        add("vector", f)
                        cbr = inc("m_cb", 1)
                        wait("tensor", "m_cb", cbr)
                        def f(eng, psb=psb):
                            for mq in range(NMQ):
                                mm = eng.matmul(psb[:, mq * MQ:(mq + 1) * MQ],
                                                sb_ones_f[0:1, :],
                                                cwrow[0:1, mq * MQ:(mq + 1) * MQ],
                                                start=True, stop=True)
                            mm.then_inc(sems["m_cb"], 1)
                        add("tensor", f)
                        cbb = inc("m_cb", 1)
                        cwrow_tok = ("m_cb", cbb)
                        wait("vector", "m_cb", cbb)
                        wait_tok("vector", cwb_tok)
                        def f(eng, psb=psb):
                            i0 = eng.tensor_copy(cwb[:, :], psb[:, :])
                            i0.then_inc(sems["m_cbe"], 1)
                        add("vector", f)
                        cbe = inc("m_cbe", 1)
                        ps_m.set_tok(pi, ("m_cbe", cbe))

                    for grp in _fc_groups(nfc, GMAX):
                        grp_dw = []
                        for jj, fc in enumerate(grp):
                            gwi, gwb, gwt = stream_w(wg_r, w_g3[:, :, fc * 128:(fc + 1) * 128], True)
                            uwi, uwb, uwt = stream_w(wu_r, w_u3[:, :, fc * 128:(fc + 1) * 128], True)
                            dwi, dwb, dwt = stream_w(wd_r, d_wd2[fc * 128:(fc + 1) * 128, :], False)
                            grp_dw.append((dwi, dwb, dwt))
                            pi, psg, ptok = ps_m.get()
                            wait_tok("tensor", ptok)
                            wait_tok("tensor", gwt)
                            def f(eng, psg=psg, gwb=gwb, t0=t0, h2L=tuple(sb_h2)):
                                for i in range(DC):
                                    for mq in range(NMQ):
                                        mm = eng.matmul(psg[:, mq * MQ:(mq + 1) * MQ],
                                                        gwb[:, i * 128:(i + 1) * 128],
                                                        h2L[i][:, t0 + mq * MQ:t0 + (mq + 1) * MQ],
                                                        start=(i == 0), stop=(i == DC - 1))
                                mm.then_inc(sems["m_gu"], 1)
                            add("tensor", f)
                            gm = inc("m_gu", 1)
                            wg_r.set_tok(gwi, ("m_gu", gm))
                            sgi, sgb, sgtok = sgst.get()
                            wait("scalar", "m_gu", gm)
                            wait_tok("scalar", sgtok)
                            def f(eng, psg=psg, sgb=sgb):
                                a = eng.activation(out=sgb[:, :], in_=psg[:, :], func=AF.Silu)
                                a.then_inc(sems["m_evS"], 1)
                            add("scalar", f)
                            evs = inc("m_evS", 1)
                            ps_m.set_tok(pi, ("m_evS", evs))
                            pi2, psu, ptok2 = ps_m.get()
                            wait_tok("tensor", ptok2)
                            wait_tok("tensor", uwt)
                            def f(eng, psu=psu, uwb=uwb, t0=t0, h2L=tuple(sb_h2)):
                                for i in range(DC):
                                    for mq in range(NMQ):
                                        mm = eng.matmul(psu[:, mq * MQ:(mq + 1) * MQ],
                                                        uwb[:, i * 128:(i + 1) * 128],
                                                        h2L[i][:, t0 + mq * MQ:t0 + (mq + 1) * MQ],
                                                        start=(i == 0), stop=(i == DC - 1))
                                mm.then_inc(sems["m_gu"], 1)
                            add("tensor", f)
                            um = inc("m_gu", 1)
                            wu_r.set_tok(uwi, ("m_gu", um))
                            usi, usb, ustok = usst.get()
                            wait("vector", "m_gu", um)
                            wait_tok("vector", ustok)
                            if ukind == "e":
                                wait("vector", "m_cbe", cur("m_cbe"))
                                def f(eng, psu=psu, usb=usb):
                                    i0 = eng.tensor_mul(usb[:, :], psu[:, :], cwb[:, :])
                                    i0.then_inc(sems["m_evV"], 1)
                                add("vector", f)
                            else:
                                def f(eng, psu=psu, usb=usb):
                                    i0 = eng.tensor_copy(usb[:, :], psu[:, :])
                                    i0.then_inc(sems["m_evV"], 1)
                                add("vector", f)
                            evv = inc("m_evV", 1)
                            ps_m.set_tok(pi2, ("m_evV", evv))
                            wait("vector", "m_evS", evs)
                            wait("vector", "m_evV", evv)
                            wait_tok("vector", gbuf_tok[jj])
                            def f(eng, jj=jj, sgb=sgb, usb=usb):
                                i0 = eng.tensor_mul(gbuf[jj][:, :], sgb[:, :], usb[:, :])
                                i0.then_inc(sems["m_gp"], 1)
                            add("vector", f)
                            gp = inc("m_gp", 1)
                            sgst.set_tok(sgi, ("m_gp", gp))
                            usst.set_tok(usi, ("m_gp", gp))
                        gp_end = cur("m_gp")
                        for hc in range(DC):
                            pi, psd, ptok = ps_m.get()
                            wait_tok("tensor", ptok)
                            for jj, fc in enumerate(grp):
                                dwi, dwb, dwt = grp_dw[jj]
                                if hc == 0:
                                    wait_tok("tensor", dwt)
                                    wait("tensor", "m_gp", gp_end - len(grp) + jj + 1)
                                def f(eng, psd=psd, dwb=dwb, jj=jj, hc=hc,
                                      first=(jj == 0), last=(jj == len(grp) - 1)):
                                    for mq in range(NMQ):
                                        mm = eng.matmul(psd[:, mq * MQ:(mq + 1) * MQ],
                                                        dwb[:, hc * 128:(hc + 1) * 128],
                                                        gbuf[jj][:, mq * MQ:(mq + 1) * MQ],
                                                        start=(first and mq >= 0),
                                                        stop=last)
                                    if last:
                                        mm.then_inc(sems["m_dn"], 1)
                                add("tensor", f)
                            dn = inc("m_dn", 1)
                            wait("vector", "m_dn", dn)
                            wait_tok("vector", last_acc[hc])
                            def f(eng, psd=psd, hc=hc):
                                i0 = eng.scalar_tensor_tensor(sb_yacc[hc][:, :], psd[:, :],
                                                              1.0, sb_yacc[hc][:, :],
                                                              ALU.mult, ALU.add)
                                i0.then_inc(sems["m_acc"], 1)
                            add("vector", f)
                            acc_t = inc("m_acc", 1)
                            last_acc[hc] = ("m_acc", acc_t)
                            ps_m.set_tok(pi, ("m_acc", acc_t))
                        for jj in range(len(grp)):
                            dwi, dwb, dwt = grp_dw[jj]
                            wd_r.set_tok(dwi, ("m_dn", cur("m_dn")))
                            gbuf_tok[jj] = ("m_dn", cur("m_dn"))
                    if ukind == "e":
                        cwb_tok = ("m_evV", cur("m_evV"))

                wait("vector", "m_acc", cur("m_acc"))
                if c.SPARSE and th == 0 and True:
                    wait("sync", "eos0", cur("eos0"))
                    wait("sync", "eos1", cur("eos1"))
                NBLK = MH // 128
                for hc in range(DC):
                    xi, xb, xtok = xo2.get()
                    wait_tok("sync", xtok)
                    xt = dma("sync", xb[:, :], d_x2[hc, :, t0:t0 + MH], xo2.sem(xi))
                    if c.SPARSE:
                        y1i, y1b, y1tok = yrb.get()
                        wait_tok("sync", y1tok)
                        y2i, y2b, y2tok = yrb.get()
                        wait_tok("sync", y2tok)
                        dma("sync", y1b[:, :],
                            d_y1[t0:t0 + MH, hc * 128:(hc + 1) * 128],
                            yrb.sem(y1i), transpose=True)
                        dma("sync", y2b[:, :],
                            d_y2[t0:t0 + MH, hc * 128:(hc + 1) * 128],
                            yrb.sem(y2i), transpose=True)
                        y1t, y2t = cur(yrb.sem(y1i)), cur(yrb.sem(y2i))
                    si, sb_, stok2 = ost.get()
                    wait("vector", xo2.sem(xi), xt)
                    wait_tok("vector", stok2)
                    if not c.SPARSE:
                        def f(eng, xb=xb, sb_=sb_, hc=hc):
                            i0 = eng.tensor_add(sb_[:, :], xb[:, :], sb_yacc[hc][:, :])
                            i0.then_inc(sems["o_ev"], 1)
                        add("vector", f)
                        ev = inc("o_ev", 1)
                    else:
                        wait("vector", yrb.sem(y1i), y1t)
                        wait("vector", yrb.sem(y2i), y2t)
                        vg9 = cur("vch")
                        inc("vch", 2)
                        def f(eng, xb=xb, sb_=sb_, hc=hc, y1b=y1b, y2b=y2b, vg9=vg9):
                            eng.tensor_add(sb_[:, :], xb[:, :], sb_yacc[hc][:, :]).then_inc(sems["vch"], 1)
                            eng.wait_ge(sems["vch"], vg9 + 1)
                            eng.tensor_add(sb_[:, :], sb_[:, :], y1b[:, :]).then_inc(sems["vch"], 1)
                            eng.wait_ge(sems["vch"], vg9 + 2)
                            i0 = eng.tensor_add(sb_[:, :], sb_[:, :], y2b[:, :])
                            i0.then_inc(sems["o_ev"], 1)
                        add("vector", f)
                        ev = inc("o_ev", 1)
                        yrb.set_tok(y1i, ("o_ev", ev))
                        yrb.set_tok(y2i, ("o_ev", ev))
                    xo2.set_tok(xi, ("o_ev", ev))
                    wait("sync", "o_ev", ev)
                    sp = dma("sync", d_out[hc * 128:(hc + 1) * 128, t0:t0 + MH], sb_[:, :],
                             f"ost{si}")
                    ost.set_tok(si, (f"ost{si}", sp))
            wait("sync", "ost0", cur("ost0"))
            wait("sync", "ost1", cur("ost1"))
        ph_b.close()

        with nc.Block() as block:
            def run(fns):
                def g(eng, fns=fns):
                    for f in fns:
                        f(eng)
                return g
            block.sync(run(sched["sync"]))
            block.tensor(run(sched["tensor"]))
            block.vector(run(sched["vector"]))
            block.scalar(run(sched["scalar"]))
            block.gpsimd(run(sched["gpsimd"]))

    return nc



def host_layout(c: CFG, x, Wqkv, Wproj, gw, wg, wu, wd, swg, swu, swd):
    f32 = np.float32
    D, SEQ, HC = c.D, c.SEQ, c.HC
    nheads = 2 * HC
    pi = np.concatenate([np.arange(0, 128, 2), np.arange(1, 128, 2)])

    Wqkv = np.asarray(Wqkv, dtype=f32)
    Wq = Wqkv[0 * D:1 * D].reshape(nheads, 128, D)[:, pi, :]
    Wk = Wqkv[1 * D:2 * D].reshape(nheads, 128, D)[:, pi, :]
    Wv = Wqkv[2 * D:3 * D].reshape(nheads, 128, D)

    pos = np.arange(SEQ, dtype=f32)
    invf = (1.0 / (10000.0 ** (np.arange(0, 128, 2, dtype=f32) / f32(128.0)))).astype(f32)
    ang = invf[:, None] * pos[None, :]
    C64 = np.cos(ang).astype(f32)
    S64 = np.sin(ang).astype(f32)
    ropeC = np.ascontiguousarray(np.concatenate([C64, C64], axis=0))
    ropeS = np.ascontiguousarray(np.concatenate([-S64, S64], axis=0))

    kk = np.arange(128)[:, None]
    qq = np.arange(c.QT)[None, :]
    mask = np.ascontiguousarray(np.concatenate(
        [np.where(kk > qq - 128 * v, f32(NEG_BIG), f32(0.0)) for v in range(c.VR)],
        axis=1).astype(f32))

    common = dict(
        wprojT=np.ascontiguousarray(np.asarray(Wproj, dtype=f32).T),
        ropeC=ropeC, ropeS=ropeS, maskdiag=mask,
        gwT=np.ascontiguousarray(np.asarray(gw, dtype=f32).T),
        wgT=np.ascontiguousarray(np.asarray(wg, dtype=f32).transpose(0, 2, 1)),
        wuT=np.ascontiguousarray(np.asarray(wu, dtype=f32).transpose(0, 2, 1)),
        wdT=np.ascontiguousarray(np.asarray(wd, dtype=f32).transpose(0, 2, 1)),
        swgT=np.ascontiguousarray(np.asarray(swg, dtype=f32).T),
        swuT=np.ascontiguousarray(np.asarray(swu, dtype=f32).T),
        swdT=np.ascontiguousarray(np.asarray(swd, dtype=f32).T),
        onesbf=np.ones((128, 128), dtype=ml_dtypes.bfloat16),
        onesf=np.ones((1, 128), dtype=f32),
        idf=np.ascontiguousarray(np.eye(128, dtype=f32)),
        idbf=np.ascontiguousarray(np.eye(128).astype(ml_dtypes.bfloat16)),
        ltriT=np.ascontiguousarray(
            np.triu(np.ones((128, 128)), k=1).astype(ml_dtypes.bfloat16)),
        iotaT=np.ascontiguousarray(
            (np.arange(c.NTILE_R, dtype=f32) * 128.0).reshape(1, -1)),
        iotaP=np.ascontiguousarray(np.arange(128, dtype=f32).reshape(128, 1)),
        constf=np.ascontiguousarray(np.tile(np.array(
            [[0.0, 1.0, 1e-5, 1.0 / 128.0, 1.0 / math.sqrt(128.0), 0.0, 0.0,
              0.0]], dtype=f32), (128, 1))),
    )

    in_maps = []
    for core in range(c.n_cores):
        b, hh = core // 2, core % 2
        heads = range(hh * HC, (hh + 1) * HC)
        wqkvT = np.concatenate(
            [np.concatenate([Wq[h] for h in heads], axis=0),
             np.concatenate([Wk[h] for h in heads], axis=0),
             np.concatenate([Wv[h] for h in heads], axis=0)], axis=0).T
        xb = np.asarray(x[b], dtype=f32).T
        m = dict(common)
        m["wqkvT"] = np.ascontiguousarray(wqkvT)
        m["xT"] = np.ascontiguousarray(xb)
        m["xownT"] = np.ascontiguousarray(xb[:, hh * c.OWN:(hh + 1) * c.OWN])
        if c.HC >= 4:
            hgroups = [list(range(0, c.HC // 2)),
                       list(range(c.HC // 2, 3 * c.HC // 4)),
                       list(range(3 * c.HC // 4, c.HC))]
        elif c.HC >= 2:
            hgroups = [list(range(0, c.HC // 2)), list(range(c.HC // 2, c.HC))]
        else:
            hgroups = [list(range(c.HC))]
        yidx = np.zeros((128, c.DC), dtype=np.int32)
        for i in range(c.DC):
            p_i, lh = i // c.HC, i % c.HC
            g = next(gg for gg, hgg in enumerate(hgroups) if lh in hgg)
            jj = hgroups[g].index(lh)
            s = 2 * (core // 2) + p_i
            base = (s * len(hgroups[g]) + jj) * 128
            yidx[:, i] = base + np.arange(128)
        m["yidx"] = np.ascontiguousarray(yidx)
        in_maps.append(m)
    return in_maps


def gather_out(c: CFG, results, B):
    out = np.empty((B, c.SEQ, c.D), dtype=np.float32)
    for core in range(c.n_cores):
        b, hh = core // 2, core % 2
        out[b, hh * c.OWN:(hh + 1) * c.OWN, :] = np.asarray(results[core]["outT"]).T
    return out


def kernel(L0_attn_x, L0_attn_W_qkv, L0_attn_W_proj, L0_ffn_gate_weight,
           L0_ffn_w_gate, L0_ffn_w_up, L0_ffn_w_down, L0_ffn_sw_gate,
           L0_ffn_sw_up, L0_ffn_sw_down):
    c = CFG()
    x = np.asarray(L0_attn_x, dtype=np.float32)
    nc = build(c)
    in_maps = host_layout(c, x, L0_attn_W_qkv, L0_attn_W_proj, L0_ffn_gate_weight,
                          L0_ffn_w_gate, L0_ffn_w_up, L0_ffn_w_down,
                          L0_ffn_sw_gate, L0_ffn_sw_up, L0_ffn_sw_down)
    res = run_bass_kernel_spmd(nc, in_maps, core_ids=list(range(c.n_cores)))
    return gather_out(c, res.results, x.shape[0])


# version 17
# speedup vs baseline: 1.0528x; 1.0528x over previous
import math
from contextlib import ExitStack

import numpy as np
import ml_dtypes

import concourse.bass as bass
import concourse.mybir as mybir
from concourse.bass_utils import run_bass_kernel_spmd
from concourse.aot_env import allow_ici_all_to_all

F32 = mybir.dt.float32
BF16 = mybir.dt.bfloat16
AF = mybir.ActivationFunctionType
ALU = mybir.AluOpType
AX = mybir.AxisListType

ENGINES = ("sync", "tensor", "vector", "scalar", "gpsimd")


def _free_val_regs(nc, eng, n0):
    import bass_rust as _br
    freed = set()
    for nm in nc.inst_names()[n0:]:
        ins = nc.get_inst_by_name(nm)
        for arg in list(ins.ins) + list(ins.outs):
            rr = getattr(arg, "regref", None)
            if rr and "_val_" in rr and rr not in freed:
                nc.free_register(_br.RegisterHandle(name=rr, engine=ins.engine))
                freed.add(rr)
    eng.get_value_cache().barrier()
NEG_BIG = -1.0e30


class CFG:
    def __init__(self, D=2048, SEQ=4096, HC=8, F=1408, FS=2816, n_cores=8):
        self.D = D
        self.SEQ = SEQ
        self.HC = HC
        self.F = F
        self.FS = FS
        self.E = 8
        self.n_cores = n_cores
        self.OWN = SEQ // 2
        self.DC = D // 128
        self.QKW = HC * 128
        self.TT = min(256, self.SEQ)
        self.QT = min(512, self.SEQ)
        self.NKC = SEQ // 128
        self.NQT = SEQ // self.QT
        self.VR = self.QT // 128
        self.FC = F // 128
        self.FSC = FS // 128
        self.MT = min(512, self.OWN)
        self.NMT = self.OWN // self.MT
        self.TH = 2 if self.OWN > 1024 else 1
        self.MH = self.OWN // self.TH
        self.MQ = min(512, self.MH)
        self.NMQ = self.MH // self.MQ
        self.QKT = min(512, SEQ)
        self.NQK = SEQ // self.QKT
        self.NT2 = SEQ // self.QT
        self.SPARSE = True
        self.NTILE_R = self.OWN // 128 + 8
        self.NSLOT_R = self.NTILE_R * 128
        assert D % 128 == 0 and SEQ % self.QT == 0 and self.OWN % 128 == 0
        assert D == 2 * HC * 128, "heads*head_dim must equal d_model"
        assert F % 128 == 0 and FS % 128 == 0
        assert self.OWN % self.TT == 0 and SEQ % self.TT == 0
        assert self.MH % 128 == 0


def _fc_groups(n, gmax=6):
    out, i = [], 0
    while i < n:
        g = min(gmax, n - i)
        out.append(list(range(i, i + g)))
        i += g
    return out


class Ring:

    def __init__(self, slots):
        self.s = list(slots)
        self.tok = [None] * len(self.s)
        self.k = 0

    def get(self):
        i = self.k % len(self.s)
        self.k += 1
        return i, self.s[i], self.tok[i]

    def set_tok(self, i, tok):
        self.tok[i] = tok


def build(cfg: CFG):
    nc = bass.Bass("TRN2", target_bir_lowering=False)
    c = cfg
    DC, HC, SEQ, OWN, TT, QT = c.DC, c.HC, c.SEQ, c.OWN, c.TT, c.QT
    INV_SQRT_HD = 1.0 / math.sqrt(128.0)

    d_xT = nc.dram_tensor("xT", [c.D, SEQ], F32, kind="ExternalInput")
    d_xownT = nc.dram_tensor("xownT", [c.D, OWN], F32, kind="ExternalInput")
    d_wqkvT = nc.dram_tensor("wqkvT", [c.D, 3 * c.QKW], F32, kind="ExternalInput")
    d_wprojT = nc.dram_tensor("wprojT", [c.D, c.D], F32, kind="ExternalInput")
    d_ropeC = nc.dram_tensor("ropeC", [128, SEQ], F32, kind="ExternalInput")
    d_ropeS = nc.dram_tensor("ropeS", [128, SEQ], F32, kind="ExternalInput")
    d_mask = nc.dram_tensor("maskdiag", [128, c.VR * QT], F32, kind="ExternalInput")
    d_gwT = nc.dram_tensor("gwT", [c.D, c.E], F32, kind="ExternalInput")
    d_wgT = nc.dram_tensor("wgT", [c.E, c.D, c.F], F32, kind="ExternalInput")
    d_wuT = nc.dram_tensor("wuT", [c.E, c.D, c.F], F32, kind="ExternalInput")
    d_wdT = nc.dram_tensor("wdT", [c.E, c.F, c.D], F32, kind="ExternalInput")
    d_swgT = nc.dram_tensor("swgT", [c.D, c.FS], F32, kind="ExternalInput")
    d_swuT = nc.dram_tensor("swuT", [c.D, c.FS], F32, kind="ExternalInput")
    d_swdT = nc.dram_tensor("swdT", [c.FS, c.D], F32, kind="ExternalInput")
    d_onesbf = nc.dram_tensor("onesbf", [128, 128], BF16, kind="ExternalInput")
    d_onesf = nc.dram_tensor("onesf", [1, 128], F32, kind="ExternalInput")
    d_constf = nc.dram_tensor("constf", [128, 8], F32, kind="ExternalInput")
    d_idf = nc.dram_tensor("idf", [128, 128], F32, kind="ExternalInput")
    d_out = nc.dram_tensor("outT", [c.D, OWN], F32, kind="ExternalOutput")

    d_yidx = nc.dram_tensor("yidx", [128, DC], mybir.dt.int32, kind="ExternalInput")
    d_ltriT = nc.dram_tensor("ltriT", [128, 128], BF16, kind="ExternalInput")
    d_iotaT = nc.dram_tensor("iotaT", [1, c.NTILE_R], F32, kind="ExternalInput")
    d_iotaP = nc.dram_tensor("iotaP", [128, 1], F32, kind="ExternalInput")
    d_idbf = nc.dram_tensor("idbf", [128, 128], BF16, kind="ExternalInput")
    d_wgbf = nc.dram_tensor("wgbf", [c.E * 128, (c.D // 128) * c.F], BF16, kind="Internal")
    d_wubf = nc.dram_tensor("wubf", [c.E * 128, (c.D // 128) * c.F], BF16, kind="Internal")
    d_wdbf = nc.dram_tensor("wdbf", [c.E * 128, (c.F // 128) * c.D], BF16, kind="Internal")
    d_h2fm = nc.dram_tensor("h2fm", [DC, 128, OWN], BF16, kind="Internal")
    d_h2tm = nc.dram_tensor("h2tm", [OWN, c.D], BF16, kind="Internal")
    d_idxf = [nc.dram_tensor(f"idxf{r}", [c.NSLOT_R, 1], F32, kind="Internal")
              for r in range(2)]
    d_cwsf = [nc.dram_tensor(f"cwsf{r}", [c.NSLOT_R, 1], F32, kind="Internal")
              for r in range(2)]
    d_y1 = nc.dram_tensor("yr1", [OWN, c.D], BF16, kind="Internal")
    d_y2 = nc.dram_tensor("yr2", [OWN, c.D], BF16, kind="Internal")
    d_qkv = nc.dram_tensor("qkv_spill", [3 * HC, 128, SEQ], BF16, kind="Internal")
    d_x2 = nc.dram_tensor("x2_spill", [DC, 128, OWN], F32, kind="Internal")

    sched = {e: [] for e in ENGINES}
    ctr = {}
    sems = {}

    def add(e, f):
        sched[e].append(f)

    def inc(name, n=1):
        getsem(name)
        ctr[name] = ctr.get(name, 0) + n
        return ctr[name]

    def cur(name):
        return ctr.get(name, 0)

    def wait(e, sem, tgt):
        if tgt is None or tgt <= 0:
            return
        getsem(sem)
        def f(eng, sem=sem, tgt=tgt):
            eng.wait_ge(sems[sem], tgt)
        add(e, f)

    def wait_tok(e, tok):
        if tok is not None:
            wait(e, tok[0], tok[1])

    def dma(qeng, dst, src, sem, **kw):
        getsem(sem)
        tgt = inc(sem, 16)
        def f(eng, dst=dst, src=src, sem=sem, kw=kw):
            eng.dma_start(out=dst, in_=src, **kw).then_inc(sems[sem], 16)
        add(qeng, f)
        return tgt

    class DmaRing:

        def __init__(self, name, slots, nsem=None):
            self.name = name
            self.s = list(slots)
            self.tok = [None] * len(self.s)
            self.k = 0
            self.nsem = nsem or len(self.s)
            for i in range(self.nsem):
                getsem(f"{name}{i}")

        def get(self):
            i = self.k % len(self.s)
            self.k += 1
            return i, self.s[i], self.tok[i]

        def sem(self, i):
            return f"{self.name}{i % self.nsem}"

        def set_tok(self, i, tok):
            self.tok[i] = tok

    bar_n = [0]
    bar_snap = [{}]

    def barrier(exclude=()):
        getsem("bar")
        bar_n[0] += 1
        tgt = 5 * bar_n[0]
        snap = dict(ctr)
        changed = [(s, v) for s, v in snap.items()
                   if s != "bar" and s not in exclude
                   and v != bar_snap[0].get(s, 0)]
        for s in exclude:
            if s in snap:
                snap[s] = bar_snap[0].get(s, 0)
        bar_snap[0] = snap
        for e in ENGINES:
            for s, v in changed:
                wait(e, s, v)
            def f(eng, tgt=tgt):
                eng.sem_inc(sems["bar"], 1)
                eng.wait_ge(sems["bar"], tgt)
            add(e, f)

    st = ExitStack()

    def getsem(name):
        if name not in sems:
            sems[name] = st.enter_context(nc.semaphore(name))
        return name

    with st, allow_ici_all_to_all():

        sb_ones_bf = st.enter_context(nc.sbuf_tensor("ones_bf", [128, 128], BF16))
        sb_ones_f = st.enter_context(nc.sbuf_tensor("ones_f", [1, 128], F32))
        sb_cst = st.enter_context(nc.sbuf_tensor("cstf", [128, 8], F32))
        dma("sync", sb_ones_bf[:, :], d_onesbf[:, :], "const")
        dma("sync", sb_ones_f[:, :], d_onesf[:, :], "const")
        dma("sync", sb_cst[:, :], d_constf[:, :], "const")
        CONST_VALS = [0.0, 1.0, 1e-5, 1.0 / 128.0, INV_SQRT_HD]
        for ci, val in enumerate(CONST_VALS):
            nc.const_aps.register(F32, val, sb_cst[:, ci:ci + 1])
        const_base = cur("const")
        for e in ("tensor", "vector", "scalar"):
            wait(e, "const", const_base)

        _apply_tmp = [st.enter_context(nc.sbuf_tensor(f"lnt{p}", [128, TT], F32))
                      for p in range(2)]

        pump_pairs = []
        if c.SPARSE:
            for e in range(c.E):
                for dc2 in range(DC):
                    pump_pairs.append((d_wgT.ap()[e][dc2 * 128:(dc2 + 1) * 128, :],
                                       d_wgbf.ap()[e * 128:(e + 1) * 128,
                                                   dc2 * c.F:(dc2 + 1) * c.F], c.F))
                    pump_pairs.append((d_wuT.ap()[e][dc2 * 128:(dc2 + 1) * 128, :],
                                       d_wubf.ap()[e * 128:(e + 1) * 128,
                                                   dc2 * c.F:(dc2 + 1) * c.F], c.F))
                for fc in range(c.FC):
                    pump_pairs.append((d_wdT.ap()[e][fc * 128:(fc + 1) * 128, :],
                                       d_wdbf.ap()[e * 128:(e + 1) * 128,
                                                   fc * c.D:(fc + 1) * c.D], c.D))
            ph_pump = ExitStack()
            pcb = DmaRing("pcb", [ph_pump.enter_context(nc.sbuf_tensor(f"pcb{r}", [128, 2048], BF16))
                                  for r in range(4)], nsem=2)
        pump_i = [0]
        pump_q = []

        def _pump_store(slot, dstap, wdt, tgt):
            wait("gpsimd", pcb.sem(slot), tgt)
            sp = dma("gpsimd", dstap, pcb.s[slot][:, 0:wdt], f"ost{slot % 2}")
            pcb.set_tok(slot, (f"ost{slot % 2}", sp))

        def pump(n):
            if not c.SPARSE:
                return
            for _ in range(n):
                if pump_i[0] >= len(pump_pairs):
                    return
                srcap, dstap, wdt = pump_pairs[pump_i[0]]
                pump_i[0] += 1
                i, buf, tok = pcb.get()
                if len(pump_q) >= 3:
                    _pump_store(*pump_q.pop(0))
                wait_tok("gpsimd", tok)
                dma("gpsimd", buf[:, 0:wdt], srcap, pcb.sem(i))
                pump_q.append((i, dstap, wdt, cur(pcb.sem(i))))

        def pump_drain():
            if not c.SPARSE:
                return
            while pump_i[0] < len(pump_pairs):
                pump(1)
            while pump_q:
                _pump_store(*pump_q.pop(0))

        def pump_done_waits(e):
            if c.SPARSE:
                assert pump_i[0] >= len(pump_pairs) and not pump_q, \
                    "precast pump unfinished"
                wait(e, "ost0", cur("ost0"))
                wait(e, "ost1", cur("ost1"))

        def emit_ln(px, NT, src_ap, out_T, ph):
            TTL = TT
            xbuf = [[ph.enter_context(nc.sbuf_tensor(f"{px}xb{p}_{i}", [128, TTL], F32))
                     for i in range(DC)] for p in range(2)]
            xbf = [ph.enter_context(nc.sbuf_tensor(f"{px}xbf{p}", [128, TTL], BF16)) for p in range(2)]
            xsq = [ph.enter_context(nc.sbuf_tensor(f"{px}xsq{p}", [128, TTL], BF16)) for p in range(2)]
            sml = ph.enter_context(nc.sbuf_tensor(f"{px}sml", [1, 12 * TTL], F32))
            ps_sm = [ph.enter_context(nc.psum_tensor(f"{px}psm{p}", [1, TTL], F32)) for p in range(2)]
            ps_sq = [ph.enter_context(nc.psum_tensor(f"{px}psq{p}", [1, TTL], F32)) for p in range(2)]
            ps_bc = [ph.enter_context(nc.psum_tensor(f"{px}bc{p}", [128, 2 * TTL], F32)) for p in range(2)]
            s_x, s_pre, s_stat, s_sml, s_bc, s_apl = ("ln_x", "ln_pre",
                "ln_stat", "ln_sml", "ln_bc", "ln_apl")
            b_stat, b_sml, b_apl = cur(s_stat), cur(s_sml), cur(s_apl)
            invD = 1.0 / c.D
            for t in range(NT):
                p = t % 2
                s_xp = f"{s_x}{p}"
                if t >= 2:
                    wait("sync", s_apl, b_apl + (t - 1) * DC)
                for i in range(DC):
                    dma("sync", xbuf[p][i][:, :], src_ap(i, t), s_xp)
                ld = cur(s_xp)
                wait("scalar", s_xp, ld)
                for i in range(DC):
                    k = t * DC + i
                    if k >= 2:
                        wait("scalar", s_stat, b_stat + k - 1)
                    sc = inc("sch", 1)
                    def f(eng, i=i, p=p, k=k, sc=sc):
                        eng.copy(out=xbf[k % 2][:, :], in_=xbuf[p][i][:, :]).then_inc(sems["sch"], 1)
                        eng.wait_ge(sems["sch"], sc)
                        a = eng.activation(out=xsq[k % 2][:, :], in_=xbuf[p][i][:, :],
                                           func=AF.Square)
                        a.then_inc(sems[s_pre], 1)
                    add("scalar", f)
                    inc(s_pre, 1)
                    wait("tensor", s_pre, cur(s_pre))
                    if i == 0 and t >= 2:
                        wait("tensor", s_sml, b_sml + 3 * (t - 2) + 1)
                    def f(eng, i=i, p=p, k=k):
                        eng.matmul(ps_sm[p][0:1, :], sb_ones_bf[:, 0:1], xbf[k % 2][:, :],
                                   start=(i == 0), stop=(i == DC - 1))
                        mm = eng.matmul(ps_sq[p][0:1, :], sb_ones_bf[:, 0:1], xsq[k % 2][:, :],
                                        start=(i == 0), stop=(i == DC - 1))
                        mm.then_inc(sems[s_stat], 1)
                    add("tensor", f)
                    inc(s_stat, 1)
                o = 6 * p
                S_mean = sml[0:1, (o + 0) * TTL:(o + 1) * TTL]
                S_msq = sml[0:1, (o + 1) * TTL:(o + 2) * TTL]
                S_var = sml[0:1, (o + 2) * TTL:(o + 3) * TTL]
                S_sd = sml[0:1, (o + 3) * TTL:(o + 4) * TTL]
                S_rstd = sml[0:1, (o + 4) * TTL:(o + 5) * TTL]
                S_mrs = sml[0:1, (o + 5) * TTL:(o + 6) * TTL]
                wait("vector", s_stat, b_stat + (t + 1) * DC)
                c1 = inc("vch", 2)
                def f(eng, p=p, S_mean=S_mean, S_msq=S_msq, S_var=S_var, c1=c1):
                    eng.tensor_scalar_mul(S_mean, ps_sm[p][0:1, :], invD).then_inc(sems["vch"], 1)
                    eng.tensor_scalar_mul(S_var, ps_sq[p][0:1, :], invD).then_inc(sems["vch"], 1)
                    eng.wait_ge(sems["vch"], c1)
                    eng.tensor_mul(S_msq, S_mean, S_mean).then_inc(sems["vch"], 1)
                    eng.wait_ge(sems["vch"], c1 + 1)
                    i0 = eng.tensor_sub(S_var, S_var, S_msq)
                    i0.then_inc(sems[s_sml], 1)
                add("vector", f)
                inc("vch", 1)
                v1 = inc(s_sml, 1)
                wait("scalar", s_sml, v1)
                def f(eng, S_sd=S_sd, S_var=S_var):
                    a = eng.activation(out=S_sd, in_=S_var, func=AF.Sqrt, bias=1e-5)
                    a.then_inc(sems[s_sml], 1)
                add("scalar", f)
                v2 = inc(s_sml, 1)
                wait("vector", s_sml, v2)
                c2 = inc("vch", 2)
                def f(eng, S_sd=S_sd, S_rstd=S_rstd, S_mrs=S_mrs, S_mean=S_mean, c2=c2):
                    eng.reciprocal(S_rstd, S_sd).then_inc(sems["vch"], 1)
                    eng.wait_ge(sems["vch"], c2 - 1)
                    eng.tensor_mul(S_mrs, S_mean, S_rstd).then_inc(sems["vch"], 1)
                    eng.wait_ge(sems["vch"], c2)
                    i0 = eng.tensor_scalar_mul(S_mrs, S_mrs, -1.0)
                    i0.then_inc(sems[s_sml], 1)
                add("vector", f)
                v3 = inc(s_sml, 1)
                wait("tensor", s_sml, v3)
                if t >= 2:
                    wait("tensor", s_apl, b_apl + (t - 1) * DC)
                def f(eng, p=p, S_rstd=S_rstd, S_mrs=S_mrs):
                    eng.matmul(ps_bc[p][:, 0:TTL], sb_ones_f[0:1, :], S_rstd,
                               start=True, stop=True)
                    mm = eng.matmul(ps_bc[p][:, TTL:2 * TTL], sb_ones_f[0:1, :], S_mrs,
                                    start=True, stop=True)
                    mm.then_inc(sems[s_bc], 1)
                add("tensor", f)
                btgt = inc(s_bc, 1)
                wait("vector", s_bc, btgt)
                for i in range(DC):
                    apl_pre = cur(s_apl)
                    mc = inc("vch", 1)
                    def f(eng, i=i, p=p, t=t, apl_pre=apl_pre, mc=mc):
                        if apl_pre:
                            eng.wait_ge(sems[s_apl], apl_pre)
                        eng.tensor_mul(_apply_tmp[p][:, :], xbuf[p][i][:, :],
                                       ps_bc[p][:, 0:TTL]).then_inc(sems["vch"], 1)
                        eng.wait_ge(sems["vch"], mc)
                        i1 = eng.tensor_add(out_T[i][:, t * TTL:(t + 1) * TTL],
                                            _apply_tmp[p][:, :], ps_bc[p][:, TTL:2 * TTL])
                        i1.then_inc(sems[s_apl], 1)
                    add("vector", f)
                    inc(s_apl, 1)

        ph_a = ExitStack()
        sb_xnT = [ph_a.enter_context(nc.sbuf_tensor(f"xnT{i}", [128, SEQ], BF16))
                  for i in range(DC)]

        with ExitStack() as ph:
            emit_ln("ln1", SEQ // TT,
                    lambda i, t: d_xT[i * 128:(i + 1) * 128, t * TT:(t + 1) * TT],
                    sb_xnT, ph)
            barrier()

        with ExitStack() as ph:
            sb_ropeC = ph.enter_context(nc.sbuf_tensor("ropeCs", [128, SEQ], F32))
            sb_ropeS = ph.enter_context(nc.sbuf_tensor("ropeSs", [128, SEQ], F32))
            dma("sync", sb_ropeC[:, :], d_ropeC[:, :], "ropeld")
            dma("sync", sb_ropeS[:, :], d_ropeS[:, :], "ropeld")
            rope_tgt = cur("ropeld")
            wsb = DmaRing("wsb", [ph.enter_context(nc.sbuf_tensor(f"wqkv{r}", [128, DC * 128], BF16))
                                  for r in range(4)])
            stg = Ring([ph.enter_context(nc.sbuf_tensor(f"qstg{r}", [128, QT], BF16))
                        for r in range(4)])
            rt1 = [ph.enter_context(nc.sbuf_tensor(f"ropet1{p}", [128, QT], F32)) for p in range(2)]
            rt2 = [ph.enter_context(nc.sbuf_tensor(f"ropet2{p}", [128, QT], F32)) for p in range(2)]
            ps_qk = Ring([ph.enter_context(nc.psum_tensor(f"psqk{r}", [128, c.QKT], F32))
                          for r in range(2)])
            wq3 = d_wqkvT.ap().rearrange("(i p) f -> p i f", p=128)
            wait("vector", "ropeld", rope_tgt)

            NQK = c.NQK
            HW = c.QKT // QT
            order = []
            for h in range(HC):
                order += [(0, h), (1, h), (2, h)]
            for sidx, (kind, h) in enumerate(order):
                col0 = kind * c.QKW + h * 128
                wi, wbuf, wtok = wsb.get()
                wait_tok("gpsimd", wtok)
                dma("gpsimd", wbuf[:, :].rearrange("p (i f) -> p i f", f=128),
                    wq3[:, :, col0:col0 + 128], wsb.sem(wi))
                wld = cur(wsb.sem(wi))
                wsb.set_tok(wi, ("qk_mm", (sidx + 1) * NQK))
                for tt in range(NQK):
                    pi, psb, ptok = ps_qk.get()
                    if tt == 0:
                        wait("tensor", wsb.sem(wi), wld)
                    wait_tok("tensor", ptok)
                    def f(eng, wbuf=wbuf, psb=psb, tt=tt):
                        for i in range(DC):
                            mm = eng.matmul(psb[:, :], wbuf[:, i * 128:(i + 1) * 128],
                                            sb_xnT[i][:, tt * c.QKT:(tt + 1) * c.QKT],
                                            start=(i == 0), stop=(i == DC - 1))
                        mm.then_inc(sems["qk_mm"], 1)
                    add("tensor", f)
                    mmt = inc("qk_mm", 1)
                    ptok_new = None
                    for hw in range(HW):
                        gcol = tt * c.QKT + hw * QT
                        gi = (sidx * NQK + tt) * HW + hw
                        si, sbuf_, stok = stg.get()
                        if kind in (0, 1):
                            if hw == 0:
                                wait("vector", "qk_mm", mmt)
                            wait_tok("vector", stok)
                            p2 = gi % 2
                            wait("vector", "qk_stage", cur("qk_stage"))
                            vc = inc("vch", 2)
                            cur_evac = inc("qk_evac", 1)
                            def f(eng, psb=psb, p2=p2, hw=hw, gcol=gcol, sbuf_=sbuf_,
                                  vc=vc, cur_evac=cur_evac):
                                hs = hw * QT
                                eng.tensor_mul(rt1[p2][:, :], psb[:, hs:hs + QT],
                                               sb_ropeC[:, gcol:gcol + QT]).then_inc(sems["vch"], 1)
                                eng.tensor_mul(rt2[p2][0:64, :], psb[64:128, hs:hs + QT],
                                               sb_ropeS[0:64, gcol:gcol + QT]).then_inc(sems["vch"], 1)
                                eng.wait_ge(sems["vch"], vc)
                                i2 = eng.tensor_mul(rt2[p2][64:128, :], psb[0:64, hs:hs + QT],
                                                    sb_ropeS[64:128, gcol:gcol + QT])
                                i2.then_inc(sems["qk_evac"], 1)
                                eng.wait_ge(sems["qk_evac"], cur_evac)
                                i3 = eng.tensor_add(sbuf_[:, :], rt1[p2][:, :], rt2[p2][:, :])
                                i3.then_inc(sems["qk_stage"], 1)
                            add("vector", f)
                            ptok_new = ("qk_evac", cur_evac)
                            stok_sem, stgt = "qk_stage", inc("qk_stage", 1)
                        else:
                            if hw == 0:
                                wait("scalar", "qk_mm", mmt)
                            wait_tok("scalar", stok)
                            def f(eng, psb=psb, sbuf_=sbuf_, hw=hw):
                                a = eng.copy(out=sbuf_[:, :], in_=psb[:, hw * QT:(hw + 1) * QT])
                                a.then_inc(sems["qk_evacS"], 1)
                            add("scalar", f)
                            ptok_new = ("qk_evacS", inc("qk_evacS", 1))
                            stok_sem, stgt = "qk_evacS", cur("qk_evacS")
                        wait("sync", stok_sem, stgt)
                        sp = dma("sync", d_qkv[sidx, :, gcol:gcol + QT], sbuf_[:, :],
                                 f"qsp{si}")
                        stg.set_tok(si, (f"qsp{si}", sp))
                    ps_qk.set_tok(pi, ptok_new)
                pump(5)
            barrier()
        ph_a.close()

        ph_y = ExitStack()
        sb_yT = [ph_y.enter_context(nc.sbuf_tensor(f"yT{i}", [128, SEQ], BF16))
                 for i in range(HC)]
        sb_yidx = ph_y.enter_context(nc.sbuf_tensor("yidxsb", [128, DC], mybir.dt.int32))

        if HC >= 8:
            hgroups = [list(range(0, HC - 4)), [HC - 4, HC - 3], [HC - 2], [HC - 1]]
        elif HC >= 2:
            hgroups = [list(range(0, HC // 2)), list(range(HC // 2, HC))]
        else:
            hgroups = [list(range(HC))]
        d_cins, d_couts = [], []
        for g, hg in enumerate(hgroups):
            sz = c.n_cores * len(hg) * 128
            d_cins.append(nc.dram_tensor(f"cc_in{g}", [sz, OWN], BF16, kind="Internal"))
            d_couts.append(nc.dram_tensor(f"cc_out{g}", [sz, OWN], BF16, kind="Internal"))

        def yf_ap(i):
            p = i // HC
            return sb_yT[i % HC][:, p * OWN:(p + 1) * OWN]

        gat_tot = []

        def emit_exchange_cc(g):
            hg = hgroups[g]
            nhg = len(hg)
            wait("gpsimd", "a_y", (hg[-1] + 1) * c.NQT)
            for d in range(c.n_cores):
                rho = d % 2
                for jj, i in enumerate(hg):
                    dma("gpsimd",
                        d_cins[g][(d * nhg + jj) * 128:(d * nhg + jj + 1) * 128, :],
                        sb_yT[i][:, rho * OWN:(rho + 1) * OWN], "cc_wr")
            wait("gpsimd", "cc_wr", cur("cc_wr"))
            def f(eng, g=g):
                cc = eng.collective_compute(
                    "AllToAll", ALU.bypass,
                    replica_groups=[list(range(c.n_cores))],
                    ins=[d_cins[g].ap().opt()], outs=[d_couts[g].ap().opt()])
                cc.then_inc(sems["cc_done"])
            add("gpsimd", f)
            inc("cc_done", 1)

        def emit_gather(g):
            hg = hgroups[g]
            sem = f"ygat{g % 2}"
            getsem(sem)
            wait("gpsimd", "cc_done", g + 1)
            wait("gpsimd", "yixld", cur("yixld"))
            chunks = [p * HC + lh for p in range(2) for lh in hg]
            for i in chunks:
                dst = yf_ap(i)
                inc(sem, 16)
                def f(eng, dst=dst, i=i, g=g, sem=sem):
                    n0 = nc.n_instructions()
                    dd = eng.indirect_dma_start(
                        out=dst, out_offset=None,
                        in_=d_couts[g].ap(),
                        in_offset=bass.IndirectOffsetOnAxis(
                            ap=sb_yidx[:, i:i + 1], axis=0))
                    dd.then_inc(sems[sem], 16)
                    _free_val_regs(nc, eng, n0)
                add("gpsimd", f)
            gat_tot.append((sem, cur(sem)))

        with ExitStack() as ph:
            sb_mask = ph.enter_context(nc.sbuf_tensor("maskd", [128, c.VR * QT], F32))
            dma("sync", sb_mask[:, :], d_mask[:, :], "mskld")
            dma("sync", sb_yidx[:, :], d_yidx[:, :], "yixld")
            wait("vector", "mskld", cur("mskld"))
            for r in range(4):
                wait("sync", f"qsp{r}", cur(f"qsp{r}"))

            hb = DmaRing("ahl", [
                {"q": ph.enter_context(nc.sbuf_tensor(f"qh{r}", [128, SEQ], BF16)),
                 "k": ph.enter_context(nc.sbuf_tensor(f"kh{r}", [128, SEQ], BF16)),
                 "v": ph.enter_context(nc.sbuf_tensor(f"vh{r}", [128, SEQ], BF16))}
                for r in range(2)])
            nsq = Ring([ph.enter_context(nc.sbuf_tensor(f"nsq{r}", [128, QT], BF16))
                        for r in range(2)])
            nrow = ph.enter_context(nc.sbuf_tensor("nrow", [1, 2 * c.NT2], F32))
            nsml = ph.enter_context(nc.sbuf_tensor("nsml", [1, 8], F32))
            biasA = [ph.enter_context(nc.sbuf_tensor(f"biasA{r}", [128, 1], F32))
                     for r in range(2)]
            ptb = Ring([ph.enter_context(nc.sbuf_tensor(f"pt{r}", [128, QT], BF16))
                        for r in range(6)])
            smb = Ring([ph.enter_context(nc.sbuf_tensor(f"smb{r}", [128, QT], F32))
                        for r in range(5)])
            rbs = ph.enter_context(nc.sbuf_tensor("rbs", [128, QT], F32))
            rcp = ph.enter_context(nc.sbuf_tensor("rcp", [1, QT], F32))
            ps_s = Ring([ph.enter_context(nc.psum_tensor(f"pss{r}", [128, QT], F32))
                         for r in range(4)])
            ps_o = Ring([ph.enter_context(nc.psum_tensor(f"pso{r}", [128, QT], F32))
                         for r in range(2)])
            ps_sum = ph.enter_context(nc.psum_tensor("pssum", [1, QT], F32))
            ps_rb = ph.enter_context(nc.psum_tensor("psrb", [128, QT], F32))

            ps_sum_tok = None
            ps_rb_tok = None
            rbs_tok = None
            rcp_tok = None
            exps_at_head_end = []

            for h in range(HC):
                bi, bufs, btok = hb.get()
                hsem = hb.sem(bi)
                wait_tok("sync", btok)
                dma("sync", bufs["q"][:, :], d_qkv[3 * h + 0, :, :], hsem)
                dma("sync", bufs["k"][:, :], d_qkv[3 * h + 1, :, :], hsem)
                for kc in range(c.NKC):
                    dma("sync", bufs["v"][:, kc * 128:(kc + 1) * 128],
                        d_qkv[3 * h + 2, :, kc * 128:(kc + 1) * 128], hsem,
                        transpose=True)
                ldt = cur(hsem)

                wait("scalar", hsem, ldt)
                wait("vector", "a_nrd", cur("a_nrd"))
                for pidx, part in enumerate(("q", "k")):
                    for j in range(c.NT2):
                        g = pidx * c.NT2 + j
                        qi, qb, qtok = nsq.get()
                        wait_tok("scalar", qtok)
                        def f(eng, qb=qb, part=part, j=j, bufs=bufs):
                            a = eng.activation(out=qb[:, :],
                                               in_=bufs[part][:, j * QT:(j + 1) * QT],
                                               func=AF.Square)
                            a.then_inc(sems["a_nrm"], 1)
                        add("scalar", f)
                        nt = inc("a_nrm", 1)
                        nsq.set_tok(qi, ("a_nmm", cur("a_nmm") + 1))
                        if g % 2 == 0:
                            nb, ntok = ps_sum, ps_sum_tok
                        else:
                            nb, ntok = ps_rb, ps_rb_tok
                        wait("tensor", "a_nrm", nt)
                        wait_tok("tensor", ntok)
                        def f(eng, nb=nb, qb=qb):
                            mm = eng.matmul(nb[0:1, :], sb_ones_bf[:, 0:1], qb[:, :],
                                            start=True, stop=True)
                            mm.then_inc(sems["a_nmm"], 1)
                        add("tensor", f)
                        nmm = inc("a_nmm", 1)
                        wait("vector", "a_nmm", nmm)
                        def f(eng, nb=nb, g=g):
                            r = eng.reduce_max(nrow[0:1, g:g + 1], nb[0:1, :], axis=AX.X)
                            r.then_inc(sems["a_nrd"], 1)
                        add("vector", f)
                        ntok2 = ("a_nrd", inc("a_nrd", 1))
                        if g % 2 == 0:
                            ps_sum_tok = ntok2
                        else:
                            ps_rb_tok = ntok2
                wait("vector", "a_nrd", cur("a_nrd"))
                wait("vector", "a_nrm", cur("a_nrm"))
                vc = inc("vch", 2)
                def f(eng, vc=vc):
                    eng.reduce_max(nsml[0:1, 0:1], nrow[0:1, 0:c.NT2], axis=AX.X).then_inc(sems["vch"], 1)
                    eng.reduce_max(nsml[0:1, 1:2], nrow[0:1, c.NT2:2 * c.NT2], axis=AX.X).then_inc(sems["vch"], 1)
                    eng.wait_ge(sems["vch"], vc)
                    i0 = eng.tensor_mul(nsml[0:1, 2:3], nsml[0:1, 0:1], nsml[0:1, 1:2])
                    i0.then_inc(sems["a_nrd"], 1)
                add("vector", f)
                nprod = inc("a_nrd", 1)
                wait("scalar", "a_nrd", nprod)
                def f(eng):
                    a = eng.activation(out=nsml[0:1, 3:4], in_=nsml[0:1, 2:3],
                                       func=AF.Sqrt, scale=1.0 / 128.0)
                    a.then_inc(sems["a_nrm"], 1)
                add("scalar", f)
                nb_t = inc("a_nrm", 1)
                wait("vector", "a_nrm", nb_t)
                wait("vector", "a_nmm", cur("a_nmm"))
                def f(eng):
                    i0 = eng.tensor_scalar_mul(nsml[0:1, 4:5], nsml[0:1, 3:4], -1.0)
                    i0.then_inc(sems["a_nrd"], 1)
                add("vector", f)
                negb = inc("a_nrd", 1)
                wait("tensor", "a_nrd", negb)
                wait_tok("tensor", ps_rb_tok)
                def f(eng):
                    mm = eng.matmul(ps_rb[:, 0:1], sb_ones_f[0:1, :], nsml[0:1, 4:5],
                                    start=True, stop=True)
                    mm.then_inc(sems["a_nmm"], 1)
                add("tensor", f)
                bmm = inc("a_nmm", 1)
                wait("vector", "a_nmm", bmm)
                if h >= 2:
                    wait("vector", "a_exp", exps_at_head_end[h - 2])
                bA = biasA[h % 2]
                def f(eng, bA=bA):
                    i0 = eng.tensor_copy(bA[:, :], ps_rb[:, 0:1])
                    i0.then_inc(sems["a_bias"], 1)
                add("vector", f)
                ps_rb_tok = ("a_bias", inc("a_bias", 1))
                wait("scalar", "a_bias", cur("a_bias"))

                for qt in range(c.NQT):
                    kcs = list(range(0, (qt + 1) * c.VR))
                    nkc = len(kcs)
                    SKEW = min(5, nkc)
                    kstate = {}
                    octx = {}

                    def emitA(ki, qt=qt, kcs=kcs, bufs=bufs, kstate=kstate):
                        kc = kcs[ki]
                        diag_v = kc - qt * c.VR
                        si, ssb, stok2 = ps_s.get()
                        wait_tok("tensor", stok2)
                        def f(eng, ssb=ssb, kc=kc, qt=qt, bufs=bufs):
                            mm = eng.matmul(ssb[:, :], bufs["k"][:, kc * 128:(kc + 1) * 128],
                                            bufs["q"][:, qt * QT:(qt + 1) * QT],
                                            start=True, stop=True)
                            mm.then_inc(sems["a_smm"], 1)
                        add("tensor", f)
                        smm = inc("a_smm", 1)
                        pti, ptbuf, pttok = ptb.get()
                        bA = biasA[h % 2]
                        if diag_v >= 0:
                            mi, msb, mtok = smb.get()
                            wait("vector", "a_smm", smm)
                            wait_tok("vector", mtok)
                            def f(eng, msb=msb, ssb=ssb, diag_v=diag_v):
                                i0 = eng.tensor_add(msb[:, :], ssb[:, :],
                                                    sb_mask[:, diag_v * QT:(diag_v + 1) * QT])
                                i0.then_inc(sems["a_msk"], 1)
                            add("vector", f)
                            mk = inc("a_msk", 1)
                            ps_s.set_tok(si, ("a_msk", mk))
                            wait("scalar", "a_msk", mk)
                            wait_tok("scalar", pttok)
                            def f(eng, msb=msb, ptbuf=ptbuf, bA=bA):
                                a = eng.activation(out=ptbuf[:, :], in_=msb[:, :],
                                                   func=AF.Exp, bias=bA[:, 0:1],
                                                   scale=INV_SQRT_HD)
                                a.then_inc(sems["a_exp"], 1)
                            add("scalar", f)
                            ex = inc("a_exp", 1)
                            smb.set_tok(mi, ("a_exp", ex))
                        else:
                            wait("scalar", "a_smm", smm)
                            wait_tok("scalar", pttok)
                            def f(eng, ssb=ssb, ptbuf=ptbuf, bA=bA):
                                a = eng.activation(out=ptbuf[:, :], in_=ssb[:, :],
                                                   func=AF.Exp, bias=bA[:, 0:1],
                                                   scale=INV_SQRT_HD)
                                a.then_inc(sems["a_exp"], 1)
                            add("scalar", f)
                            ex = inc("a_exp", 1)
                            ps_s.set_tok(si, ("a_exp", ex))
                        kstate[ki] = (ex, pti, ptbuf)

                    def emitB(ki, qt=qt, kcs=kcs, bufs=bufs, kstate=kstate, octx=octx):
                        nonlocal ps_sum_tok
                        kc = kcs[ki]
                        nkc = len(kcs)
                        ex, pti, ptbuf = kstate.pop(ki)
                        if ki == 0:
                            oi, osb, otok = ps_o.get()
                            wait_tok("tensor", otok)
                            wait_tok("tensor", ps_sum_tok)
                            octx["o"] = (oi, osb)
                        oi, osb = octx["o"]
                        wait("tensor", "a_exp", ex)
                        def f(eng, osb=osb, ptbuf=ptbuf, kc=kc, ki=ki, nkc=nkc, bufs=bufs):
                            eng.matmul(osb[:, :], bufs["v"][:, kc * 128:(kc + 1) * 128],
                                       ptbuf[:, :], start=(ki == 0), stop=(ki == nkc - 1))
                            mm = eng.matmul(ps_sum[0:1, :], sb_ones_bf[:, 0:1], ptbuf[:, :],
                                            start=(ki == 0), stop=(ki == nkc - 1))
                            mm.then_inc(sems["a_pv"], 1)
                        add("tensor", f)
                        pv = inc("a_pv", 1)
                        ptb.set_tok(pti, ("a_pv", pv))
                        return pv

                    for ki in range(SKEW):
                        emitA(ki)
                    for ki in range(nkc):
                        if ki + SKEW < nkc:
                            emitA(ki + SKEW)
                        pv = emitB(ki)
                    oi, osb = octx["o"]
                    wait("vector", "a_pv", pv)
                    wait_tok("vector", rcp_tok)
                    def f(eng):
                        i0 = eng.reciprocal(rcp[0:1, :], ps_sum[0:1, :])
                        i0.then_inc(sems["a_rcp"], 1)
                    add("vector", f)
                    ps_sum_tok = ("a_rcp", inc("a_rcp", 1))
                    wait("tensor", "a_rcp", ps_sum_tok[1])
                    wait_tok("tensor", ps_rb_tok)
                    def f(eng):
                        mm = eng.matmul(ps_rb[:, :], sb_ones_f[0:1, :], rcp[0:1, :],
                                        start=True, stop=True)
                        mm.then_inc(sems["a_rbc"], 1)
                    add("tensor", f)
                    rbc = inc("a_rbc", 1)
                    rcp_tok = ("a_rbc", rbc)
                    wait("scalar", "a_rbc", rbc)
                    wait_tok("scalar", rbs_tok)
                    def f(eng):
                        a = eng.copy(out=rbs[:, :], in_=ps_rb[:, :])
                        a.then_inc(sems["a_rbe"], 1)
                    add("scalar", f)
                    rbe = inc("a_rbe", 1)
                    ps_rb_tok = ("a_rbe", rbe)
                    wait("vector", "a_rbe", rbe)
                    def f(eng, h=h, qt=qt, osb=osb):
                        i0 = eng.scalar_tensor_tensor(sb_yT[h][:, qt * QT:(qt + 1) * QT],
                                                      osb[:, :], 1.0, rbs[:, :],
                                                      ALU.mult, ALU.mult)
                        i0.then_inc(sems["a_y"], 1)
                    add("vector", f)
                    ytgt = inc("a_y", 1)
                    ps_o.set_tok(oi, ("a_y", ytgt))
                    rbs_tok = ("a_y", ytgt)
                hb.set_tok(bi, ("a_y", cur("a_y")))
                exps_at_head_end.append(cur("a_exp"))
                for g2 in range(len(hgroups) - 1):
                    if h == hgroups[g2][-1]:
                        emit_exchange_cc(g2)
                        emit_gather(g2)
                if h < HC - 2:
                    pump(20)

            emit_exchange_cc(len(hgroups) - 1)
            barrier(exclude=("cc_done",))

        with ExitStack() as ph:
            emit_gather(len(hgroups) - 1)
            wpj = DmaRing("wpj", [ph.enter_context(nc.sbuf_tensor(f"wpj{r}", [128, DC * 128], BF16))
                                  for r in range(2)])
            xob = DmaRing("pxo", [ph.enter_context(nc.sbuf_tensor(f"xob{r}", [128, OWN], F32))
                                  for r in range(2)])
            stgp = Ring([ph.enter_context(nc.sbuf_tensor(f"pstg{r}", [128, c.MT], F32))
                         for r in range(4)])
            ps_p = Ring([ph.enter_context(nc.psum_tensor(f"psp{r}", [128, c.MT], F32))
                         for r in range(2)])
            wpr = d_wprojT.ap().rearrange("(i p) f -> p i f", p=128)
            csets = []
            for g, hg in enumerate(hgroups):
                csets.append([p * HC + lh for p in range(2) for lh in hg])
            for o in range(DC):
                wi, wbuf, wtok = wpj.get()
                wait_tok("gpsimd", wtok)
                dma("gpsimd", wbuf[:, :].rearrange("p (i f) -> p i f", f=128),
                    wpr[:, :, o * 128:(o + 1) * 128], wpj.sem(wi))
                wld = cur(wpj.sem(wi))
                pump(7)
                wpj.set_tok(wi, ("pj_mm", (o + 1) * c.NMT))
                xi, xbuf2, xtok = xob.get()
                wait_tok("sync", xtok)
                dma("sync", xbuf2[:, :], d_xownT[o * 128:(o + 1) * 128, :], xob.sem(xi))
                xld = cur(xob.sem(xi))
                xob.set_tok(xi, ("pj_ev", (o + 1) * c.NMT))
                for mt in range(c.NMT):
                    pi, psb, ptok = ps_p.get()
                    wait_tok("tensor", ptok)
                    if mt == 0:
                        wait("tensor", wpj.sem(wi), wld)
                    nci = 0
                    for g, iset in enumerate(csets):
                        wait("tensor", gat_tot[g][0], gat_tot[g][1])
                        first, last = nci == 0, (nci + len(iset) == DC)
                        def f(eng, o=o, mt=mt, psb=psb, wbuf=wbuf, iset=list(iset),
                              first=first, last=last):
                            for j, i in enumerate(iset):
                                mm = eng.matmul(psb[:, :], wbuf[:, i * 128:(i + 1) * 128],
                                                yf_ap(i)[:, mt * c.MT:(mt + 1) * c.MT],
                                                start=(first and j == 0),
                                                stop=(last and j == len(iset) - 1))
                            if last:
                                mm.then_inc(sems["pj_mm"], 1)
                        add("tensor", f)
                        nci += len(iset)
                    mmt = inc("pj_mm", 1)
                    si, sstg, stok2 = stgp.get()
                    wait("vector", "pj_mm", mmt)
                    if mt == 0:
                        wait("vector", xob.sem(xi), xld)
                    wait_tok("vector", stok2)
                    def f(eng, psb=psb, sstg=sstg, xbuf2=xbuf2, mt=mt):
                        i0 = eng.tensor_add(sstg[:, :], psb[:, :],
                                            xbuf2[:, mt * c.MT:(mt + 1) * c.MT])
                        i0.then_inc(sems["pj_ev"], 1)
                    add("vector", f)
                    ps_p.set_tok(pi, ("pj_ev", inc("pj_ev", 1)))
                    wait("sync", "pj_ev", cur("pj_ev"))
                    sp = dma("sync", d_x2[o, :, mt * c.MT:(mt + 1) * c.MT], sstg[:, :],
                             f"psp{si}")
                    stgp.set_tok(si, (f"psp{si}", sp))
            pump_drain()
            barrier()
        ph_y.close()

        if c.SPARSE:
            ph_pump.close()

        ph_b = ExitStack()
        ph_h2 = ExitStack()
        NTK = OWN // 128
        sb_cwtm = (None if c.SPARSE else
                   ph_b.enter_context(nc.sbuf_tensor("cwtm", [128, NTK * c.E], F32)))
        sb_etile_i = ph_b.enter_context(nc.sbuf_tensor("etileI", [1, 2 * c.NTILE_R],
                                                       mybir.dt.int32))
        sb_idxw_i = ph_b.enter_context(nc.sbuf_tensor("idxwI", [128, 2 * c.NTILE_R],
                                                      mybir.dt.int32))
        _h2scope = ph_h2 if c.SPARSE else ph_b
        sb_h2 = [_h2scope.enter_context(nc.sbuf_tensor(f"h2T{i}", [128, OWN], BF16))
                 for i in range(DC)]
        with ExitStack() as ph:
            for r in range(4):
                wait("sync", f"psp{r}", cur(f"psp{r}"))
            emit_ln("ln2", OWN // TT,
                    lambda i, t: d_x2[i, :, t * TT:(t + 1) * TT],
                    sb_h2, ph)
            barrier()

        with ExitStack() as ph:
            gwsb = ph.enter_context(nc.sbuf_tensor("gwsb", [128, DC * c.E], BF16))
            dma("gpsimd", gwsb[:, :].rearrange("p (i e) -> p i e", e=c.E),
                d_gwT.ap().rearrange("(i p) e -> p i e", p=128), "gwl")
            wait("tensor", "gwl", cur("gwl"))
            e8 = Ring([ph.enter_context(nc.sbuf_tensor(f"e8_{r}", [128, c.E], F32))
                       for r in range(2)])
            rsm = ph.enter_context(nc.sbuf_tensor("rsm", [128, 8], F32))
            cwtk = ph.enter_context(nc.sbuf_tensor("cwtk", [128, c.E], F32))
            ps_lg = Ring([ph.enter_context(nc.psum_tensor(f"pslg{r}", [128, c.E], F32))
                          for r in range(2)])
            if c.SPARSE:
                msk8 = ph.enter_context(nc.sbuf_tensor("msk8", [128, c.E], F32))
                sb_H = [ph.enter_context(nc.sbuf_tensor(f"sbH{r}", [128, NTK * c.E], BF16))
                        for r in range(2)]
                sb_w12 = ph.enter_context(nc.sbuf_tensor("sbw12", [128, 2 * NTK], F32))
                sb_cnt = ph.enter_context(nc.sbuf_tensor("sbcnt", [8, 2 * NTK], F32))
                ps_cnt = ph.enter_context(nc.psum_tensor("pscnt", [8, 2], F32))
                wait("sync", "ln_apl", cur("ln_apl"))
                for i in range(DC):
                    dma("sync", d_h2fm[i, :, :], sb_h2[i][:, :], "h2sp")
                sb_idb5 = ph.enter_context(nc.sbuf_tensor("idb5", [128, 128], BF16))
                dma("sync", sb_idb5[:, :], d_idbf[:, :], "mskld")
                idb5_t = cur("mskld")
                hstg = Ring([ph.enter_context(nc.sbuf_tensor(f"hst{r}", [128, c.D], BF16))
                             for r in range(2)])
                ps_ht = ph.enter_context(nc.psum_tensor("psht", [128, 128], BF16))
                wait("tensor", "mskld", idb5_t)
                wait("tensor", "ln_apl", cur("ln_apl"))
                ps_ht_tok = [None]
            for tk in range(NTK):
                if c.SPARSE:
                    tb = tk
                    i7, hbuf, htok = hstg.get()
                    for ic in range(DC):
                        s7 = 0
                        psh = ps_ht[:, :]
                        wait_tok("tensor", ps_ht_tok[s7])
                        def f(eng, psh=psh, ic=ic, tb=tb):
                            mm = eng.transpose(psh, sb_h2[ic][:, tb * 128:(tb + 1) * 128],
                                               sb_idb5[:, :])
                            mm.then_inc(sems["m_tp"], 1)
                        add("tensor", f)
                        t7 = inc("m_tp", 1)
                        wait("vector", "m_tp", t7)
                        if ic == 0:
                            wait_tok("vector", htok)
                        def f(eng, psh=psh, hbuf=hbuf, ic=ic):
                            i0 = eng.tensor_copy(hbuf[:, ic * 128:(ic + 1) * 128], psh)
                            i0.then_inc(sems["m_evS"], 1)
                        add("vector", f)
                        ps_ht_tok[s7] = ("m_evS", inc("m_evS", 1))
                    wait("sync", "m_evS", cur("m_evS"))
                    sp7 = dma("sync", d_h2tm[tb * 128:(tb + 1) * 128, :], hbuf[:, :],
                              "h2sp")
                    hstg.set_tok(i7, ("h2sp", sp7))
                li, lgb, ltok = ps_lg.get()
                wait_tok("tensor", ltok)
                def f(eng, lgb=lgb, tk=tk, h2L=tuple(sb_h2)):
                    for i in range(DC):
                        mm = eng.matmul(lgb[:, :], h2L[i][:, tk * 128:(tk + 1) * 128],
                                        gwsb[:, i * c.E:(i + 1) * c.E],
                                        start=(i == 0), stop=(i == DC - 1))
                    mm.then_inc(sems["rt_mm"], 1)
                add("tensor", f)
                mmt = inc("rt_mm", 1)
                wait("vector", "rt_mm", mmt)
                wait("vector", "rt_exp", cur("rt_exp"))
                vg = inc("vch", 1)
                def f(eng, lgb=lgb, vg=vg):
                    eng.reduce_max(rsm[:, 0:1], lgb[:, :], axis=AX.X).then_inc(sems["vch"], 1)
                    eng.wait_ge(sems["vch"], vg)
                    i0 = eng.tensor_scalar_mul(rsm[:, 1:2], rsm[:, 0:1], -1.0)
                    i0.then_inc(sems["rt_cw"], 1)
                add("vector", f)
                v1 = inc("rt_cw", 1)
                wait("scalar", "rt_cw", v1)
                ei, e8b, etok = e8.get()
                wait_tok("scalar", etok)
                def f(eng, lgb=lgb, e8b=e8b):
                    a = eng.activation(out=e8b[:, :], in_=lgb[:, :], func=AF.Exp,
                                       bias=rsm[:, 1:2], scale=1.0)
                    a.then_inc(sems["rt_exp"], 1)
                add("scalar", f)
                ex = inc("rt_exp", 1)
                ps_lg.set_tok(li, ("rt_exp", ex))
                wait("vector", "rt_exp", ex)
                wait("vector", "rt_cw", cur("rt_cw"))
                if not c.SPARSE:
                    vg0 = cur("vch")
                    inc("vch", 6)
                    def f(eng, e8b=e8b, tk=tk, vg0=vg0):
                        ops = [
                            lambda: eng.tensor_scalar(cwtk[:, :], e8b[:, :], 1.0, None, ALU.is_equal),
                            lambda: eng.scalar_tensor_tensor(cwtk[:, :], cwtk[:, :], -2.0, e8b[:, :],
                                                             ALU.mult, ALU.add),
                            lambda: eng.reduce_max(rsm[:, 2:3], cwtk[:, :], axis=AX.X),
                            lambda: eng.tensor_scalar(rsm[:, 3:4], rsm[:, 2:3], 1.0, None, ALU.add),
                            lambda: eng.reciprocal(rsm[:, 4:5], rsm[:, 3:4]),
                            lambda: eng.tensor_scalar(cwtk[:, :], e8b[:, :], rsm[:, 2:3], None, ALU.is_ge),
                        ]
                        for j, op in enumerate(ops):
                            op().then_inc(sems["vch"], 1)
                            eng.wait_ge(sems["vch"], vg0 + j + 1)
                        i0 = eng.scalar_tensor_tensor(sb_cwtm[:, tk * c.E:(tk + 1) * c.E],
                                                      e8b[:, :], rsm[:, 4:5],
                                                      cwtk[:, :], ALU.mult, ALU.mult)
                        i0.then_inc(sems["rt_cw"], 1)
                    add("vector", f)
                    e8.set_tok(ei, ("rt_cw", inc("rt_cw", 1)))
                else:
                    vg0 = cur("vch")
                    inc("vch", 8)
                    H1s = sb_H[0][:, tk * c.E:(tk + 1) * c.E]
                    H2s = sb_H[1][:, tk * c.E:(tk + 1) * c.E]
                    def f(eng, e8b=e8b, tk=tk, vg0=vg0, H1s=H1s, H2s=H2s):
                        ops = [
                            lambda: eng.tensor_scalar(cwtk[:, :], e8b[:, :], 1.0, None, ALU.is_equal),
                            lambda: eng.tensor_copy(H1s, cwtk[:, :]),
                            lambda: eng.scalar_tensor_tensor(msk8[:, :], cwtk[:, :], -2.0, e8b[:, :],
                                                             ALU.mult, ALU.add),
                            lambda: eng.reduce_max(rsm[:, 2:3], msk8[:, :], axis=AX.X),
                            lambda: eng.tensor_scalar(rsm[:, 3:4], rsm[:, 2:3], 1.0, None, ALU.add),
                            lambda: eng.reciprocal(sb_w12[:, tk:tk + 1], rsm[:, 3:4]),
                            lambda: eng.tensor_scalar(msk8[:, :], e8b[:, :], rsm[:, 2:3], None, ALU.is_ge),
                            lambda: eng.tensor_sub(msk8[:, :], msk8[:, :], cwtk[:, :]),
                        ]
                        for j, op in enumerate(ops):
                            op().then_inc(sems["vch"], 1)
                            eng.wait_ge(sems["vch"], vg0 + j + 1)
                        eng.tensor_copy(H2s, msk8[:, :]).then_inc(sems["vch"], 1)
                        eng.wait_ge(sems["vch"], vg0 + 9)
                        i0 = eng.tensor_mul(sb_w12[:, NTK + tk:NTK + tk + 1],
                                            rsm[:, 2:3], sb_w12[:, tk:tk + 1])
                        i0.then_inc(sems["rt_cw"], 1)
                    add("vector", f)
                    inc("vch", 1)
                    e8.set_tok(ei, ("rt_cw", inc("rt_cw", 1)))
                    rtc = cur("rt_cw")
                    for r in range(2):
                        Hs = sb_H[r][:, tk * c.E:(tk + 1) * c.E]
                        wait("tensor", "rt_cw", rtc)
                        if tk > 0 or r > 0:
                            wait("tensor", "rt_cn", cur("rt_cn"))
                        def f(eng, Hs=Hs, r=r):
                            mm = eng.matmul(ps_cnt[:, r:r + 1], Hs, sb_ones_bf[:, 0:1],
                                            start=True, stop=True)
                            mm.then_inc(sems["rt_mm"], 1)
                        add("tensor", f)
                        mmc = inc("rt_mm", 1)
                        wait("vector", "rt_mm", mmc)
                        def f(eng, r=r, tk=tk):
                            i0 = eng.tensor_copy(sb_cnt[:, r * NTK + tk:r * NTK + tk + 1],
                                                 ps_cnt[:, r:r + 1])
                            i0.then_inc(sems["rt_cn"], 1)
                        add("vector", f)
                        inc("rt_cn", 1)
            rt_done = cur("rt_cw")

            if c.SPARSE:
                sb_ltri = ph.enter_context(nc.sbuf_tensor("ltrisb", [128, 128], BF16))
                sb_iotaT = ph.enter_context(nc.sbuf_tensor("iotaTsb", [1, c.NTILE_R], F32))
                sb_iotaP = ph.enter_context(nc.sbuf_tensor("iotaPsb", [128, 1], F32))
                sb_idf5 = ph.enter_context(nc.sbuf_tensor("idf5", [128, 128], F32))
                dma("sync", sb_ltri[:, :], d_ltriT[:, :], "mskld")
                dma("sync", sb_iotaT[:, :], d_iotaT[:, :], "mskld")
                dma("sync", sb_iotaP[:, :], d_iotaP[:, :], "mskld")
                dma("sync", sb_idf5[:, :], d_idf[:, :], "mskld")
                aux5 = cur("mskld")
                sb_one8 = ph.enter_context(nc.sbuf_tensor("one8", [8, max(NTK, 8)], F32))
                sb_cb = ph.enter_context(nc.sbuf_tensor("sbcb", [8, 2 * NTK], F32))
                sb_sml5 = ph.enter_context(nc.sbuf_tensor("sml5", [8, 8], F32))
                sb_prow = ph.enter_context(nc.sbuf_tensor("prow", [1, 32], F32))
                sb_padb = ph.enter_context(nc.sbuf_tensor("padb", [128, 16], F32))
                sb_etf = ph.enter_context(nc.sbuf_tensor("etf", [1, 2 * c.NTILE_R], F32))
                sb_selb = ph.enter_context(nc.sbuf_tensor("selb", [128, 8], F32))
                sb_t8 = ph.enter_context(nc.sbuf_tensor("t8", [128, 8], F32))
                sb_cbr = ph.enter_context(nc.sbuf_tensor("cbr", [1, 8], F32))
                slt = Ring([ph.enter_context(nc.sbuf_tensor(f"slt{r}", [128, 1], mybir.dt.int32))
                            for r in range(2)])
                tkid = Ring([ph.enter_context(nc.sbuf_tensor(f"tkid{r}", [128, 1], F32))
                             for r in range(2)])
                cwv = Ring([ph.enter_context(nc.sbuf_tensor(f"cwv{r}", [128, 1], F32))
                            for r in range(2)])
                pfo = ph.enter_context(nc.sbuf_tensor("pfo", [128, c.NTILE_R], F32))
                ps_P = Ring([ph.enter_context(nc.psum_tensor(f"psP{r}", [128, 8], F32))
                             for r in range(1)])
                ps_sc = ph.enter_context(nc.psum_tensor("pssc", [1, 8], F32))
                ps_pb = ph.enter_context(nc.psum_tensor("pspb", [128, 8], F32))

                def f(eng):
                    eng.memset(sb_one8[:, :], 1.0).then_inc(sems["pfv"], 1)
                    eng.memset(pfo[:, :], float(OWN)).then_inc(sems["pfv"], 1)
                add("vector", f)
                inc("pfv", 2)
                wait("sync", "pfv", cur("pfv"))
                for r in range(2):
                    dma("sync", d_idxf[r].ap().rearrange("(p a) one -> p (a one)", p=128),
                        pfo[:, :], "ropeld")
                wait("vector", "ropeld", cur("ropeld"))
                def f(eng):
                    eng.memset(pfo[:, :], 0.0).then_inc(sems["pfv"], 1)
                add("vector", f)
                inc("pfv", 1)
                wait("sync", "pfv", cur("pfv"))
                for r in range(2):
                    dma("sync", d_cwsf[r].ap().rearrange("(p a) one -> p (a one)", p=128),
                        pfo[:, :], "ropeld")
                pfl_t = cur("ropeld")

                ps_sc_tok = None
                ps_pb_tok = None
                for r in range(2):
                    cnt_s = sb_cnt[:, r * NTK:(r + 1) * NTK]
                    cb_s = sb_cb[:, r * NTK:(r + 1) * NTK]
                    wait("vector", "rt_cn", cur("rt_cn"))
                    NPK = NTK
                    vg = cur("vch")
                    inc("vch", 3 + NPK)
                    def f(eng, cnt_s=cnt_s, cb_s=cb_s, r=r, vg=vg, NPK=NPK):
                        eng.tensor_tensor_scan(cb_s, cnt_s, sb_one8[:, 0:NTK], 0.0,
                                               ALU.add, ALU.mult).then_inc(sems["vch"], 1)
                        eng.wait_ge(sems["vch"], vg + 1)
                        eng.tensor_sub(cb_s, cb_s, cnt_s).then_inc(sems["vch"], 1)
                        eng.reduce_sum(sb_sml5[:, 4 * r:4 * r + 1], cnt_s, axis=AX.X).then_inc(sems["vch"], 1)
                        eng.wait_ge(sems["vch"], vg + 3)
                        nslot = sb_sml5[:, 4 * r:4 * r + 1]
                        acc = sb_sml5[:, 4 * r + 1:4 * r + 2]
                        w = vg + 3
                        eng.tensor_scalar(acc, nslot, 0.0, None, ALU.is_gt).then_inc(sems["vch"], 1)
                        w += 1
                        eng.wait_ge(sems["vch"], w)
                        for k in range(1, NPK):
                            eng.scalar_tensor_tensor(acc, nslot, float(128 * k), acc,
                                                     ALU.is_gt, ALU.add).then_inc(sems["vch"], 1)
                            w += 1
                            eng.wait_ge(sems["vch"], w)
                        i0 = eng.tensor_scalar(sb_sml5[:, 4 * r + 3:4 * r + 4],
                                               acc, 128.0, None, ALU.mult)
                        i0.then_inc(sems["rt_cn"], 1)
                    add("vector", f)
                    pz = inc("rt_cn", 1)
                    wait("tensor", "rt_cn", pz)
                    wait("tensor", "mskld", aux5)
                    wait_tok("tensor", ps_sc_tok)
                    def f(eng, r=r):
                        mm = eng.transpose(ps_sc[0:1, :], sb_sml5[:, 4 * r + 3:4 * r + 4],
                                           sb_idf5[0:8, 0:8])
                        mm.then_inc(sems["rt_mm"], 1)
                    add("tensor", f)
                    tmm = inc("rt_mm", 1)
                    wait("vector", "rt_mm", tmm)
                    vg = cur("vch")
                    inc("vch", 2)
                    def f(eng, r=r, vg=vg):
                        eng.tensor_copy(sb_prow[0:1, 16 * r:16 * r + 8], ps_sc[0:1, :]).then_inc(sems["vch"], 1)
                        eng.wait_ge(sems["vch"], vg + 1)
                        eng.tensor_tensor_scan(sb_prow[0:1, 16 * r + 8:16 * r + 16],
                                               sb_prow[0:1, 16 * r:16 * r + 8],
                                               sb_one8[0:1, 0:8], 0.0,
                                               ALU.add, ALU.mult).then_inc(sems["vch"], 1)
                        eng.wait_ge(sems["vch"], vg + 2)
                        i0 = eng.tensor_sub(sb_prow[0:1, 16 * r:16 * r + 8],
                                            sb_prow[0:1, 16 * r + 8:16 * r + 16],
                                            sb_prow[0:1, 16 * r:16 * r + 8])
                        i0.then_inc(sems["rt_cn"], 1)
                    add("vector", f)
                    ps_sc_tok = ("rt_cn", inc("rt_cn", 1))
                    prow_t = ps_sc_tok[1]
                    wait("tensor", "rt_cn", prow_t)
                    wait_tok("tensor", ps_pb_tok)
                    def f(eng, r=r):
                        mm = eng.matmul(ps_pb[:, :], sb_ones_f[0:1, :],
                                        sb_prow[0:1, 16 * r:16 * r + 8],
                                        start=True, stop=True)
                        mm.then_inc(sems["rt_mm"], 1)
                    add("tensor", f)
                    bmm = inc("rt_mm", 1)
                    wait("vector", "rt_mm", bmm)
                    def f(eng, r=r):
                        i0 = eng.tensor_copy(sb_padb[:, 8 * r:8 * r + 8], ps_pb[:, :])
                        i0.then_inc(sems["rt_cn"], 1)
                    add("vector", f)
                    ps_pb_tok = ("rt_cn", inc("rt_cn", 1))
                    wait("vector", "rt_cn", cur("rt_cn"))
                    vg = cur("vch")
                    inc("vch", 9)
                    def f(eng, r=r, vg=vg):
                        eng.tensor_scalar(sb_etf[0:1, r * c.NTILE_R:(r + 1) * c.NTILE_R],
                                          sb_iotaT[0:1, :], sb_prow[0:1, 16 * r:16 * r + 1],
                                          None, ALU.is_ge).then_inc(sems["vch"], 1)
                        for e2 in range(1, 8):
                            eng.wait_ge(sems["vch"], vg + e2)
                            eng.scalar_tensor_tensor(
                                sb_etf[0:1, r * c.NTILE_R:(r + 1) * c.NTILE_R],
                                sb_iotaT[0:1, :], sb_prow[0:1, 16 * r + e2:16 * r + e2 + 1],
                                sb_etf[0:1, r * c.NTILE_R:(r + 1) * c.NTILE_R],
                                ALU.is_ge, ALU.add).then_inc(sems["vch"], 1)
                        eng.wait_ge(sems["vch"], vg + 8)
                        eng.tensor_scalar(sb_etf[0:1, r * c.NTILE_R:(r + 1) * c.NTILE_R],
                                          sb_etf[0:1, r * c.NTILE_R:(r + 1) * c.NTILE_R],
                                          -1.0, None, ALU.add).then_inc(sems["vch"], 1)
                        eng.wait_ge(sems["vch"], vg + 9)
                        i0 = eng.tensor_copy(sb_etile_i[0:1, r * c.NTILE_R:(r + 1) * c.NTILE_R],
                                             sb_etf[0:1, r * c.NTILE_R:(r + 1) * c.NTILE_R])
                        i0.then_inc(sems["cmpz"], 1)
                    add("vector", f)
                    inc("cmpz", 1)

                    for tk in range(NTK):
                        Hs = sb_H[r][:, tk * c.E:(tk + 1) * c.E]
                        pi5, Pps, Ptok = ps_P.get()
                        wait_tok("tensor", Ptok)
                        wait("tensor", "rt_cw", rt_done)
                        def f(eng, Pps=Pps, Hs=Hs):
                            mm = eng.matmul(Pps[:, :], sb_ltri[:, :], Hs,
                                            start=True, stop=True)
                            mm.then_inc(sems["rt_mm"], 1)
                        add("tensor", f)
                        pmm = inc("rt_mm", 1)
                        wait("tensor", "rt_cn", cur("rt_cn"))
                        wait_tok("tensor", ps_sc_tok)
                        def f(eng, r=r, tk=tk):
                            mm = eng.transpose(ps_sc[0:1, :],
                                               sb_cb[:, r * NTK + tk:r * NTK + tk + 1],
                                               sb_idf5[0:8, 0:8])
                            mm.then_inc(sems["rt_mm"], 1)
                        add("tensor", f)
                        cmm = inc("rt_mm", 1)
                        wait("vector", "rt_mm", cmm)
                        def f(eng):
                            i0 = eng.tensor_copy(sb_cbr[0:1, :], ps_sc[0:1, :])
                            i0.then_inc(sems["rt_cn"], 1)
                        add("vector", f)
                        ps_sc_tok = ("rt_cn", inc("rt_cn", 1))
                        wait("tensor", "rt_cn", ps_sc_tok[1])
                        wait_tok("tensor", ps_pb_tok)
                        def f(eng):
                            mm = eng.matmul(ps_pb[:, :], sb_ones_f[0:1, :], sb_cbr[0:1, :],
                                            start=True, stop=True)
                            mm.then_inc(sems["rt_mm"], 1)
                        add("tensor", f)
                        bmm2 = inc("rt_mm", 1)
                        si5, slti, stok5 = slt.get()
                        ti5, tkb, ttok5 = tkid.get()
                        ci5, cwb5, ctok5 = cwv.get()
                        wait("vector", "rt_mm", max(pmm, bmm2))
                        wait_tok("vector", stok5)
                        wait_tok("vector", ttok5)
                        wait_tok("vector", ctok5)
                        vg = cur("vch")
                        inc("vch", 5)
                        wsrc = sb_w12[:, (r * NTK) + tk:(r * NTK) + tk + 1]
                        def f(eng, r=r, tk=tk, Pps=Pps, Hs=Hs, slti=slti, tkb=tkb,
                              cwb5=cwb5, vg=vg, wsrc=wsrc):
                            eng.tensor_add(sb_selb[:, :], ps_pb[:, :],
                                           sb_padb[:, 8 * r:8 * r + 8]).then_inc(sems["vch"], 1)
                            eng.wait_ge(sems["vch"], vg + 1)
                            eng.tensor_add(sb_t8[:, :], Pps[:, :], sb_selb[:, :]).then_inc(sems["vch"], 1)
                            eng.wait_ge(sems["vch"], vg + 2)
                            eng.tensor_mul(sb_t8[:, :], sb_t8[:, :], Hs).then_inc(sems["vch"], 1)
                            eng.wait_ge(sems["vch"], vg + 3)
                            eng.reduce_sum(sb_t8[:, 0:1], sb_t8[:, :], axis=AX.X).then_inc(sems["vch"], 1)
                            eng.wait_ge(sems["vch"], vg + 4)
                            eng.tensor_copy(slti[:, :], sb_t8[:, 0:1]).then_inc(sems["vch"], 1)
                            eng.tensor_scalar(tkb[:, :], sb_iotaP[:, :], float(tk * 128),
                                              None, ALU.add)
                            i0 = eng.tensor_copy(cwb5[:, :], wsrc)
                            i0.then_inc(sems["cmpv"], 1)
                        add("vector", f)
                        cmv = inc("cmpv", 1)
                        ps_P.set_tok(pi5, ("cmpv", cmv))
                        ps_pb_tok = ("cmpv", cmv)
                        wait("gpsimd", "cmpv", cmv)
                        wait("gpsimd", "ropeld", pfl_t)
                        ssem = f"sct{si5}"
                        getsem(ssem)
                        def f(eng, r=r, slti=slti, tkb=tkb, cwb5=cwb5, ssem=ssem):
                            n0 = nc.n_instructions()
                            d0 = eng.indirect_dma_start(
                                out=d_idxf[r].ap(),
                                out_offset=bass.IndirectOffsetOnAxis(slti[:, 0:1], axis=0),
                                in_=tkb[:, :], in_offset=None,
                                bounds_check=c.NSLOT_R - 1, oob_is_err=False)
                            d0.then_inc(sems[ssem], 16)
                            d1 = eng.indirect_dma_start(
                                out=d_cwsf[r].ap(),
                                out_offset=bass.IndirectOffsetOnAxis(slti[:, 0:1], axis=0),
                                in_=cwb5[:, :], in_offset=None,
                                bounds_check=c.NSLOT_R - 1, oob_is_err=False)
                            d1.then_inc(sems[ssem], 16)
                            _free_val_regs(nc, eng, n0)
                        add("gpsimd", f)
                        sc5 = inc(ssem, 32)
                        slt.set_tok(si5, (ssem, sc5))
                        tkid.set_tok(ti5, (ssem, sc5))
                        cwv.set_tok(ci5, (ssem, sc5))

                NT_ALL = 2 * c.NTILE_R
                sb_chg = ph.enter_context(nc.sbuf_tensor("chg", [1, NT_ALL], F32))
                sb_wi1 = ph.enter_context(nc.sbuf_tensor("wi1", [128, NT_ALL], F32))
                ps_wi = ph.enter_context(nc.psum_tensor("pswi", [128, NT_ALL], F32))
                wait("vector", "cmpz", cur("cmpz"))
                vg = cur("vch")
                inc("vch", 3)
                def f(eng, vg=vg):
                    eng.memset(sb_chg[0:1, :], 1.0).then_inc(sems["vch"], 1)
                    eng.wait_ge(sems["vch"], vg + 1)
                    eng.tensor_tensor(sb_chg[0:1, 1:NT_ALL], sb_etf[0:1, 1:NT_ALL],
                                      sb_etf[0:1, 0:NT_ALL - 1],
                                      ALU.not_equal).then_inc(sems["vch"], 1)
                    eng.wait_ge(sems["vch"], vg + 2)
                    eng.tensor_scalar(sb_chg[0:1, :], sb_chg[0:1, :],
                                      float(-c.E * 128), None,
                                      ALU.mult).then_inc(sems["vch"], 1)
                    eng.wait_ge(sems["vch"], vg + 3)
                    i0 = eng.scalar_tensor_tensor(sb_chg[0:1, :], sb_etf[0:1, :],
                                                  128.0, sb_chg[0:1, :], ALU.mult, ALU.add)
                    i0.then_inc(sems["cmpz"], 1)
                add("vector", f)
                chg1 = inc("cmpz", 1)
                wait("vector", "cmpz", chg1)
                def f(eng):
                    i0 = eng.tensor_scalar(sb_chg[0:1, :], sb_chg[0:1, :],
                                           float(c.E * 128), None, ALU.add)
                    i0.then_inc(sems["cmpz"], 1)
                add("vector", f)
                chg2 = inc("cmpz", 1)
                wait("tensor", "cmpz", chg2)
                wait_tok("tensor", ps_pb_tok)
                def f(eng):
                    mm = eng.matmul(ps_wi[:, :], sb_ones_f[0:1, :], sb_chg[0:1, :],
                                    start=True, stop=True)
                    mm.then_inc(sems["rt_mm"], 1)
                add("tensor", f)
                wmm = inc("rt_mm", 1)
                wait("vector", "rt_mm", wmm)
                vg2 = cur("vch")
                inc("vch", 1)
                def f(eng, vg2=vg2):
                    eng.tensor_scalar(sb_wi1[:, :], ps_wi[:, :], sb_iotaP[:, 0:1],
                                      None, ALU.add).then_inc(sems["vch"], 1)
                    eng.wait_ge(sems["vch"], vg2 + 1)
                    i0 = eng.tensor_copy(sb_idxw_i[:, :], sb_wi1[:, :])
                    i0.then_inc(sems["cmpz"], 1)
                add("vector", f)
                inc("cmpz", 1)

            barrier()

        if c.SPARSE:
            ph_h2.close()
            with ExitStack() as ph:
                NT_ALL = 2 * c.NTILE_R
                FSL = [(fs * 512, min(512, c.F - fs * 512))
                       for fs in range((c.F + 511) // 512)]
                sb_idbf = ph.enter_context(nc.sbuf_tensor("idbfsb", [128, 128], BF16))
                sb_idf6 = ph.enter_context(nc.sbuf_tensor("idf6", [128, 128], F32))
                dma("sync", sb_idbf[:, :], d_idbf[:, :], "yixld")
                dma("sync", sb_idf6[:, :], d_idf[:, :], "yixld")
                ps_A = ph.enter_context(nc.psum_tensor("psA6", [128, c.F], F32))
                ps_B = ph.enter_context(nc.psum_tensor("psB6", [128, c.F], F32))
                ps_i = ps_A
                sb_ir = [ph.enter_context(nc.sbuf_tensor(f"idxraw{r}", [c.NTILE_R, 128], F32))
                         for r in range(2)]
                sb_cr = [ph.enter_context(nc.sbuf_tensor(f"cwsraw{r}", [c.NTILE_R, 128], F32))
                         for r in range(2)]
                sb_idx = ph.enter_context(nc.sbuf_tensor("sbidx", [128, NT_ALL], mybir.dt.int32))
                sb_cws = ph.enter_context(nc.sbuf_tensor("sbcws", [128, NT_ALL], F32))
                wait("sync", "sct0", cur("sct0"))
                wait("sync", "sct1", cur("sct1"))
                for r in range(2):
                    dma("sync", sb_ir[r][:, :],
                        d_idxf[r].ap().rearrange("(j p) one -> j (p one)", p=128), "yixld")
                    dma("sync", sb_cr[r][:, :],
                        d_cwsf[r].ap().rearrange("(j p) one -> j (p one)", p=128), "yixld")
                aux6 = cur("yixld")
                ps_x = Ring([ph.enter_context(nc.psum_tensor(f"psx{r}", [128, 128], BF16))
                             for r in range(2)])
                wait("tensor", "yixld", aux6)
                NTr = c.NTILE_R
                def f(eng):
                    eng.transpose(ps_i[:, 0:NTr], sb_ir[0][:, :], sb_idf6[0:NTr, 0:NTr])
                    eng.transpose(ps_i[:, NTr:NT_ALL], sb_ir[1][:, :],
                                  sb_idf6[0:NTr, 0:NTr]).then_inc(sems["m_tp"], 1)
                add("tensor", f)
                t1 = inc("m_tp", 1)
                wait("vector", "m_tp", t1)
                def f(eng):
                    i0 = eng.tensor_copy(sb_idx[:, :], ps_i[:, 0:NT_ALL])
                    i0.then_inc(sems["sidx"], 1)
                add("vector", f)
                inc("sidx", 1)
                wait("tensor", "sidx", cur("sidx"))
                def f(eng):
                    eng.transpose(ps_i[:, 0:NTr], sb_cr[0][:, :], sb_idf6[0:NTr, 0:NTr])
                    eng.transpose(ps_i[:, NTr:NT_ALL], sb_cr[1][:, :],
                                  sb_idf6[0:NTr, 0:NTr]).then_inc(sems["m_tp"], 1)
                add("tensor", f)
                t2 = inc("m_tp", 1)
                wait("vector", "m_tp", t2)
                def f(eng):
                    i0 = eng.tensor_copy(sb_cws[:, :], ps_i[:, 0:NT_ALL])
                    i0.then_inc(sems["sidx"], 1)
                add("vector", f)
                inc("sidx", 1)

                wgsb = ph.enter_context(nc.sbuf_tensor("wgsb6", [128, DC * c.F], BF16))
                wusb = ph.enter_context(nc.sbuf_tensor("wusb6", [128, DC * c.F], BF16))
                wdsb = ph.enter_context(nc.sbuf_tensor("wdsb6", [128, c.FC * c.D], BF16))
                xg = DmaRing("xg", [ph.enter_context(nc.sbuf_tensor(f"xg{r}", [128, c.D], BF16))
                                    for r in range(2)])
                XT = Ring([ph.enter_context(nc.sbuf_tensor(f"XT{r}", [128, DC * 128], BF16))
                           for r in range(2)])
                sgb = Ring([ph.enter_context(nc.sbuf_tensor(f"sgb6{r}", [128, c.F], BF16))
                            for r in range(1)])
                ub = Ring([ph.enter_context(nc.sbuf_tensor(f"ub6{r}", [128, c.F], BF16))
                           for r in range(1)])
                gq = Ring([ph.enter_context(nc.sbuf_tensor(f"gq6{r}", [128, c.F], BF16))
                           for r in range(1)])
                gT = Ring([ph.enter_context(nc.sbuf_tensor(f"gT6{r}", [128, c.FC * 128], BF16))
                           for r in range(2)])
                eob = Ring([ph.enter_context(nc.sbuf_tensor(f"eob{r}", [128, c.D], BF16))
                            for r in range(2)])
                pump_done_waits("sync")
                wait("sync", "cmpz", cur("cmpz"))
                wait("gpsimd", "sidx", cur("sidx"))
                wait("scalar", "sidx", cur("sidx"))
                wait("vector", "sidx", cur("sidx"))
                psA_tok = None
                psB_tok = None
                if True:
                    for j in range(NT_ALL):
                        rblk = 0 if j < c.NTILE_R else 1
                        d_yr = d_y1 if rblk == 0 else d_y2
                        ga_pre, up_pre, dn_pre = cur("m_ga"), cur("m_up"), cur("m_dn")
                        wait("gpsimd", "m_ga", ga_pre)
                        wait("gpsimd", "m_up", up_pre)
                        wait("gpsimd", "m_dn", dn_pre)
                        def f(eng, j=j):
                            n0 = nc.n_instructions()
                            eng.indirect_dma_start(
                                out=wgsb[:, :], out_offset=None,
                                in_=d_wgbf.ap(),
                                in_offset=bass.IndirectOffsetOnAxis(sb_idxw_i[:, j:j + 1], axis=0),
                                bounds_check=c.E * 128 - 1,
                                oob_is_err=False).then_inc(sems["swg"], 16)
                            eng.indirect_dma_start(
                                out=wusb[:, :], out_offset=None,
                                in_=d_wubf.ap(),
                                in_offset=bass.IndirectOffsetOnAxis(sb_idxw_i[:, j:j + 1], axis=0),
                                bounds_check=c.E * 128 - 1,
                                oob_is_err=False).then_inc(sems["swu"], 16)
                            eng.indirect_dma_start(
                                out=wdsb[:, :], out_offset=None,
                                in_=d_wdbf.ap(),
                                in_offset=bass.IndirectOffsetOnAxis(sb_idxw_i[:, j:j + 1], axis=0),
                                bounds_check=c.E * 128 - 1,
                                oob_is_err=False).then_inc(sems["swd"], 16)
                            _free_val_regs(nc, eng, n0)
                        add("gpsimd", f)
                        inc("swg", 16); inc("swu", 16); inc("swd", 16)
                        gi6, xbuf6, xtok6 = xg.get()
                        wait_tok("gpsimd", xtok6)
                        def f(eng, xbuf6=xbuf6, j=j, gs=xg.sem(gi6)):
                            n0 = nc.n_instructions()
                            dd = eng.indirect_dma_start(
                                out=xbuf6[:, :], out_offset=None,
                                in_=d_h2tm.ap(),
                                in_offset=bass.IndirectOffsetOnAxis(sb_idx[:, j:j + 1], axis=0),
                                bounds_check=OWN - 1, oob_is_err=False)
                            dd.then_inc(sems[gs], 16)
                            _free_val_regs(nc, eng, n0)
                        add("gpsimd", f)
                        xld = inc(xg.sem(gi6), 16)
                        xi6, XTb, XTtok = XT.get()
                        wait("tensor", xg.sem(gi6), xld)
                        wait_tok("tensor", XTtok)
                        for ic in range(DC):
                            pi6, psx, pxtok = ps_x.get()
                            wait_tok("tensor", pxtok)
                            def f(eng, psx=psx, xbuf6=xbuf6, ic=ic):
                                mm = eng.transpose(psx[:, :], xbuf6[:, ic * 128:(ic + 1) * 128],
                                                   sb_idbf[:, :])
                                mm.then_inc(sems["m_tp"], 1)
                            add("tensor", f)
                            tmm6 = inc("m_tp", 1)
                            wait("vector", "m_tp", tmm6)
                            def f(eng, psx=psx, XTb=XTb, ic=ic):
                                i0 = eng.tensor_copy(XTb[:, ic * 128:(ic + 1) * 128], psx[:, :])
                                i0.then_inc(sems["m_cb"], 1)
                            add("vector", f)
                            ps_x.set_tok(pi6, ("m_cb", inc("m_cb", 1)))
                        xev = cur("m_cb")
                        xg.set_tok(gi6, ("m_tp", cur("m_tp")))
                        if j == 0:
                            wait("tensor", "sidx", cur("sidx"))
                        wait("tensor", "swg", 16 * (j + 1))
                        wait("tensor", "m_cb", xev)
                        wait_tok("tensor", psA_tok)
                        def f(eng, XTb=XTb):
                            for ic in range(DC):
                                for fs0, fn in FSL:
                                    mm = eng.matmul(ps_A[:, fs0:fs0 + fn],
                                                    XTb[:, ic * 128:(ic + 1) * 128],
                                                    wgsb[:, ic * c.F + fs0:ic * c.F + fs0 + fn],
                                                    start=(ic == 0), stop=(ic == DC - 1))
                            mm.then_inc(sems["m_ga"], 1)
                        add("tensor", f)
                        gat = inc("m_ga", 1)
                        si6, sgb6, sgtok6 = sgst_ = sgb.get()
                        wait("scalar", "m_ga", gat)
                        wait_tok("scalar", sgtok6)
                        def f(eng, sgb6=sgb6):
                            a = eng.activation(out=sgb6[:, :], in_=ps_A[:, :], func=AF.Silu)
                            a.then_inc(sems["m_evS"], 1)
                        add("scalar", f)
                        evs6 = inc("m_evS", 1)
                        wait("tensor", "swu", 16 * (j + 1))
                        wait_tok("tensor", psB_tok)
                        def f(eng, XTb=XTb):
                            for ic in range(DC):
                                for fs0, fn in FSL:
                                    mm = eng.matmul(ps_B[:, fs0:fs0 + fn],
                                                    XTb[:, ic * 128:(ic + 1) * 128],
                                                    wusb[:, ic * c.F + fs0:ic * c.F + fs0 + fn],
                                                    start=(ic == 0), stop=(ic == DC - 1))
                            mm.then_inc(sems["m_up"], 1)
                        add("tensor", f)
                        upt = inc("m_up", 1)
                        XT.set_tok(xi6, ("m_up", upt))
                        ui6, ub6, ubtok6 = ub.get()
                        wait("vector", "m_up", upt)
                        wait_tok("vector", ubtok6)
                        def f(eng, ub6=ub6, j=j):
                            i0 = eng.tensor_scalar(ub6[:, :], ps_B[:, :],
                                                   sb_cws[:, j:j + 1], None, ALU.mult)
                            i0.then_inc(sems["m_evV"], 1)
                        add("vector", f)
                        evv6 = inc("m_evV", 1)
                        qi6, gq6, gqtok6 = gq.get()
                        wait("vector", "m_evS", evs6)
                        wait("vector", "m_evV", evv6)
                        wait_tok("vector", gqtok6)
                        def f(eng, gq6=gq6, sgb6=sgb6, ub6=ub6):
                            i0 = eng.tensor_mul(gq6[:, :], sgb6[:, :], ub6[:, :])
                            i0.then_inc(sems["m_gp"], 1)
                        add("vector", f)
                        gp6 = inc("m_gp", 1)
                        sgb.set_tok(si6, ("m_gp", gp6))
                        ub.set_tok(ui6, ("m_gp", gp6))
                        gi7, gTb, gTtok = gT.get()
                        wait("tensor", "m_gp", gp6)
                        wait_tok("tensor", gTtok)
                        for fc in range(c.FC):
                            pi7, psx, pxtok = ps_x.get()
                            wait_tok("tensor", pxtok)
                            def f(eng, psx=psx, gq6=gq6, fc=fc):
                                mm = eng.transpose(psx[:, :], gq6[:, fc * 128:(fc + 1) * 128],
                                                   sb_idbf[:, :])
                                mm.then_inc(sems["m_tp"], 1)
                            add("tensor", f)
                            tmm7 = inc("m_tp", 1)
                            wait("vector", "m_tp", tmm7)
                            def f(eng, psx=psx, gTb=gTb, fc=fc):
                                i0 = eng.tensor_copy(gTb[:, fc * 128:(fc + 1) * 128], psx[:, :])
                                i0.then_inc(sems["m_cbe"], 1)
                            add("vector", f)
                            ps_x.set_tok(pi7, ("m_cbe", inc("m_cbe", 1)))
                        gev = cur("m_cbe")
                        gq.set_tok(qi6, ("m_tp", cur("m_tp")))
                        ei6, eo6, eotok6 = eob.get()
                        DH = c.D // 2
                        DNS = [(nn * 512, min(512, DH - nn * 512))
                               for nn in range((DH + 511) // 512)]
                        for hhf, pst in ((0, ps_A), (1, ps_B)):
                            wait("tensor", "m_cbe", gev)
                            wait("tensor", "swd", 16 * (j + 1))
                            if hhf == 0:
                                wait("tensor", "m_evS", evs6)
                            else:
                                wait("tensor", "m_evV", evv6)
                            def f(eng, gTb=gTb, pst=pst, hhf=hhf, DH=DH, DNS=DNS):
                                for fc in range(c.FC):
                                    for n0, nw in DNS:
                                        mm = eng.matmul(
                                            pst[:, n0:n0 + nw],
                                            gTb[:, fc * 128:(fc + 1) * 128],
                                            wdsb[:, fc * c.D + hhf * DH + n0:
                                                 fc * c.D + hhf * DH + n0 + nw],
                                            start=(fc == 0), stop=(fc == c.FC - 1))
                                mm.then_inc(sems["m_dn"], 1)
                            add("tensor", f)
                            dnt = inc("m_dn", 1)
                            wait("vector", "m_dn", dnt)
                            if hhf == 0:
                                wait_tok("vector", eotok6)
                            def f(eng, eo6=eo6, pst=pst, hhf=hhf, DH=DH):
                                i0 = eng.tensor_copy(eo6[:, hhf * DH:(hhf + 1) * DH],
                                                     pst[:, 0:DH])
                                i0.then_inc(sems["m_acc"], 1)
                            add("vector", f)
                            acc6 = inc("m_acc", 1)
                            if hhf == 0:
                                psA_tok = ("m_acc", acc6)
                            else:
                                psB_tok = ("m_acc", acc6)
                        gT.set_tok(gi7, ("m_dn", cur("m_dn")))
                        wait("gpsimd", "m_acc", cur("m_acc"))
                        def f(eng, eo6=eo6, j=j, d_yr=d_yr, es=f"eos{ei6}"):
                            n0 = nc.n_instructions()
                            dd = eng.indirect_dma_start(
                                out=d_yr.ap(),
                                out_offset=bass.IndirectOffsetOnAxis(sb_idx[:, j:j + 1], axis=0),
                                in_=eo6[:, :], in_offset=None,
                                bounds_check=OWN - 1, oob_is_err=False)
                            dd.then_inc(sems[es], 16)
                            _free_val_regs(nc, eng, n0)
                        add("gpsimd", f)
                        eob.set_tok(ei6, (f"eos{ei6}", inc(f"eos{ei6}", 16)))
                barrier()

        with ExitStack() as ph:
            GMAX = 6
            MH, MQ, NMQ = c.MH, c.MQ, c.NMQ
            if c.SPARSE:
                sb_h2 = [ph.enter_context(nc.sbuf_tensor(f"h2Tb{i}", [128, OWN], BF16))
                         for i in range(DC)]
                for i in range(DC):
                    dma("sync", sb_h2[i][:, :], d_h2fm[i, :, :], "h2sp")
                wait("tensor", "h2sp", cur("h2sp"))
                wait("vector", "h2sp", cur("h2sp"))
            sb_idf = ph.enter_context(nc.sbuf_tensor("idfsb", [128, 128], F32))
            dma("sync", sb_idf[:, :], d_idf[:, :], "idfld")
            wait("tensor", "idfld", cur("idfld"))
            sb_yacc = [ph.enter_context(nc.sbuf_tensor(f"yac{i}", [128, MH], BF16))
                       for i in range(DC)]
            wg_r = DmaRing("wpj", [ph.enter_context(nc.sbuf_tensor(f"mwg{r}", [128, DC * 128], BF16))
                                   for r in range(2)])
            wu_r = DmaRing("mwu", [ph.enter_context(nc.sbuf_tensor(f"mwu{r}", [128, DC * 128], BF16))
                                   for r in range(2)])
            wd_r = DmaRing("mwd", [ph.enter_context(nc.sbuf_tensor(f"mwd{r}", [128, DC * 128], BF16))
                                   for r in range(GMAX)])
            gbuf = [ph.enter_context(nc.sbuf_tensor(f"gbf{j}", [128, MH], BF16))
                    for j in range(GMAX)]
            sgst = Ring([ph.enter_context(nc.sbuf_tensor(f"sgs{r}", [128, MH], BF16))
                         for r in range(2)])
            usst = Ring([ph.enter_context(nc.sbuf_tensor(f"uss{r}", [128, MH], BF16))
                         for r in range(2)])
            cwrow = ph.enter_context(nc.sbuf_tensor("cwrow", [1, MH], F32))
            cwb = ph.enter_context(nc.sbuf_tensor("cwbb", [128, MH], F32))
            xo2 = DmaRing("ox", [ph.enter_context(nc.sbuf_tensor(f"xo2{r}", [128, MH], F32))
                                 for r in range(2)])
            yrb = DmaRing("qsp", [ph.enter_context(nc.sbuf_tensor(f"yrb{r}", [128, MH], BF16))
                                  for r in range(4)])
            ost = Ring([ph.enter_context(nc.sbuf_tensor(f"ost{r}", [128, MH], F32))
                        for r in range(2)])
            getsem("ost0"); getsem("ost1")
            ps_m = Ring([ph.enter_context(nc.psum_tensor(f"psm{r}", [128, MH], F32))
                         for r in range(4)])

            def stream_w(ring, src_ap, three_d):
                i, buf, tok = ring.get()
                wait_tok("gpsimd", tok)
                if three_d:
                    dma("gpsimd", buf[:, :].rearrange("p (i f) -> p i f", f=128),
                        src_ap, ring.sem(i))
                else:
                    dma("gpsimd", buf[:, :], src_ap, ring.sem(i))
                return i, buf, (ring.sem(i), cur(ring.sem(i)))

            units = ([] if c.SPARSE else [("e", e) for e in range(c.E)]) + [("s", None)]
            gbuf_tok = [None] * GMAX
            cwb_tok = None
            cwrow_tok = None

            for th in range(c.TH):
                t0 = th * MH
                last_acc = [None] * DC
                first_dn = [True] * DC
                for ukind, ue in units:
                    nfc = c.FC if ukind == "e" else c.FSC
                    if ukind == "e":
                        w_g3 = d_wgT.ap()[ue].rearrange("(i p) f -> p i f", p=128)
                        w_u3 = d_wuT.ap()[ue].rearrange("(i p) f -> p i f", p=128)
                        d_wd2 = d_wdT.ap()[ue]
                    else:
                        w_g3 = d_swgT.ap().rearrange("(i p) f -> p i f", p=128)
                        w_u3 = d_swuT.ap().rearrange("(i p) f -> p i f", p=128)
                        d_wd2 = d_swdT.ap()

                    if ukind == "e":
                        pi, psb, ptok = ps_m.get()
                        wait_tok("tensor", ptok)
                        wait("tensor", "rt_cw", rt_done)
                        ntk_h = MH // 128
                        for jtk in range(ntk_h):
                            tk = th * ntk_h + jtk
                            def f(eng, tk=tk, jtk=jtk, psb=psb, ue=ue):
                                mm = eng.transpose(psb[0:1, jtk * 128:(jtk + 1) * 128],
                                                   sb_cwtm[:, tk * c.E + ue:tk * c.E + ue + 1],
                                                   sb_idf[:, :])
                                mm.then_inc(sems["m_tp"], 1)
                            add("tensor", f)
                            inc("m_tp", 1)
                        wait("vector", "m_tp", cur("m_tp"))
                        wait_tok("vector", cwrow_tok)
                        def f(eng, psb=psb):
                            i0 = eng.tensor_copy(cwrow[0:1, :], psb[0:1, 0:MH])
                            i0.then_inc(sems["m_cb"], 1)
                        add("vector", f)
                        cbr = inc("m_cb", 1)
                        wait("tensor", "m_cb", cbr)
                        def f(eng, psb=psb):
                            for mq in range(NMQ):
                                mm = eng.matmul(psb[:, mq * MQ:(mq + 1) * MQ],
                                                sb_ones_f[0:1, :],
                                                cwrow[0:1, mq * MQ:(mq + 1) * MQ],
                                                start=True, stop=True)
                            mm.then_inc(sems["m_cb"], 1)
                        add("tensor", f)
                        cbb = inc("m_cb", 1)
                        cwrow_tok = ("m_cb", cbb)
                        wait("vector", "m_cb", cbb)
                        wait_tok("vector", cwb_tok)
                        def f(eng, psb=psb):
                            i0 = eng.tensor_copy(cwb[:, :], psb[:, :])
                            i0.then_inc(sems["m_cbe"], 1)
                        add("vector", f)
                        cbe = inc("m_cbe", 1)
                        ps_m.set_tok(pi, ("m_cbe", cbe))

                    for grp in _fc_groups(nfc, GMAX):
                        grp_dw = []
                        for jj, fc in enumerate(grp):
                            gwi, gwb, gwt = stream_w(wg_r, w_g3[:, :, fc * 128:(fc + 1) * 128], True)
                            uwi, uwb, uwt = stream_w(wu_r, w_u3[:, :, fc * 128:(fc + 1) * 128], True)
                            dwi, dwb, dwt = stream_w(wd_r, d_wd2[fc * 128:(fc + 1) * 128, :], False)
                            grp_dw.append((dwi, dwb, dwt))
                            pi, psg, ptok = ps_m.get()
                            wait_tok("tensor", ptok)
                            wait_tok("tensor", gwt)
                            def f(eng, psg=psg, gwb=gwb, t0=t0, h2L=tuple(sb_h2)):
                                for i in range(DC):
                                    for mq in range(NMQ):
                                        mm = eng.matmul(psg[:, mq * MQ:(mq + 1) * MQ],
                                                        gwb[:, i * 128:(i + 1) * 128],
                                                        h2L[i][:, t0 + mq * MQ:t0 + (mq + 1) * MQ],
                                                        start=(i == 0), stop=(i == DC - 1))
                                mm.then_inc(sems["m_gu"], 1)
                            add("tensor", f)
                            gm = inc("m_gu", 1)
                            wg_r.set_tok(gwi, ("m_gu", gm))
                            sgi, sgb, sgtok = sgst.get()
                            wait("scalar", "m_gu", gm)
                            wait_tok("scalar", sgtok)
                            def f(eng, psg=psg, sgb=sgb):
                                a = eng.activation(out=sgb[:, :], in_=psg[:, :], func=AF.Silu)
                                a.then_inc(sems["m_evS"], 1)
                            add("scalar", f)
                            evs = inc("m_evS", 1)
                            ps_m.set_tok(pi, ("m_evS", evs))
                            pi2, psu, ptok2 = ps_m.get()
                            wait_tok("tensor", ptok2)
                            wait_tok("tensor", uwt)
                            def f(eng, psu=psu, uwb=uwb, t0=t0, h2L=tuple(sb_h2)):
                                for i in range(DC):
                                    for mq in range(NMQ):
                                        mm = eng.matmul(psu[:, mq * MQ:(mq + 1) * MQ],
                                                        uwb[:, i * 128:(i + 1) * 128],
                                                        h2L[i][:, t0 + mq * MQ:t0 + (mq + 1) * MQ],
                                                        start=(i == 0), stop=(i == DC - 1))
                                mm.then_inc(sems["m_gu"], 1)
                            add("tensor", f)
                            um = inc("m_gu", 1)
                            wu_r.set_tok(uwi, ("m_gu", um))
                            usi, usb, ustok = usst.get()
                            wait("vector", "m_gu", um)
                            wait_tok("vector", ustok)
                            if ukind == "e":
                                wait("vector", "m_cbe", cur("m_cbe"))
                                def f(eng, psu=psu, usb=usb):
                                    i0 = eng.tensor_mul(usb[:, :], psu[:, :], cwb[:, :])
                                    i0.then_inc(sems["m_evV"], 1)
                                add("vector", f)
                            else:
                                def f(eng, psu=psu, usb=usb):
                                    i0 = eng.tensor_copy(usb[:, :], psu[:, :])
                                    i0.then_inc(sems["m_evV"], 1)
                                add("vector", f)
                            evv = inc("m_evV", 1)
                            ps_m.set_tok(pi2, ("m_evV", evv))
                            wait("vector", "m_evS", evs)
                            wait("vector", "m_evV", evv)
                            wait_tok("vector", gbuf_tok[jj])
                            def f(eng, jj=jj, sgb=sgb, usb=usb):
                                i0 = eng.tensor_mul(gbuf[jj][:, :], sgb[:, :], usb[:, :])
                                i0.then_inc(sems["m_gp"], 1)
                            add("vector", f)
                            gp = inc("m_gp", 1)
                            sgst.set_tok(sgi, ("m_gp", gp))
                            usst.set_tok(usi, ("m_gp", gp))
                        gp_end = cur("m_gp")
                        for hc in range(DC):
                            pi, psd, ptok = ps_m.get()
                            wait_tok("tensor", ptok)
                            for jj, fc in enumerate(grp):
                                dwi, dwb, dwt = grp_dw[jj]
                                if hc == 0:
                                    wait_tok("tensor", dwt)
                                    wait("tensor", "m_gp", gp_end - len(grp) + jj + 1)
                                def f(eng, psd=psd, dwb=dwb, jj=jj, hc=hc,
                                      first=(jj == 0), last=(jj == len(grp) - 1)):
                                    for mq in range(NMQ):
                                        mm = eng.matmul(psd[:, mq * MQ:(mq + 1) * MQ],
                                                        dwb[:, hc * 128:(hc + 1) * 128],
                                                        gbuf[jj][:, mq * MQ:(mq + 1) * MQ],
                                                        start=(first and mq >= 0),
                                                        stop=last)
                                    if last:
                                        mm.then_inc(sems["m_dn"], 1)
                                add("tensor", f)
                            dn = inc("m_dn", 1)
                            wait("vector", "m_dn", dn)
                            wait_tok("vector", last_acc[hc])
                            if first_dn[hc]:
                                first_dn[hc] = False
                                def f(eng, psd=psd, hc=hc):
                                    i0 = eng.tensor_copy(sb_yacc[hc][:, :], psd[:, :])
                                    i0.then_inc(sems["m_acc"], 1)
                            else:
                                def f(eng, psd=psd, hc=hc):
                                    i0 = eng.scalar_tensor_tensor(sb_yacc[hc][:, :], psd[:, :],
                                                                  1.0, sb_yacc[hc][:, :],
                                                                  ALU.mult, ALU.add)
                                    i0.then_inc(sems["m_acc"], 1)
                            add("vector", f)
                            acc_t = inc("m_acc", 1)
                            last_acc[hc] = ("m_acc", acc_t)
                            ps_m.set_tok(pi, ("m_acc", acc_t))
                        for jj in range(len(grp)):
                            dwi, dwb, dwt = grp_dw[jj]
                            wd_r.set_tok(dwi, ("m_dn", cur("m_dn")))
                            gbuf_tok[jj] = ("m_dn", cur("m_dn"))
                    if ukind == "e":
                        cwb_tok = ("m_evV", cur("m_evV"))

                wait("vector", "m_acc", cur("m_acc"))
                if c.SPARSE and th == 0 and True:
                    wait("sync", "eos0", cur("eos0"))
                    wait("sync", "eos1", cur("eos1"))
                NBLK = MH // 128
                for hc in range(DC):
                    xi, xb, xtok = xo2.get()
                    wait_tok("sync", xtok)
                    xt = dma("sync", xb[:, :], d_x2[hc, :, t0:t0 + MH], xo2.sem(xi))
                    if c.SPARSE:
                        y1i, y1b, y1tok = yrb.get()
                        wait_tok("sync", y1tok)
                        y2i, y2b, y2tok = yrb.get()
                        wait_tok("sync", y2tok)
                        dma("sync", y1b[:, :],
                            d_y1[t0:t0 + MH, hc * 128:(hc + 1) * 128],
                            yrb.sem(y1i), transpose=True)
                        dma("sync", y2b[:, :],
                            d_y2[t0:t0 + MH, hc * 128:(hc + 1) * 128],
                            yrb.sem(y2i), transpose=True)
                        y1t, y2t = cur(yrb.sem(y1i)), cur(yrb.sem(y2i))
                    si, sb_, stok2 = ost.get()
                    wait("vector", xo2.sem(xi), xt)
                    wait_tok("vector", stok2)
                    if not c.SPARSE:
                        def f(eng, xb=xb, sb_=sb_, hc=hc):
                            i0 = eng.tensor_add(sb_[:, :], xb[:, :], sb_yacc[hc][:, :])
                            i0.then_inc(sems["o_ev"], 1)
                        add("vector", f)
                        ev = inc("o_ev", 1)
                    else:
                        wait("vector", yrb.sem(y1i), y1t)
                        wait("vector", yrb.sem(y2i), y2t)
                        vg9 = cur("vch")
                        inc("vch", 2)
                        def f(eng, xb=xb, sb_=sb_, hc=hc, y1b=y1b, y2b=y2b, vg9=vg9):
                            eng.tensor_add(sb_[:, :], xb[:, :], sb_yacc[hc][:, :]).then_inc(sems["vch"], 1)
                            eng.wait_ge(sems["vch"], vg9 + 1)
                            eng.tensor_add(sb_[:, :], sb_[:, :], y1b[:, :]).then_inc(sems["vch"], 1)
                            eng.wait_ge(sems["vch"], vg9 + 2)
                            i0 = eng.tensor_add(sb_[:, :], sb_[:, :], y2b[:, :])
                            i0.then_inc(sems["o_ev"], 1)
                        add("vector", f)
                        ev = inc("o_ev", 1)
                        yrb.set_tok(y1i, ("o_ev", ev))
                        yrb.set_tok(y2i, ("o_ev", ev))
                    xo2.set_tok(xi, ("o_ev", ev))
                    wait("sync", "o_ev", ev)
                    sp = dma("sync", d_out[hc * 128:(hc + 1) * 128, t0:t0 + MH], sb_[:, :],
                             f"ost{si}")
                    ost.set_tok(si, (f"ost{si}", sp))
            wait("sync", "ost0", cur("ost0"))
            wait("sync", "ost1", cur("ost1"))
        ph_b.close()

        with nc.Block() as block:
            def run(fns):
                def g(eng, fns=fns):
                    for f in fns:
                        f(eng)
                return g
            block.sync(run(sched["sync"]))
            block.tensor(run(sched["tensor"]))
            block.vector(run(sched["vector"]))
            block.scalar(run(sched["scalar"]))
            block.gpsimd(run(sched["gpsimd"]))

    return nc



def host_layout(c: CFG, x, Wqkv, Wproj, gw, wg, wu, wd, swg, swu, swd):
    f32 = np.float32
    D, SEQ, HC = c.D, c.SEQ, c.HC
    nheads = 2 * HC
    pi = np.concatenate([np.arange(0, 128, 2), np.arange(1, 128, 2)])

    Wqkv = np.asarray(Wqkv, dtype=f32)
    Wq = Wqkv[0 * D:1 * D].reshape(nheads, 128, D)[:, pi, :]
    Wk = Wqkv[1 * D:2 * D].reshape(nheads, 128, D)[:, pi, :]
    Wv = Wqkv[2 * D:3 * D].reshape(nheads, 128, D)

    pos = np.arange(SEQ, dtype=f32)
    invf = (1.0 / (10000.0 ** (np.arange(0, 128, 2, dtype=f32) / f32(128.0)))).astype(f32)
    ang = invf[:, None] * pos[None, :]
    C64 = np.cos(ang).astype(f32)
    S64 = np.sin(ang).astype(f32)
    ropeC = np.ascontiguousarray(np.concatenate([C64, C64], axis=0))
    ropeS = np.ascontiguousarray(np.concatenate([-S64, S64], axis=0))

    kk = np.arange(128)[:, None]
    qq = np.arange(c.QT)[None, :]
    mask = np.ascontiguousarray(np.concatenate(
        [np.where(kk > qq - 128 * v, f32(NEG_BIG), f32(0.0)) for v in range(c.VR)],
        axis=1).astype(f32))

    common = dict(
        wprojT=np.ascontiguousarray(np.asarray(Wproj, dtype=f32).T),
        ropeC=ropeC, ropeS=ropeS, maskdiag=mask,
        gwT=np.ascontiguousarray(np.asarray(gw, dtype=f32).T),
        wgT=np.ascontiguousarray(np.asarray(wg, dtype=f32).transpose(0, 2, 1)),
        wuT=np.ascontiguousarray(np.asarray(wu, dtype=f32).transpose(0, 2, 1)),
        wdT=np.ascontiguousarray(np.asarray(wd, dtype=f32).transpose(0, 2, 1)),
        swgT=np.ascontiguousarray(np.asarray(swg, dtype=f32).T),
        swuT=np.ascontiguousarray(np.asarray(swu, dtype=f32).T),
        swdT=np.ascontiguousarray(np.asarray(swd, dtype=f32).T),
        onesbf=np.ones((128, 128), dtype=ml_dtypes.bfloat16),
        onesf=np.ones((1, 128), dtype=f32),
        idf=np.ascontiguousarray(np.eye(128, dtype=f32)),
        idbf=np.ascontiguousarray(np.eye(128).astype(ml_dtypes.bfloat16)),
        ltriT=np.ascontiguousarray(
            np.triu(np.ones((128, 128)), k=1).astype(ml_dtypes.bfloat16)),
        iotaT=np.ascontiguousarray(
            (np.arange(c.NTILE_R, dtype=f32) * 128.0).reshape(1, -1)),
        iotaP=np.ascontiguousarray(np.arange(128, dtype=f32).reshape(128, 1)),
        constf=np.ascontiguousarray(np.tile(np.array(
            [[0.0, 1.0, 1e-5, 1.0 / 128.0, 1.0 / math.sqrt(128.0), 0.0, 0.0,
              0.0]], dtype=f32), (128, 1))),
    )

    in_maps = []
    for core in range(c.n_cores):
        b, hh = core // 2, core % 2
        heads = range(hh * HC, (hh + 1) * HC)
        wqkvT = np.concatenate(
            [np.concatenate([Wq[h] for h in heads], axis=0),
             np.concatenate([Wk[h] for h in heads], axis=0),
             np.concatenate([Wv[h] for h in heads], axis=0)], axis=0).T
        xb = np.asarray(x[b], dtype=f32).T
        m = dict(common)
        m["wqkvT"] = np.ascontiguousarray(wqkvT)
        m["xT"] = np.ascontiguousarray(xb)
        m["xownT"] = np.ascontiguousarray(xb[:, hh * c.OWN:(hh + 1) * c.OWN])
        if c.HC >= 8:
            hgroups = [list(range(0, c.HC - 4)),
                       [c.HC - 4, c.HC - 3], [c.HC - 2], [c.HC - 1]]
        elif c.HC >= 2:
            hgroups = [list(range(0, c.HC // 2)), list(range(c.HC // 2, c.HC))]
        else:
            hgroups = [list(range(c.HC))]
        yidx = np.zeros((128, c.DC), dtype=np.int32)
        for i in range(c.DC):
            p_i, lh = i // c.HC, i % c.HC
            g = next(gg for gg, hgg in enumerate(hgroups) if lh in hgg)
            jj = hgroups[g].index(lh)
            s = 2 * (core // 2) + p_i
            base = (s * len(hgroups[g]) + jj) * 128
            yidx[:, i] = base + np.arange(128)
        m["yidx"] = np.ascontiguousarray(yidx)
        in_maps.append(m)
    return in_maps


def gather_out(c: CFG, results, B):
    out = np.empty((B, c.SEQ, c.D), dtype=np.float32)
    for core in range(c.n_cores):
        b, hh = core // 2, core % 2
        out[b, hh * c.OWN:(hh + 1) * c.OWN, :] = np.asarray(results[core]["outT"]).T
    return out


def kernel(L0_attn_x, L0_attn_W_qkv, L0_attn_W_proj, L0_ffn_gate_weight,
           L0_ffn_w_gate, L0_ffn_w_up, L0_ffn_w_down, L0_ffn_sw_gate,
           L0_ffn_sw_up, L0_ffn_sw_down):
    c = CFG()
    x = np.asarray(L0_attn_x, dtype=np.float32)
    nc = build(c)
    in_maps = host_layout(c, x, L0_attn_W_qkv, L0_attn_W_proj, L0_ffn_gate_weight,
                          L0_ffn_w_gate, L0_ffn_w_up, L0_ffn_w_down,
                          L0_ffn_sw_gate, L0_ffn_sw_up, L0_ffn_sw_down)
    res = run_bass_kernel_spmd(nc, in_maps, core_ids=list(range(c.n_cores)))
    return gather_out(c, res.results, x.shape[0])


# version 21
# speedup vs baseline: 1.0853x; 1.0309x over previous
import math
from contextlib import ExitStack

import numpy as np
import ml_dtypes

import concourse.bass as bass
import concourse.mybir as mybir
from concourse.bass_utils import run_bass_kernel_spmd
from concourse.aot_env import allow_ici_all_to_all

F32 = mybir.dt.float32
BF16 = mybir.dt.bfloat16
AF = mybir.ActivationFunctionType
ALU = mybir.AluOpType
AX = mybir.AxisListType

ENGINES = ("sync", "tensor", "vector", "scalar", "gpsimd")


def _free_val_regs(nc, eng, n0):
    import bass_rust as _br
    freed = set()
    for nm in nc.inst_names()[n0:]:
        ins = nc.get_inst_by_name(nm)
        for arg in list(ins.ins) + list(ins.outs):
            rr = getattr(arg, "regref", None)
            if rr and "_val_" in rr and rr not in freed:
                nc.free_register(_br.RegisterHandle(name=rr, engine=ins.engine))
                freed.add(rr)
    eng.get_value_cache().barrier()
NEG_BIG = -1.0e30


class CFG:
    def __init__(self, D=2048, SEQ=4096, HC=8, F=1408, FS=2816, n_cores=8):
        self.D = D
        self.SEQ = SEQ
        self.HC = HC
        self.F = F
        self.FS = FS
        self.E = 8
        self.n_cores = n_cores
        self.OWN = SEQ // 2
        self.DC = D // 128
        self.QKW = HC * 128
        self.TT = min(256, self.SEQ)
        self.QT = min(512, self.SEQ)
        self.NKC = SEQ // 128
        self.NQT = SEQ // self.QT
        self.VR = self.QT // 128
        self.FC = F // 128
        self.FSC = FS // 128
        self.MT = min(512, self.OWN)
        self.NMT = self.OWN // self.MT
        self.TH = 2 if self.OWN > 1024 else 1
        self.MH = self.OWN // self.TH
        self.MQ = min(512, self.MH)
        self.NMQ = self.MH // self.MQ
        self.QKT = min(512, SEQ)
        self.NQK = SEQ // self.QKT
        self.NT2 = SEQ // self.QT
        self.SPARSE = True
        self.NTILE_R = self.OWN // 128 + 8
        self.NSLOT_R = self.NTILE_R * 128
        assert D % 128 == 0 and SEQ % self.QT == 0 and self.OWN % 128 == 0
        assert D == 2 * HC * 128, "heads*head_dim must equal d_model"
        assert F % 128 == 0 and FS % 128 == 0
        assert self.OWN % self.TT == 0 and SEQ % self.TT == 0
        assert self.MH % 128 == 0


def _fc_groups(n, gmax=6):
    out, i = [], 0
    while i < n:
        g = min(gmax, n - i)
        out.append(list(range(i, i + g)))
        i += g
    return out


class Ring:

    def __init__(self, slots):
        self.s = list(slots)
        self.tok = [None] * len(self.s)
        self.k = 0

    def get(self):
        i = self.k % len(self.s)
        self.k += 1
        return i, self.s[i], self.tok[i]

    def set_tok(self, i, tok):
        self.tok[i] = tok


def build(cfg: CFG):
    nc = bass.Bass("TRN2", target_bir_lowering=False)
    c = cfg
    DC, HC, SEQ, OWN, TT, QT = c.DC, c.HC, c.SEQ, c.OWN, c.TT, c.QT
    INV_SQRT_HD = 1.0 / math.sqrt(128.0)

    d_xT = nc.dram_tensor("xT", [c.D, SEQ], F32, kind="ExternalInput")
    d_xownT = nc.dram_tensor("xownT", [c.D, OWN], F32, kind="ExternalInput")
    d_wqkvT = nc.dram_tensor("wqkvT", [c.D, 3 * c.QKW], F32, kind="ExternalInput")
    d_wprojT = nc.dram_tensor("wprojT", [c.D, c.D], F32, kind="ExternalInput")
    d_ropeC = nc.dram_tensor("ropeC", [128, SEQ], F32, kind="ExternalInput")
    d_ropeS = nc.dram_tensor("ropeS", [128, SEQ], F32, kind="ExternalInput")
    d_mask = nc.dram_tensor("maskdiag", [128, c.VR * QT], F32, kind="ExternalInput")
    d_gwT = nc.dram_tensor("gwT", [c.D, c.E], F32, kind="ExternalInput")
    d_wgT = nc.dram_tensor("wgT", [c.E, c.D, c.F], F32, kind="ExternalInput")
    d_wuT = nc.dram_tensor("wuT", [c.E, c.D, c.F], F32, kind="ExternalInput")
    d_wdT = nc.dram_tensor("wdT", [c.E, c.F, c.D], F32, kind="ExternalInput")
    d_swgT = nc.dram_tensor("swgT", [c.D, c.FS], F32, kind="ExternalInput")
    d_swuT = nc.dram_tensor("swuT", [c.D, c.FS], F32, kind="ExternalInput")
    d_swdT = nc.dram_tensor("swdT", [c.FS, c.D], F32, kind="ExternalInput")
    d_onesbf = nc.dram_tensor("onesbf", [128, 128], BF16, kind="ExternalInput")
    d_onesf = nc.dram_tensor("onesf", [1, 128], F32, kind="ExternalInput")
    d_constf = nc.dram_tensor("constf", [128, 8], F32, kind="ExternalInput")
    d_idf = nc.dram_tensor("idf", [128, 128], F32, kind="ExternalInput")
    d_out = nc.dram_tensor("outT", [c.D, OWN], F32, kind="ExternalOutput")

    d_yidx = nc.dram_tensor("yidx", [128, DC], mybir.dt.int32, kind="ExternalInput")
    d_ltriT = nc.dram_tensor("ltriT", [128, 128], BF16, kind="ExternalInput")
    d_iotaT = nc.dram_tensor("iotaT", [1, c.NTILE_R], F32, kind="ExternalInput")
    d_iotaP = nc.dram_tensor("iotaP", [128, 1], F32, kind="ExternalInput")
    d_idbf = nc.dram_tensor("idbf", [128, 128], BF16, kind="ExternalInput")
    d_wgbf = nc.dram_tensor("wgbf", [c.E * 128, (c.D // 128) * c.F], BF16, kind="Internal")
    d_wubf = nc.dram_tensor("wubf", [c.E * 128, (c.D // 128) * c.F], BF16, kind="Internal")
    d_wdbf = nc.dram_tensor("wdbf", [c.E * 128, (c.F // 128) * c.D], BF16, kind="Internal")
    d_h2fm = nc.dram_tensor("h2fm", [DC, 128, OWN], BF16, kind="Internal")
    d_h2tm = nc.dram_tensor("h2tm", [OWN, c.D], BF16, kind="Internal")
    d_idxf = [nc.dram_tensor(f"idxf{r}", [c.NSLOT_R, 1], F32, kind="Internal")
              for r in range(2)]
    d_cwsf = [nc.dram_tensor(f"cwsf{r}", [c.NSLOT_R, 1], F32, kind="Internal")
              for r in range(2)]
    d_y1 = nc.dram_tensor("yr1", [OWN, c.D], BF16, kind="Internal")
    d_y2 = nc.dram_tensor("yr2", [OWN, c.D], BF16, kind="Internal")
    d_qkv = nc.dram_tensor("qkv_spill", [3 * HC, 128, SEQ], BF16, kind="Internal")
    d_x2 = nc.dram_tensor("x2_spill", [DC, 128, OWN], F32, kind="Internal")

    sched = {e: [] for e in ENGINES}
    ctr = {}
    sems = {}

    def add(e, f):
        sched[e].append(f)

    def inc(name, n=1):
        getsem(name)
        ctr[name] = ctr.get(name, 0) + n
        return ctr[name]

    def cur(name):
        return ctr.get(name, 0)

    def wait(e, sem, tgt):
        if tgt is None or tgt <= 0:
            return
        getsem(sem)
        def f(eng, sem=sem, tgt=tgt):
            eng.wait_ge(sems[sem], tgt)
        add(e, f)

    def wait_tok(e, tok):
        if tok is not None:
            wait(e, tok[0], tok[1])

    def dma(qeng, dst, src, sem, **kw):
        getsem(sem)
        tgt = inc(sem, 16)
        def f(eng, dst=dst, src=src, sem=sem, kw=kw):
            eng.dma_start(out=dst, in_=src, **kw).then_inc(sems[sem], 16)
        add(qeng, f)
        return tgt

    class DmaRing:

        def __init__(self, name, slots, nsem=None):
            self.name = name
            self.s = list(slots)
            self.tok = [None] * len(self.s)
            self.k = 0
            self.nsem = nsem or len(self.s)
            for i in range(self.nsem):
                getsem(f"{name}{i}")

        def get(self):
            i = self.k % len(self.s)
            self.k += 1
            return i, self.s[i], self.tok[i]

        def sem(self, i):
            return f"{self.name}{i % self.nsem}"

        def set_tok(self, i, tok):
            self.tok[i] = tok

    bar_n = [0]
    bar_snap = [{}]

    def barrier(exclude=()):
        getsem("bar")
        bar_n[0] += 1
        tgt = 5 * bar_n[0]
        snap = dict(ctr)
        changed = [(s, v) for s, v in snap.items()
                   if s != "bar" and s not in exclude
                   and v != bar_snap[0].get(s, 0)]
        for s in exclude:
            if s in snap:
                snap[s] = bar_snap[0].get(s, 0)
        bar_snap[0] = snap
        for e in ENGINES:
            for s, v in changed:
                wait(e, s, v)
            def f(eng, tgt=tgt):
                eng.sem_inc(sems["bar"], 1)
                eng.wait_ge(sems["bar"], tgt)
            add(e, f)

    st = ExitStack()

    def getsem(name):
        if name not in sems:
            sems[name] = st.enter_context(nc.semaphore(name))
        return name

    with st, allow_ici_all_to_all():

        sb_ones_bf = st.enter_context(nc.sbuf_tensor("ones_bf", [128, 128], BF16))
        sb_ones_f = st.enter_context(nc.sbuf_tensor("ones_f", [1, 128], F32))
        sb_cst = st.enter_context(nc.sbuf_tensor("cstf", [128, 8], F32))
        dma("sync", sb_ones_bf[:, :], d_onesbf[:, :], "const")
        dma("sync", sb_ones_f[:, :], d_onesf[:, :], "const")
        dma("sync", sb_cst[:, :], d_constf[:, :], "const")
        CONST_VALS = [0.0, 1.0, 1e-5, 1.0 / 128.0, INV_SQRT_HD]
        for ci, val in enumerate(CONST_VALS):
            nc.const_aps.register(F32, val, sb_cst[:, ci:ci + 1])
        const_base = cur("const")
        for e in ("tensor", "vector", "scalar"):
            wait(e, "const", const_base)

        _apply_tmp = [st.enter_context(nc.sbuf_tensor(f"lnt{p}", [128, TT], F32))
                      for p in range(2)]

        pump_pairs = []
        if c.SPARSE:
            for e in range(c.E):
                for dc2 in range(DC):
                    pump_pairs.append((d_wgT.ap()[e][dc2 * 128:(dc2 + 1) * 128, :],
                                       d_wgbf.ap()[e * 128:(e + 1) * 128,
                                                   dc2 * c.F:(dc2 + 1) * c.F], c.F))
                    pump_pairs.append((d_wuT.ap()[e][dc2 * 128:(dc2 + 1) * 128, :],
                                       d_wubf.ap()[e * 128:(e + 1) * 128,
                                                   dc2 * c.F:(dc2 + 1) * c.F], c.F))
                for fc in range(c.FC):
                    pump_pairs.append((d_wdT.ap()[e][fc * 128:(fc + 1) * 128, :],
                                       d_wdbf.ap()[e * 128:(e + 1) * 128,
                                                   fc * c.D:(fc + 1) * c.D], c.D))
            ph_pump = ExitStack()
            pcb = DmaRing("pcb", [ph_pump.enter_context(nc.sbuf_tensor(f"pcb{r}", [128, 2048], BF16))
                                  for r in range(4)], nsem=2)
        pump_i = [0]
        pump_q = []

        def _pump_store(slot, dstap, wdt, tgt):
            tgt = max(tgt, cur(pcb.sem(slot)))
            wait("gpsimd", pcb.sem(slot), tgt)
            sp = dma("gpsimd", dstap, pcb.s[slot][:, 0:wdt], f"ost{slot % 2}")
            pcb.set_tok(slot, (f"ost{slot % 2}", sp))

        def pump(n):
            if not c.SPARSE:
                return
            for _ in range(n):
                if pump_i[0] >= len(pump_pairs):
                    return
                srcap, dstap, wdt = pump_pairs[pump_i[0]]
                pump_i[0] += 1
                i, buf, tok = pcb.get()
                if len(pump_q) >= 3:
                    _pump_store(*pump_q.pop(0))
                wait_tok("gpsimd", tok)
                dma("gpsimd", buf[:, 0:wdt], srcap, pcb.sem(i))
                pump_q.append((i, dstap, wdt, cur(pcb.sem(i))))

        def pump_drain():
            if not c.SPARSE:
                return
            while pump_i[0] < len(pump_pairs):
                pump(1)
            while pump_q:
                _pump_store(*pump_q.pop(0))

        def pump_done_waits(e):
            if c.SPARSE:
                assert pump_i[0] >= len(pump_pairs) and not pump_q, \
                    "precast pump unfinished"
                wait(e, "ost0", cur("ost0"))
                wait(e, "ost1", cur("ost1"))

        def emit_ln(px, NT, src_ap, out_T, ph):
            TTL = TT
            xbuf = [[ph.enter_context(nc.sbuf_tensor(f"{px}xb{p}_{i}", [128, TTL], F32))
                     for i in range(DC)] for p in range(2)]
            xbf = [ph.enter_context(nc.sbuf_tensor(f"{px}xbf{p}", [128, TTL], BF16)) for p in range(2)]
            xsq = [ph.enter_context(nc.sbuf_tensor(f"{px}xsq{p}", [128, TTL], BF16)) for p in range(2)]
            sml = ph.enter_context(nc.sbuf_tensor(f"{px}sml", [1, 12 * TTL], F32))
            ps_sm = [ph.enter_context(nc.psum_tensor(f"{px}psm{p}", [1, TTL], F32)) for p in range(2)]
            ps_sq = [ph.enter_context(nc.psum_tensor(f"{px}psq{p}", [1, TTL], F32)) for p in range(2)]
            ps_bc = [ph.enter_context(nc.psum_tensor(f"{px}bc{p}", [128, 2 * TTL], F32)) for p in range(2)]
            s_x, s_pre, s_stat, s_sml, s_bc, s_apl = ("ln_x", "ln_pre",
                "ln_stat", "ln_sml", "ln_bc", "ln_apl")
            b_stat, b_sml, b_apl = cur(s_stat), cur(s_sml), cur(s_apl)
            invD = 1.0 / c.D
            for t in range(NT):
                p = t % 2
                s_xp = f"{s_x}{p}"
                if t >= 2:
                    wait("sync", s_apl, b_apl + (t - 1) * DC)
                for i in range(DC):
                    dma("sync", xbuf[p][i][:, :], src_ap(i, t), s_xp)
                ld = cur(s_xp)
                wait("scalar", s_xp, ld)
                for i in range(DC):
                    k = t * DC + i
                    if k >= 2:
                        wait("scalar", s_stat, b_stat + k - 1)
                    sc = inc("sch", 1)
                    def f(eng, i=i, p=p, k=k, sc=sc):
                        eng.copy(out=xbf[k % 2][:, :], in_=xbuf[p][i][:, :]).then_inc(sems["sch"], 1)
                        eng.wait_ge(sems["sch"], sc)
                        a = eng.activation(out=xsq[k % 2][:, :], in_=xbuf[p][i][:, :],
                                           func=AF.Square)
                        a.then_inc(sems[s_pre], 1)
                    add("scalar", f)
                    inc(s_pre, 1)
                    wait("tensor", s_pre, cur(s_pre))
                    if i == 0 and t >= 2:
                        wait("tensor", s_sml, b_sml + 3 * (t - 2) + 1)
                    def f(eng, i=i, p=p, k=k):
                        eng.matmul(ps_sm[p][0:1, :], sb_ones_bf[:, 0:1], xbf[k % 2][:, :],
                                   start=(i == 0), stop=(i == DC - 1))
                        mm = eng.matmul(ps_sq[p][0:1, :], sb_ones_bf[:, 0:1], xsq[k % 2][:, :],
                                        start=(i == 0), stop=(i == DC - 1))
                        mm.then_inc(sems[s_stat], 1)
                    add("tensor", f)
                    inc(s_stat, 1)
                o = 6 * p
                S_mean = sml[0:1, (o + 0) * TTL:(o + 1) * TTL]
                S_msq = sml[0:1, (o + 1) * TTL:(o + 2) * TTL]
                S_var = sml[0:1, (o + 2) * TTL:(o + 3) * TTL]
                S_sd = sml[0:1, (o + 3) * TTL:(o + 4) * TTL]
                S_rstd = sml[0:1, (o + 4) * TTL:(o + 5) * TTL]
                S_mrs = sml[0:1, (o + 5) * TTL:(o + 6) * TTL]
                wait("vector", s_stat, b_stat + (t + 1) * DC)
                c1 = inc("vch", 2)
                def f(eng, p=p, S_mean=S_mean, S_msq=S_msq, S_var=S_var, c1=c1):
                    eng.tensor_scalar_mul(S_mean, ps_sm[p][0:1, :], invD).then_inc(sems["vch"], 1)
                    eng.tensor_scalar_mul(S_var, ps_sq[p][0:1, :], invD).then_inc(sems["vch"], 1)
                    eng.wait_ge(sems["vch"], c1)
                    eng.tensor_mul(S_msq, S_mean, S_mean).then_inc(sems["vch"], 1)
                    eng.wait_ge(sems["vch"], c1 + 1)
                    i0 = eng.tensor_sub(S_var, S_var, S_msq)
                    i0.then_inc(sems[s_sml], 1)
                add("vector", f)
                inc("vch", 1)
                v1 = inc(s_sml, 1)
                wait("scalar", s_sml, v1)
                def f(eng, S_sd=S_sd, S_var=S_var):
                    a = eng.activation(out=S_sd, in_=S_var, func=AF.Sqrt, bias=1e-5)
                    a.then_inc(sems[s_sml], 1)
                add("scalar", f)
                v2 = inc(s_sml, 1)
                wait("vector", s_sml, v2)
                c2 = inc("vch", 2)
                def f(eng, S_sd=S_sd, S_rstd=S_rstd, S_mrs=S_mrs, S_mean=S_mean, c2=c2):
                    eng.reciprocal(S_rstd, S_sd).then_inc(sems["vch"], 1)
                    eng.wait_ge(sems["vch"], c2 - 1)
                    eng.tensor_mul(S_mrs, S_mean, S_rstd).then_inc(sems["vch"], 1)
                    eng.wait_ge(sems["vch"], c2)
                    i0 = eng.tensor_scalar_mul(S_mrs, S_mrs, -1.0)
                    i0.then_inc(sems[s_sml], 1)
                add("vector", f)
                v3 = inc(s_sml, 1)
                wait("tensor", s_sml, v3)
                if t >= 2:
                    wait("tensor", s_apl, b_apl + (t - 1) * DC)
                def f(eng, p=p, S_rstd=S_rstd, S_mrs=S_mrs):
                    eng.matmul(ps_bc[p][:, 0:TTL], sb_ones_f[0:1, :], S_rstd,
                               start=True, stop=True)
                    mm = eng.matmul(ps_bc[p][:, TTL:2 * TTL], sb_ones_f[0:1, :], S_mrs,
                                    start=True, stop=True)
                    mm.then_inc(sems[s_bc], 1)
                add("tensor", f)
                btgt = inc(s_bc, 1)
                wait("vector", s_bc, btgt)
                for i in range(DC):
                    apl_pre = cur(s_apl)
                    mc = inc("vch", 1)
                    def f(eng, i=i, p=p, t=t, apl_pre=apl_pre, mc=mc):
                        if apl_pre:
                            eng.wait_ge(sems[s_apl], apl_pre)
                        eng.tensor_mul(_apply_tmp[p][:, :], xbuf[p][i][:, :],
                                       ps_bc[p][:, 0:TTL]).then_inc(sems["vch"], 1)
                        eng.wait_ge(sems["vch"], mc)
                        i1 = eng.tensor_add(out_T[i][:, t * TTL:(t + 1) * TTL],
                                            _apply_tmp[p][:, :], ps_bc[p][:, TTL:2 * TTL])
                        i1.then_inc(sems[s_apl], 1)
                    add("vector", f)
                    inc(s_apl, 1)

        ph_a = ExitStack()
        sb_xnT = [ph_a.enter_context(nc.sbuf_tensor(f"xnT{i}", [128, SEQ], BF16))
                  for i in range(DC)]

        with ExitStack() as ph:
            emit_ln("ln1", SEQ // TT,
                    lambda i, t: d_xT[i * 128:(i + 1) * 128, t * TT:(t + 1) * TT],
                    sb_xnT, ph)
            barrier()

        with ExitStack() as ph:
            sb_ropeC = ph.enter_context(nc.sbuf_tensor("ropeCs", [128, SEQ], F32))
            sb_ropeS = ph.enter_context(nc.sbuf_tensor("ropeSs", [128, SEQ], F32))
            dma("sync", sb_ropeC[:, :], d_ropeC[:, :], "ropeld")
            dma("sync", sb_ropeS[:, :], d_ropeS[:, :], "ropeld")
            rope_tgt = cur("ropeld")
            wsb = DmaRing("wsb", [ph.enter_context(nc.sbuf_tensor(f"wqkv{r}", [128, DC * 128], BF16))
                                  for r in range(4)])
            stg = Ring([ph.enter_context(nc.sbuf_tensor(f"qstg{r}", [128, QT], BF16))
                        for r in range(4)])
            rt1 = [ph.enter_context(nc.sbuf_tensor(f"ropet1{p}", [128, QT], F32)) for p in range(2)]
            rt2 = [ph.enter_context(nc.sbuf_tensor(f"ropet2{p}", [128, QT], F32)) for p in range(2)]
            ps_qk = Ring([ph.enter_context(nc.psum_tensor(f"psqk{r}", [128, c.QKT], F32))
                          for r in range(2)])
            wq3 = d_wqkvT.ap().rearrange("(i p) f -> p i f", p=128)
            wait("vector", "ropeld", rope_tgt)

            NQK = c.NQK
            HW = c.QKT // QT
            order = []
            for h in range(HC):
                order += [(0, h), (1, h), (2, h)]
            for sidx, (kind, h) in enumerate(order):
                col0 = kind * c.QKW + h * 128
                wi, wbuf, wtok = wsb.get()
                wait_tok("gpsimd", wtok)
                dma("gpsimd", wbuf[:, :].rearrange("p (i f) -> p i f", f=128),
                    wq3[:, :, col0:col0 + 128], wsb.sem(wi))
                wld = cur(wsb.sem(wi))
                wsb.set_tok(wi, ("qk_mm", (sidx + 1) * NQK))
                for tt in range(NQK):
                    pi, psb, ptok = ps_qk.get()
                    if tt == 0:
                        wait("tensor", wsb.sem(wi), wld)
                    wait_tok("tensor", ptok)
                    def f(eng, wbuf=wbuf, psb=psb, tt=tt):
                        for i in range(DC):
                            mm = eng.matmul(psb[:, :], wbuf[:, i * 128:(i + 1) * 128],
                                            sb_xnT[i][:, tt * c.QKT:(tt + 1) * c.QKT],
                                            start=(i == 0), stop=(i == DC - 1))
                        mm.then_inc(sems["qk_mm"], 1)
                    add("tensor", f)
                    mmt = inc("qk_mm", 1)
                    ptok_new = None
                    for hw in range(HW):
                        gcol = tt * c.QKT + hw * QT
                        gi = (sidx * NQK + tt) * HW + hw
                        si, sbuf_, stok = stg.get()
                        if kind in (0, 1):
                            if hw == 0:
                                wait("vector", "qk_mm", mmt)
                            wait_tok("vector", stok)
                            p2 = gi % 2
                            wait("vector", "qk_stage", cur("qk_stage"))
                            vc = inc("vch", 2)
                            cur_evac = inc("qk_evac", 1)
                            def f(eng, psb=psb, p2=p2, hw=hw, gcol=gcol, sbuf_=sbuf_,
                                  vc=vc, cur_evac=cur_evac):
                                hs = hw * QT
                                eng.tensor_mul(rt1[p2][:, :], psb[:, hs:hs + QT],
                                               sb_ropeC[:, gcol:gcol + QT]).then_inc(sems["vch"], 1)
                                eng.tensor_mul(rt2[p2][0:64, :], psb[64:128, hs:hs + QT],
                                               sb_ropeS[0:64, gcol:gcol + QT]).then_inc(sems["vch"], 1)
                                eng.wait_ge(sems["vch"], vc)
                                i2 = eng.tensor_mul(rt2[p2][64:128, :], psb[0:64, hs:hs + QT],
                                                    sb_ropeS[64:128, gcol:gcol + QT])
                                i2.then_inc(sems["qk_evac"], 1)
                                eng.wait_ge(sems["qk_evac"], cur_evac)
                                i3 = eng.tensor_add(sbuf_[:, :], rt1[p2][:, :], rt2[p2][:, :])
                                i3.then_inc(sems["qk_stage"], 1)
                            add("vector", f)
                            ptok_new = ("qk_evac", cur_evac)
                            stok_sem, stgt = "qk_stage", inc("qk_stage", 1)
                        else:
                            if hw == 0:
                                wait("scalar", "qk_mm", mmt)
                            wait_tok("scalar", stok)
                            def f(eng, psb=psb, sbuf_=sbuf_, hw=hw):
                                a = eng.copy(out=sbuf_[:, :], in_=psb[:, hw * QT:(hw + 1) * QT])
                                a.then_inc(sems["qk_evacS"], 1)
                            add("scalar", f)
                            ptok_new = ("qk_evacS", inc("qk_evacS", 1))
                            stok_sem, stgt = "qk_evacS", cur("qk_evacS")
                        wait("sync", stok_sem, stgt)
                        sp = dma("sync", d_qkv[sidx, :, gcol:gcol + QT], sbuf_[:, :],
                                 f"qsp{si}")
                        stg.set_tok(si, (f"qsp{si}", sp))
                    ps_qk.set_tok(pi, ptok_new)
                pump(5)
            barrier()
        ph_a.close()

        ph_y = ExitStack()
        sb_yT = [ph_y.enter_context(nc.sbuf_tensor(f"yT{i}", [128, SEQ], BF16))
                 for i in range(HC)]
        sb_yidx = ph_y.enter_context(nc.sbuf_tensor("yidxsb", [128, DC], mybir.dt.int32))

        if HC >= 8:
            hgroups = [list(range(0, HC - 4)), [HC - 4, HC - 3], [HC - 2], [HC - 1]]
        elif HC >= 2:
            hgroups = [list(range(0, HC // 2)), list(range(HC // 2, HC))]
        else:
            hgroups = [list(range(HC))]
        d_cins, d_couts = [], []
        for g, hg in enumerate(hgroups):
            sz = c.n_cores * len(hg) * 128
            d_cins.append(nc.dram_tensor(f"cc_in{g}", [sz, OWN], BF16, kind="Internal"))
            d_couts.append(nc.dram_tensor(f"cc_out{g}", [sz, OWN], BF16, kind="Internal"))

        def yf_ap(i):
            p = i // HC
            return sb_yT[i % HC][:, p * OWN:(p + 1) * OWN]

        gat_tot = []

        def emit_exchange_cc(g):
            hg = hgroups[g]
            nhg = len(hg)
            wait("gpsimd", "a_y", (hg[-1] + 1) * c.NQT)
            for d in range(c.n_cores):
                rho = d % 2
                for jj, i in enumerate(hg):
                    dma("gpsimd",
                        d_cins[g][(d * nhg + jj) * 128:(d * nhg + jj + 1) * 128, :],
                        sb_yT[i][:, rho * OWN:(rho + 1) * OWN], "cc_wr")
            wait("gpsimd", "cc_wr", cur("cc_wr"))
            def f(eng, g=g):
                cc = eng.collective_compute(
                    "AllToAll", ALU.bypass,
                    replica_groups=[list(range(c.n_cores))],
                    ins=[d_cins[g].ap().opt()], outs=[d_couts[g].ap().opt()])
                cc.then_inc(sems["cc_done"])
            add("gpsimd", f)
            inc("cc_done", 1)

        def emit_gather(g):
            hg = hgroups[g]
            sem = f"ygat{g % 2}"
            getsem(sem)
            wait("gpsimd", "cc_done", g + 1)
            wait("gpsimd", "yixld", cur("yixld"))
            chunks = [p * HC + lh for p in range(2) for lh in hg]
            for i in chunks:
                dst = yf_ap(i)
                inc(sem, 16)
                def f(eng, dst=dst, i=i, g=g, sem=sem):
                    n0 = nc.n_instructions()
                    dd = eng.indirect_dma_start(
                        out=dst, out_offset=None,
                        in_=d_couts[g].ap(),
                        in_offset=bass.IndirectOffsetOnAxis(
                            ap=sb_yidx[:, i:i + 1], axis=0))
                    dd.then_inc(sems[sem], 16)
                    _free_val_regs(nc, eng, n0)
                add("gpsimd", f)
            gat_tot.append((sem, cur(sem)))

        with ExitStack() as ph:
            sb_mask = ph.enter_context(nc.sbuf_tensor("maskd", [128, c.VR * QT], F32))
            dma("sync", sb_mask[:, :], d_mask[:, :], "mskld")
            dma("sync", sb_yidx[:, :], d_yidx[:, :], "yixld")
            wait("vector", "mskld", cur("mskld"))
            for r in range(4):
                wait("sync", f"qsp{r}", cur(f"qsp{r}"))

            hb = DmaRing("ahl", [
                {"q": ph.enter_context(nc.sbuf_tensor(f"qh{r}", [128, SEQ], BF16)),
                 "k": ph.enter_context(nc.sbuf_tensor(f"kh{r}", [128, SEQ], BF16)),
                 "v": ph.enter_context(nc.sbuf_tensor(f"vh{r}", [128, SEQ], BF16))}
                for r in range(2)])
            nsq = Ring([ph.enter_context(nc.sbuf_tensor(f"nsq{r}", [128, QT], BF16))
                        for r in range(2)])
            nrow = ph.enter_context(nc.sbuf_tensor("nrow", [1, 2 * c.NT2], F32))
            nsml = ph.enter_context(nc.sbuf_tensor("nsml", [1, 8], F32))
            biasA = [ph.enter_context(nc.sbuf_tensor(f"biasA{r}", [128, 1], F32))
                     for r in range(2)]
            ptb = Ring([ph.enter_context(nc.sbuf_tensor(f"pt{r}", [128, QT], BF16))
                        for r in range(6)])
            smb = Ring([ph.enter_context(nc.sbuf_tensor(f"smb{r}", [128, QT], F32))
                        for r in range(5)])
            rbs = ph.enter_context(nc.sbuf_tensor("rbs", [128, QT], F32))
            rcp = ph.enter_context(nc.sbuf_tensor("rcp", [1, QT], F32))
            ps_s = Ring([ph.enter_context(nc.psum_tensor(f"pss{r}", [128, QT], F32))
                         for r in range(4)])
            ps_o = Ring([ph.enter_context(nc.psum_tensor(f"pso{r}", [128, QT], F32))
                         for r in range(2)])
            ps_sum = ph.enter_context(nc.psum_tensor("pssum", [1, QT], F32))
            ps_rb = ph.enter_context(nc.psum_tensor("psrb", [128, QT], F32))

            ps_sum_tok = None
            ps_rb_tok = None
            rbs_tok = None
            rcp_tok = None
            exps_at_head_end = []

            for h in range(HC):
                bi, bufs, btok = hb.get()
                hsem = hb.sem(bi)
                wait_tok("sync", btok)
                dma("sync", bufs["q"][:, :], d_qkv[3 * h + 0, :, :], hsem)
                dma("sync", bufs["k"][:, :], d_qkv[3 * h + 1, :, :], hsem)
                for kc in range(c.NKC):
                    dma("sync", bufs["v"][:, kc * 128:(kc + 1) * 128],
                        d_qkv[3 * h + 2, :, kc * 128:(kc + 1) * 128], hsem,
                        transpose=True)
                ldt = cur(hsem)

                wait("scalar", hsem, ldt)
                wait("vector", "a_nrd", cur("a_nrd"))
                for pidx, part in enumerate(("q", "k")):
                    for j in range(c.NT2):
                        g = pidx * c.NT2 + j
                        qi, qb, qtok = nsq.get()
                        wait_tok("scalar", qtok)
                        def f(eng, qb=qb, part=part, j=j, bufs=bufs):
                            a = eng.activation(out=qb[:, :],
                                               in_=bufs[part][:, j * QT:(j + 1) * QT],
                                               func=AF.Square)
                            a.then_inc(sems["a_nrm"], 1)
                        add("scalar", f)
                        nt = inc("a_nrm", 1)
                        nsq.set_tok(qi, ("a_nmm", cur("a_nmm") + 1))
                        if g % 2 == 0:
                            nb, ntok = ps_sum, ps_sum_tok
                        else:
                            nb, ntok = ps_rb, ps_rb_tok
                        wait("tensor", "a_nrm", nt)
                        wait_tok("tensor", ntok)
                        def f(eng, nb=nb, qb=qb):
                            mm = eng.matmul(nb[0:1, :], sb_ones_bf[:, 0:1], qb[:, :],
                                            start=True, stop=True)
                            mm.then_inc(sems["a_nmm"], 1)
                        add("tensor", f)
                        nmm = inc("a_nmm", 1)
                        wait("vector", "a_nmm", nmm)
                        def f(eng, nb=nb, g=g):
                            r = eng.reduce_max(nrow[0:1, g:g + 1], nb[0:1, :], axis=AX.X)
                            r.then_inc(sems["a_nrd"], 1)
                        add("vector", f)
                        ntok2 = ("a_nrd", inc("a_nrd", 1))
                        if g % 2 == 0:
                            ps_sum_tok = ntok2
                        else:
                            ps_rb_tok = ntok2
                wait("vector", "a_nrd", cur("a_nrd"))
                wait("vector", "a_nrm", cur("a_nrm"))
                vc = inc("vch", 2)
                def f(eng, vc=vc):
                    eng.reduce_max(nsml[0:1, 0:1], nrow[0:1, 0:c.NT2], axis=AX.X).then_inc(sems["vch"], 1)
                    eng.reduce_max(nsml[0:1, 1:2], nrow[0:1, c.NT2:2 * c.NT2], axis=AX.X).then_inc(sems["vch"], 1)
                    eng.wait_ge(sems["vch"], vc)
                    i0 = eng.tensor_mul(nsml[0:1, 2:3], nsml[0:1, 0:1], nsml[0:1, 1:2])
                    i0.then_inc(sems["a_nrd"], 1)
                add("vector", f)
                nprod = inc("a_nrd", 1)
                wait("scalar", "a_nrd", nprod)
                def f(eng):
                    a = eng.activation(out=nsml[0:1, 3:4], in_=nsml[0:1, 2:3],
                                       func=AF.Sqrt, scale=1.0 / 128.0)
                    a.then_inc(sems["a_nrm"], 1)
                add("scalar", f)
                nb_t = inc("a_nrm", 1)
                wait("vector", "a_nrm", nb_t)
                wait("vector", "a_nmm", cur("a_nmm"))
                def f(eng):
                    i0 = eng.tensor_scalar_mul(nsml[0:1, 4:5], nsml[0:1, 3:4], -1.0)
                    i0.then_inc(sems["a_nrd"], 1)
                add("vector", f)
                negb = inc("a_nrd", 1)
                wait("tensor", "a_nrd", negb)
                wait_tok("tensor", ps_rb_tok)
                def f(eng):
                    mm = eng.matmul(ps_rb[:, 0:1], sb_ones_f[0:1, :], nsml[0:1, 4:5],
                                    start=True, stop=True)
                    mm.then_inc(sems["a_nmm"], 1)
                add("tensor", f)
                bmm = inc("a_nmm", 1)
                wait("vector", "a_nmm", bmm)
                if h >= 2:
                    wait("vector", "a_exp", exps_at_head_end[h - 2])
                bA = biasA[h % 2]
                def f(eng, bA=bA):
                    i0 = eng.tensor_copy(bA[:, :], ps_rb[:, 0:1])
                    i0.then_inc(sems["a_bias"], 1)
                add("vector", f)
                ps_rb_tok = ("a_bias", inc("a_bias", 1))
                wait("scalar", "a_bias", cur("a_bias"))

                for qt in range(c.NQT):
                    kcs = list(range(0, (qt + 1) * c.VR))
                    nkc = len(kcs)
                    SKEW = min(5, nkc)
                    kstate = {}
                    octx = {}

                    def emitA(ki, qt=qt, kcs=kcs, bufs=bufs, kstate=kstate):
                        kc = kcs[ki]
                        diag_v = kc - qt * c.VR
                        si, ssb, stok2 = ps_s.get()
                        wait_tok("tensor", stok2)
                        def f(eng, ssb=ssb, kc=kc, qt=qt, bufs=bufs):
                            mm = eng.matmul(ssb[:, :], bufs["k"][:, kc * 128:(kc + 1) * 128],
                                            bufs["q"][:, qt * QT:(qt + 1) * QT],
                                            start=True, stop=True)
                            mm.then_inc(sems["a_smm"], 1)
                        add("tensor", f)
                        smm = inc("a_smm", 1)
                        pti, ptbuf, pttok = ptb.get()
                        bA = biasA[h % 2]
                        if diag_v >= 0:
                            mi, msb, mtok = smb.get()
                            wait("vector", "a_smm", smm)
                            wait_tok("vector", mtok)
                            def f(eng, msb=msb, ssb=ssb, diag_v=diag_v):
                                i0 = eng.tensor_add(msb[:, :], ssb[:, :],
                                                    sb_mask[:, diag_v * QT:(diag_v + 1) * QT])
                                i0.then_inc(sems["a_msk"], 1)
                            add("vector", f)
                            mk = inc("a_msk", 1)
                            ps_s.set_tok(si, ("a_msk", mk))
                            wait("scalar", "a_msk", mk)
                            wait_tok("scalar", pttok)
                            def f(eng, msb=msb, ptbuf=ptbuf, bA=bA):
                                a = eng.activation(out=ptbuf[:, :], in_=msb[:, :],
                                                   func=AF.Exp, bias=bA[:, 0:1],
                                                   scale=INV_SQRT_HD)
                                a.then_inc(sems["a_exp"], 1)
                            add("scalar", f)
                            ex = inc("a_exp", 1)
                            smb.set_tok(mi, ("a_exp", ex))
                        else:
                            wait("scalar", "a_smm", smm)
                            wait_tok("scalar", pttok)
                            def f(eng, ssb=ssb, ptbuf=ptbuf, bA=bA):
                                a = eng.activation(out=ptbuf[:, :], in_=ssb[:, :],
                                                   func=AF.Exp, bias=bA[:, 0:1],
                                                   scale=INV_SQRT_HD)
                                a.then_inc(sems["a_exp"], 1)
                            add("scalar", f)
                            ex = inc("a_exp", 1)
                            ps_s.set_tok(si, ("a_exp", ex))
                        kstate[ki] = (ex, pti, ptbuf)

                    def emitB(ki, qt=qt, kcs=kcs, bufs=bufs, kstate=kstate, octx=octx):
                        nonlocal ps_sum_tok
                        kc = kcs[ki]
                        nkc = len(kcs)
                        ex, pti, ptbuf = kstate.pop(ki)
                        if ki == 0:
                            oi, osb, otok = ps_o.get()
                            wait_tok("tensor", otok)
                            wait_tok("tensor", ps_sum_tok)
                            octx["o"] = (oi, osb)
                        oi, osb = octx["o"]
                        wait("tensor", "a_exp", ex)
                        def f(eng, osb=osb, ptbuf=ptbuf, kc=kc, ki=ki, nkc=nkc, bufs=bufs):
                            eng.matmul(osb[:, :], bufs["v"][:, kc * 128:(kc + 1) * 128],
                                       ptbuf[:, :], start=(ki == 0), stop=(ki == nkc - 1))
                            mm = eng.matmul(ps_sum[0:1, :], sb_ones_bf[:, 0:1], ptbuf[:, :],
                                            start=(ki == 0), stop=(ki == nkc - 1))
                            mm.then_inc(sems["a_pv"], 1)
                        add("tensor", f)
                        pv = inc("a_pv", 1)
                        ptb.set_tok(pti, ("a_pv", pv))
                        return pv

                    for ki in range(SKEW):
                        emitA(ki)
                    for ki in range(nkc):
                        if ki + SKEW < nkc:
                            emitA(ki + SKEW)
                        pv = emitB(ki)
                    oi, osb = octx["o"]
                    wait("vector", "a_pv", pv)
                    wait_tok("vector", rcp_tok)
                    def f(eng):
                        i0 = eng.reciprocal(rcp[0:1, :], ps_sum[0:1, :])
                        i0.then_inc(sems["a_rcp"], 1)
                    add("vector", f)
                    ps_sum_tok = ("a_rcp", inc("a_rcp", 1))
                    wait("tensor", "a_rcp", ps_sum_tok[1])
                    wait_tok("tensor", ps_rb_tok)
                    def f(eng):
                        mm = eng.matmul(ps_rb[:, :], sb_ones_f[0:1, :], rcp[0:1, :],
                                        start=True, stop=True)
                        mm.then_inc(sems["a_rbc"], 1)
                    add("tensor", f)
                    rbc = inc("a_rbc", 1)
                    rcp_tok = ("a_rbc", rbc)
                    wait("scalar", "a_rbc", rbc)
                    wait_tok("scalar", rbs_tok)
                    def f(eng):
                        a = eng.copy(out=rbs[:, :], in_=ps_rb[:, :])
                        a.then_inc(sems["a_rbe"], 1)
                    add("scalar", f)
                    rbe = inc("a_rbe", 1)
                    ps_rb_tok = ("a_rbe", rbe)
                    wait("vector", "a_rbe", rbe)
                    def f(eng, h=h, qt=qt, osb=osb):
                        i0 = eng.scalar_tensor_tensor(sb_yT[h][:, qt * QT:(qt + 1) * QT],
                                                      osb[:, :], 1.0, rbs[:, :],
                                                      ALU.mult, ALU.mult)
                        i0.then_inc(sems["a_y"], 1)
                    add("vector", f)
                    ytgt = inc("a_y", 1)
                    ps_o.set_tok(oi, ("a_y", ytgt))
                    rbs_tok = ("a_y", ytgt)
                hb.set_tok(bi, ("a_y", cur("a_y")))
                exps_at_head_end.append(cur("a_exp"))
                for g2 in range(len(hgroups) - 1):
                    if h == hgroups[g2][-1]:
                        emit_exchange_cc(g2)
                        emit_gather(g2)
                if h < HC - 2:
                    pump(20)

            emit_exchange_cc(len(hgroups) - 1)
            barrier(exclude=("cc_done",))

        with ExitStack() as ph:
            emit_gather(len(hgroups) - 1)
            wpj = DmaRing("wpj", [ph.enter_context(nc.sbuf_tensor(f"wpj{r}", [128, DC * 128], BF16))
                                  for r in range(2)])
            xob = DmaRing("pxo", [ph.enter_context(nc.sbuf_tensor(f"xob{r}", [128, OWN], F32))
                                  for r in range(2)])
            stgp = Ring([ph.enter_context(nc.sbuf_tensor(f"pstg{r}", [128, c.MT], F32))
                         for r in range(4)])
            ps_p = Ring([ph.enter_context(nc.psum_tensor(f"psp{r}", [128, c.MT], F32))
                         for r in range(2)])
            wpr = d_wprojT.ap().rearrange("(i p) f -> p i f", p=128)
            csets = []
            for g, hg in enumerate(hgroups):
                csets.append([p * HC + lh for p in range(2) for lh in hg])
            for o in range(DC):
                wi, wbuf, wtok = wpj.get()
                wait_tok("gpsimd", wtok)
                dma("gpsimd", wbuf[:, :].rearrange("p (i f) -> p i f", f=128),
                    wpr[:, :, o * 128:(o + 1) * 128], wpj.sem(wi))
                wld = cur(wpj.sem(wi))
                pump(7)
                wpj.set_tok(wi, ("pj_mm", (o + 1) * c.NMT))
                xi, xbuf2, xtok = xob.get()
                wait_tok("sync", xtok)
                dma("sync", xbuf2[:, :], d_xownT[o * 128:(o + 1) * 128, :], xob.sem(xi))
                xld = cur(xob.sem(xi))
                xob.set_tok(xi, ("pj_ev", (o + 1) * c.NMT))
                for mt in range(c.NMT):
                    pi, psb, ptok = ps_p.get()
                    wait_tok("tensor", ptok)
                    if mt == 0:
                        wait("tensor", wpj.sem(wi), wld)
                    nci = 0
                    for g, iset in enumerate(csets):
                        wait("tensor", gat_tot[g][0], gat_tot[g][1])
                        first, last = nci == 0, (nci + len(iset) == DC)
                        def f(eng, o=o, mt=mt, psb=psb, wbuf=wbuf, iset=list(iset),
                              first=first, last=last):
                            for j, i in enumerate(iset):
                                mm = eng.matmul(psb[:, :], wbuf[:, i * 128:(i + 1) * 128],
                                                yf_ap(i)[:, mt * c.MT:(mt + 1) * c.MT],
                                                start=(first and j == 0),
                                                stop=(last and j == len(iset) - 1))
                            if last:
                                mm.then_inc(sems["pj_mm"], 1)
                        add("tensor", f)
                        nci += len(iset)
                    mmt = inc("pj_mm", 1)
                    si, sstg, stok2 = stgp.get()
                    wait("vector", "pj_mm", mmt)
                    if mt == 0:
                        wait("vector", xob.sem(xi), xld)
                    wait_tok("vector", stok2)
                    def f(eng, psb=psb, sstg=sstg, xbuf2=xbuf2, mt=mt):
                        i0 = eng.tensor_add(sstg[:, :], psb[:, :],
                                            xbuf2[:, mt * c.MT:(mt + 1) * c.MT])
                        i0.then_inc(sems["pj_ev"], 1)
                    add("vector", f)
                    ps_p.set_tok(pi, ("pj_ev", inc("pj_ev", 1)))
                    wait("sync", "pj_ev", cur("pj_ev"))
                    sp = dma("sync", d_x2[o, :, mt * c.MT:(mt + 1) * c.MT], sstg[:, :],
                             f"psp{si}")
                    stgp.set_tok(si, (f"psp{si}", sp))
            pump_drain()
            barrier()
        ph_y.close()

        if c.SPARSE:
            ph_pump.close()

        ph_b = ExitStack()
        ph_h2 = ExitStack()
        NTK = OWN // 128
        sb_cwtm = (None if c.SPARSE else
                   ph_b.enter_context(nc.sbuf_tensor("cwtm", [128, NTK * c.E], F32)))
        sb_etile_i = ph_b.enter_context(nc.sbuf_tensor("etileI", [1, 2 * c.NTILE_R],
                                                       mybir.dt.int32))
        sb_idxw_i = ph_b.enter_context(nc.sbuf_tensor("idxwI", [128, 2 * c.NTILE_R],
                                                      mybir.dt.int32))
        _h2scope = ph_h2 if c.SPARSE else ph_b
        sb_h2 = [_h2scope.enter_context(nc.sbuf_tensor(f"h2T{i}", [128, OWN], BF16))
                 for i in range(DC)]
        with ExitStack() as ph:
            for r in range(4):
                wait("sync", f"psp{r}", cur(f"psp{r}"))
            emit_ln("ln2", OWN // TT,
                    lambda i, t: d_x2[i, :, t * TT:(t + 1) * TT],
                    sb_h2, ph)
            barrier()

        with ExitStack() as ph:
            gwsb = ph.enter_context(nc.sbuf_tensor("gwsb", [128, DC * c.E], BF16))
            dma("gpsimd", gwsb[:, :].rearrange("p (i e) -> p i e", e=c.E),
                d_gwT.ap().rearrange("(i p) e -> p i e", p=128), "gwl")
            wait("tensor", "gwl", cur("gwl"))
            e8 = Ring([ph.enter_context(nc.sbuf_tensor(f"e8_{r}", [128, c.E], F32))
                       for r in range(2)])
            rsm = ph.enter_context(nc.sbuf_tensor("rsm", [128, 8], F32))
            cwtk = ph.enter_context(nc.sbuf_tensor("cwtk", [128, c.E], F32))
            ps_lg = Ring([ph.enter_context(nc.psum_tensor(f"pslg{r}", [128, c.E], F32))
                          for r in range(2)])
            if c.SPARSE:
                msk8 = ph.enter_context(nc.sbuf_tensor("msk8", [128, c.E], F32))
                sb_H = [ph.enter_context(nc.sbuf_tensor(f"sbH{r}", [128, NTK * c.E], BF16))
                        for r in range(2)]
                sb_w12 = ph.enter_context(nc.sbuf_tensor("sbw12", [128, 2 * NTK], F32))
                sb_cnt = ph.enter_context(nc.sbuf_tensor("sbcnt", [8, 2 * NTK], F32))
                ps_cnt = ph.enter_context(nc.psum_tensor("pscnt", [8, 2], F32))
                wait("sync", "ln_apl", cur("ln_apl"))
                for i in range(DC):
                    dma("sync", d_h2fm[i, :, :], sb_h2[i][:, :], "h2sp")
                sb_idb5 = ph.enter_context(nc.sbuf_tensor("idb5", [128, 128], BF16))
                dma("sync", sb_idb5[:, :], d_idbf[:, :], "mskld")
                idb5_t = cur("mskld")
                hstg = Ring([ph.enter_context(nc.sbuf_tensor(f"hst{r}", [128, c.D], BF16))
                             for r in range(2)])
                ps_ht = ph.enter_context(nc.psum_tensor("psht", [128, 128], BF16))
                wait("tensor", "mskld", idb5_t)
                wait("tensor", "ln_apl", cur("ln_apl"))
                ps_ht_tok = [None]
            for tk in range(NTK):
                if c.SPARSE:
                    tb = tk
                    i7, hbuf, htok = hstg.get()
                    for ic in range(DC):
                        s7 = 0
                        psh = ps_ht[:, :]
                        wait_tok("tensor", ps_ht_tok[s7])
                        def f(eng, psh=psh, ic=ic, tb=tb):
                            mm = eng.transpose(psh, sb_h2[ic][:, tb * 128:(tb + 1) * 128],
                                               sb_idb5[:, :])
                            mm.then_inc(sems["m_tp"], 1)
                        add("tensor", f)
                        t7 = inc("m_tp", 1)
                        wait("vector", "m_tp", t7)
                        if ic == 0:
                            wait_tok("vector", htok)
                        def f(eng, psh=psh, hbuf=hbuf, ic=ic):
                            i0 = eng.tensor_copy(hbuf[:, ic * 128:(ic + 1) * 128], psh)
                            i0.then_inc(sems["m_evS"], 1)
                        add("vector", f)
                        ps_ht_tok[s7] = ("m_evS", inc("m_evS", 1))
                    wait("sync", "m_evS", cur("m_evS"))
                    sp7 = dma("sync", d_h2tm[tb * 128:(tb + 1) * 128, :], hbuf[:, :],
                              "h2sp")
                    hstg.set_tok(i7, ("h2sp", sp7))
                li, lgb, ltok = ps_lg.get()
                wait_tok("tensor", ltok)
                def f(eng, lgb=lgb, tk=tk, h2L=tuple(sb_h2)):
                    for i in range(DC):
                        mm = eng.matmul(lgb[:, :], h2L[i][:, tk * 128:(tk + 1) * 128],
                                        gwsb[:, i * c.E:(i + 1) * c.E],
                                        start=(i == 0), stop=(i == DC - 1))
                    mm.then_inc(sems["rt_mm"], 1)
                add("tensor", f)
                mmt = inc("rt_mm", 1)
                wait("vector", "rt_mm", mmt)
                wait("vector", "rt_exp", cur("rt_exp"))
                vg = inc("vch", 1)
                def f(eng, lgb=lgb, vg=vg):
                    eng.reduce_max(rsm[:, 0:1], lgb[:, :], axis=AX.X).then_inc(sems["vch"], 1)
                    eng.wait_ge(sems["vch"], vg)
                    i0 = eng.tensor_scalar_mul(rsm[:, 1:2], rsm[:, 0:1], -1.0)
                    i0.then_inc(sems["rt_cw"], 1)
                add("vector", f)
                v1 = inc("rt_cw", 1)
                wait("scalar", "rt_cw", v1)
                ei, e8b, etok = e8.get()
                wait_tok("scalar", etok)
                def f(eng, lgb=lgb, e8b=e8b):
                    a = eng.activation(out=e8b[:, :], in_=lgb[:, :], func=AF.Exp,
                                       bias=rsm[:, 1:2], scale=1.0)
                    a.then_inc(sems["rt_exp"], 1)
                add("scalar", f)
                ex = inc("rt_exp", 1)
                ps_lg.set_tok(li, ("rt_exp", ex))
                wait("vector", "rt_exp", ex)
                wait("vector", "rt_cw", cur("rt_cw"))
                if not c.SPARSE:
                    vg0 = cur("vch")
                    inc("vch", 6)
                    def f(eng, e8b=e8b, tk=tk, vg0=vg0):
                        ops = [
                            lambda: eng.tensor_scalar(cwtk[:, :], e8b[:, :], 1.0, None, ALU.is_equal),
                            lambda: eng.scalar_tensor_tensor(cwtk[:, :], cwtk[:, :], -2.0, e8b[:, :],
                                                             ALU.mult, ALU.add),
                            lambda: eng.reduce_max(rsm[:, 2:3], cwtk[:, :], axis=AX.X),
                            lambda: eng.tensor_scalar(rsm[:, 3:4], rsm[:, 2:3], 1.0, None, ALU.add),
                            lambda: eng.reciprocal(rsm[:, 4:5], rsm[:, 3:4]),
                            lambda: eng.tensor_scalar(cwtk[:, :], e8b[:, :], rsm[:, 2:3], None, ALU.is_ge),
                        ]
                        for j, op in enumerate(ops):
                            op().then_inc(sems["vch"], 1)
                            eng.wait_ge(sems["vch"], vg0 + j + 1)
                        i0 = eng.scalar_tensor_tensor(sb_cwtm[:, tk * c.E:(tk + 1) * c.E],
                                                      e8b[:, :], rsm[:, 4:5],
                                                      cwtk[:, :], ALU.mult, ALU.mult)
                        i0.then_inc(sems["rt_cw"], 1)
                    add("vector", f)
                    e8.set_tok(ei, ("rt_cw", inc("rt_cw", 1)))
                else:
                    vg0 = cur("vch")
                    inc("vch", 8)
                    H1s = sb_H[0][:, tk * c.E:(tk + 1) * c.E]
                    H2s = sb_H[1][:, tk * c.E:(tk + 1) * c.E]
                    def f(eng, e8b=e8b, tk=tk, vg0=vg0, H1s=H1s, H2s=H2s):
                        ops = [
                            lambda: eng.tensor_scalar(cwtk[:, :], e8b[:, :], 1.0, None, ALU.is_equal),
                            lambda: eng.tensor_copy(H1s, cwtk[:, :]),
                            lambda: eng.scalar_tensor_tensor(msk8[:, :], cwtk[:, :], -2.0, e8b[:, :],
                                                             ALU.mult, ALU.add),
                            lambda: eng.reduce_max(rsm[:, 2:3], msk8[:, :], axis=AX.X),
                            lambda: eng.tensor_scalar(rsm[:, 3:4], rsm[:, 2:3], 1.0, None, ALU.add),
                            lambda: eng.reciprocal(sb_w12[:, tk:tk + 1], rsm[:, 3:4]),
                            lambda: eng.tensor_scalar(msk8[:, :], e8b[:, :], rsm[:, 2:3], None, ALU.is_ge),
                            lambda: eng.tensor_sub(msk8[:, :], msk8[:, :], cwtk[:, :]),
                        ]
                        for j, op in enumerate(ops):
                            op().then_inc(sems["vch"], 1)
                            eng.wait_ge(sems["vch"], vg0 + j + 1)
                        eng.tensor_copy(H2s, msk8[:, :]).then_inc(sems["vch"], 1)
                        eng.wait_ge(sems["vch"], vg0 + 9)
                        i0 = eng.tensor_mul(sb_w12[:, NTK + tk:NTK + tk + 1],
                                            rsm[:, 2:3], sb_w12[:, tk:tk + 1])
                        i0.then_inc(sems["rt_cw"], 1)
                    add("vector", f)
                    inc("vch", 1)
                    e8.set_tok(ei, ("rt_cw", inc("rt_cw", 1)))
                    rtc = cur("rt_cw")
                    for r in range(2):
                        Hs = sb_H[r][:, tk * c.E:(tk + 1) * c.E]
                        wait("tensor", "rt_cw", rtc)
                        if tk > 0 or r > 0:
                            wait("tensor", "rt_cn", cur("rt_cn"))
                        def f(eng, Hs=Hs, r=r):
                            mm = eng.matmul(ps_cnt[:, r:r + 1], Hs, sb_ones_bf[:, 0:1],
                                            start=True, stop=True)
                            mm.then_inc(sems["rt_mm"], 1)
                        add("tensor", f)
                        mmc = inc("rt_mm", 1)
                        wait("vector", "rt_mm", mmc)
                        def f(eng, r=r, tk=tk):
                            i0 = eng.tensor_copy(sb_cnt[:, r * NTK + tk:r * NTK + tk + 1],
                                                 ps_cnt[:, r:r + 1])
                            i0.then_inc(sems["rt_cn"], 1)
                        add("vector", f)
                        inc("rt_cn", 1)
            rt_done = cur("rt_cw")

            if c.SPARSE:
                sb_ltri = ph.enter_context(nc.sbuf_tensor("ltrisb", [128, 128], BF16))
                sb_iotaT = ph.enter_context(nc.sbuf_tensor("iotaTsb", [1, c.NTILE_R], F32))
                sb_iotaP = ph.enter_context(nc.sbuf_tensor("iotaPsb", [128, 1], F32))
                sb_idf5 = ph.enter_context(nc.sbuf_tensor("idf5", [128, 128], F32))
                dma("sync", sb_ltri[:, :], d_ltriT[:, :], "mskld")
                dma("sync", sb_iotaT[:, :], d_iotaT[:, :], "mskld")
                dma("sync", sb_iotaP[:, :], d_iotaP[:, :], "mskld")
                dma("sync", sb_idf5[:, :], d_idf[:, :], "mskld")
                aux5 = cur("mskld")
                sb_one8 = ph.enter_context(nc.sbuf_tensor("one8", [8, max(NTK, 8)], F32))
                sb_cb = ph.enter_context(nc.sbuf_tensor("sbcb", [8, 2 * NTK], F32))
                sb_sml5 = ph.enter_context(nc.sbuf_tensor("sml5", [8, 8], F32))
                sb_prow = ph.enter_context(nc.sbuf_tensor("prow", [1, 32], F32))
                sb_padb = ph.enter_context(nc.sbuf_tensor("padb", [128, 16], F32))
                sb_etf = ph.enter_context(nc.sbuf_tensor("etf", [1, 2 * c.NTILE_R], F32))
                sb_selb = ph.enter_context(nc.sbuf_tensor("selb", [128, 8], F32))
                sb_t8 = ph.enter_context(nc.sbuf_tensor("t8", [128, 8], F32))
                sb_cbr = ph.enter_context(nc.sbuf_tensor("cbr", [1, 8], F32))
                slt = Ring([ph.enter_context(nc.sbuf_tensor(f"slt{r}", [128, 1], mybir.dt.int32))
                            for r in range(2)])
                tkid = Ring([ph.enter_context(nc.sbuf_tensor(f"tkid{r}", [128, 1], F32))
                             for r in range(2)])
                cwv = Ring([ph.enter_context(nc.sbuf_tensor(f"cwv{r}", [128, 1], F32))
                            for r in range(2)])
                pfo = ph.enter_context(nc.sbuf_tensor("pfo", [128, c.NTILE_R], F32))
                ps_P = Ring([ph.enter_context(nc.psum_tensor(f"psP{r}", [128, 8], F32))
                             for r in range(1)])
                ps_sc = ph.enter_context(nc.psum_tensor("pssc", [1, 8], F32))
                ps_pb = ph.enter_context(nc.psum_tensor("pspb", [128, 8], F32))

                def f(eng):
                    eng.memset(sb_one8[:, :], 1.0).then_inc(sems["pfv"], 1)
                    eng.memset(pfo[:, :], float(OWN)).then_inc(sems["pfv"], 1)
                add("vector", f)
                inc("pfv", 2)
                wait("sync", "pfv", cur("pfv"))
                for r in range(2):
                    dma("sync", d_idxf[r].ap().rearrange("(p a) one -> p (a one)", p=128),
                        pfo[:, :], "ropeld")
                wait("vector", "ropeld", cur("ropeld"))
                def f(eng):
                    eng.memset(pfo[:, :], 0.0).then_inc(sems["pfv"], 1)
                add("vector", f)
                inc("pfv", 1)
                wait("sync", "pfv", cur("pfv"))
                for r in range(2):
                    dma("sync", d_cwsf[r].ap().rearrange("(p a) one -> p (a one)", p=128),
                        pfo[:, :], "ropeld")
                pfl_t = cur("ropeld")

                ps_sc_tok = None
                ps_pb_tok = None
                for r in range(2):
                    cnt_s = sb_cnt[:, r * NTK:(r + 1) * NTK]
                    cb_s = sb_cb[:, r * NTK:(r + 1) * NTK]
                    wait("vector", "rt_cn", cur("rt_cn"))
                    NPK = NTK
                    vg = cur("vch")
                    inc("vch", 3 + NPK)
                    def f(eng, cnt_s=cnt_s, cb_s=cb_s, r=r, vg=vg, NPK=NPK):
                        eng.tensor_tensor_scan(cb_s, cnt_s, sb_one8[:, 0:NTK], 0.0,
                                               ALU.add, ALU.mult).then_inc(sems["vch"], 1)
                        eng.wait_ge(sems["vch"], vg + 1)
                        eng.tensor_sub(cb_s, cb_s, cnt_s).then_inc(sems["vch"], 1)
                        eng.reduce_sum(sb_sml5[:, 4 * r:4 * r + 1], cnt_s, axis=AX.X).then_inc(sems["vch"], 1)
                        eng.wait_ge(sems["vch"], vg + 3)
                        nslot = sb_sml5[:, 4 * r:4 * r + 1]
                        acc = sb_sml5[:, 4 * r + 1:4 * r + 2]
                        w = vg + 3
                        eng.tensor_scalar(acc, nslot, 0.0, None, ALU.is_gt).then_inc(sems["vch"], 1)
                        w += 1
                        eng.wait_ge(sems["vch"], w)
                        for k in range(1, NPK):
                            eng.scalar_tensor_tensor(acc, nslot, float(128 * k), acc,
                                                     ALU.is_gt, ALU.add).then_inc(sems["vch"], 1)
                            w += 1
                            eng.wait_ge(sems["vch"], w)
                        i0 = eng.tensor_scalar(sb_sml5[:, 4 * r + 3:4 * r + 4],
                                               acc, 128.0, None, ALU.mult)
                        i0.then_inc(sems["rt_cn"], 1)
                    add("vector", f)
                    pz = inc("rt_cn", 1)
                    wait("tensor", "rt_cn", pz)
                    wait("tensor", "mskld", aux5)
                    wait_tok("tensor", ps_sc_tok)
                    def f(eng, r=r):
                        mm = eng.transpose(ps_sc[0:1, :], sb_sml5[:, 4 * r + 3:4 * r + 4],
                                           sb_idf5[0:8, 0:8])
                        mm.then_inc(sems["rt_mm"], 1)
                    add("tensor", f)
                    tmm = inc("rt_mm", 1)
                    wait("vector", "rt_mm", tmm)
                    vg = cur("vch")
                    inc("vch", 2)
                    def f(eng, r=r, vg=vg):
                        eng.tensor_copy(sb_prow[0:1, 16 * r:16 * r + 8], ps_sc[0:1, :]).then_inc(sems["vch"], 1)
                        eng.wait_ge(sems["vch"], vg + 1)
                        eng.tensor_tensor_scan(sb_prow[0:1, 16 * r + 8:16 * r + 16],
                                               sb_prow[0:1, 16 * r:16 * r + 8],
                                               sb_one8[0:1, 0:8], 0.0,
                                               ALU.add, ALU.mult).then_inc(sems["vch"], 1)
                        eng.wait_ge(sems["vch"], vg + 2)
                        i0 = eng.tensor_sub(sb_prow[0:1, 16 * r:16 * r + 8],
                                            sb_prow[0:1, 16 * r + 8:16 * r + 16],
                                            sb_prow[0:1, 16 * r:16 * r + 8])
                        i0.then_inc(sems["rt_cn"], 1)
                    add("vector", f)
                    ps_sc_tok = ("rt_cn", inc("rt_cn", 1))
                    prow_t = ps_sc_tok[1]
                    wait("tensor", "rt_cn", prow_t)
                    wait_tok("tensor", ps_pb_tok)
                    def f(eng, r=r):
                        mm = eng.matmul(ps_pb[:, :], sb_ones_f[0:1, :],
                                        sb_prow[0:1, 16 * r:16 * r + 8],
                                        start=True, stop=True)
                        mm.then_inc(sems["rt_mm"], 1)
                    add("tensor", f)
                    bmm = inc("rt_mm", 1)
                    wait("vector", "rt_mm", bmm)
                    def f(eng, r=r):
                        i0 = eng.tensor_copy(sb_padb[:, 8 * r:8 * r + 8], ps_pb[:, :])
                        i0.then_inc(sems["rt_cn"], 1)
                    add("vector", f)
                    ps_pb_tok = ("rt_cn", inc("rt_cn", 1))
                    wait("vector", "rt_cn", cur("rt_cn"))
                    vg = cur("vch")
                    inc("vch", 9)
                    def f(eng, r=r, vg=vg):
                        eng.tensor_scalar(sb_etf[0:1, r * c.NTILE_R:(r + 1) * c.NTILE_R],
                                          sb_iotaT[0:1, :], sb_prow[0:1, 16 * r:16 * r + 1],
                                          None, ALU.is_ge).then_inc(sems["vch"], 1)
                        for e2 in range(1, 8):
                            eng.wait_ge(sems["vch"], vg + e2)
                            eng.scalar_tensor_tensor(
                                sb_etf[0:1, r * c.NTILE_R:(r + 1) * c.NTILE_R],
                                sb_iotaT[0:1, :], sb_prow[0:1, 16 * r + e2:16 * r + e2 + 1],
                                sb_etf[0:1, r * c.NTILE_R:(r + 1) * c.NTILE_R],
                                ALU.is_ge, ALU.add).then_inc(sems["vch"], 1)
                        eng.wait_ge(sems["vch"], vg + 8)
                        eng.tensor_scalar(sb_etf[0:1, r * c.NTILE_R:(r + 1) * c.NTILE_R],
                                          sb_etf[0:1, r * c.NTILE_R:(r + 1) * c.NTILE_R],
                                          -1.0, None, ALU.add).then_inc(sems["vch"], 1)
                        eng.wait_ge(sems["vch"], vg + 9)
                        i0 = eng.tensor_copy(sb_etile_i[0:1, r * c.NTILE_R:(r + 1) * c.NTILE_R],
                                             sb_etf[0:1, r * c.NTILE_R:(r + 1) * c.NTILE_R])
                        i0.then_inc(sems["cmpz"], 1)
                    add("vector", f)
                    inc("cmpz", 1)

                    for tk in range(NTK):
                        Hs = sb_H[r][:, tk * c.E:(tk + 1) * c.E]
                        pi5, Pps, Ptok = ps_P.get()
                        wait_tok("tensor", Ptok)
                        wait("tensor", "rt_cw", rt_done)
                        def f(eng, Pps=Pps, Hs=Hs):
                            mm = eng.matmul(Pps[:, :], sb_ltri[:, :], Hs,
                                            start=True, stop=True)
                            mm.then_inc(sems["rt_mm"], 1)
                        add("tensor", f)
                        pmm = inc("rt_mm", 1)
                        wait("tensor", "rt_cn", cur("rt_cn"))
                        wait_tok("tensor", ps_sc_tok)
                        def f(eng, r=r, tk=tk):
                            mm = eng.transpose(ps_sc[0:1, :],
                                               sb_cb[:, r * NTK + tk:r * NTK + tk + 1],
                                               sb_idf5[0:8, 0:8])
                            mm.then_inc(sems["rt_mm"], 1)
                        add("tensor", f)
                        cmm = inc("rt_mm", 1)
                        wait("vector", "rt_mm", cmm)
                        def f(eng):
                            i0 = eng.tensor_copy(sb_cbr[0:1, :], ps_sc[0:1, :])
                            i0.then_inc(sems["rt_cn"], 1)
                        add("vector", f)
                        ps_sc_tok = ("rt_cn", inc("rt_cn", 1))
                        wait("tensor", "rt_cn", ps_sc_tok[1])
                        wait_tok("tensor", ps_pb_tok)
                        def f(eng):
                            mm = eng.matmul(ps_pb[:, :], sb_ones_f[0:1, :], sb_cbr[0:1, :],
                                            start=True, stop=True)
                            mm.then_inc(sems["rt_mm"], 1)
                        add("tensor", f)
                        bmm2 = inc("rt_mm", 1)
                        si5, slti, stok5 = slt.get()
                        ti5, tkb, ttok5 = tkid.get()
                        ci5, cwb5, ctok5 = cwv.get()
                        wait("vector", "rt_mm", max(pmm, bmm2))
                        wait_tok("vector", stok5)
                        wait_tok("vector", ttok5)
                        wait_tok("vector", ctok5)
                        vg = cur("vch")
                        inc("vch", 5)
                        wsrc = sb_w12[:, (r * NTK) + tk:(r * NTK) + tk + 1]
                        def f(eng, r=r, tk=tk, Pps=Pps, Hs=Hs, slti=slti, tkb=tkb,
                              cwb5=cwb5, vg=vg, wsrc=wsrc):
                            eng.tensor_add(sb_selb[:, :], ps_pb[:, :],
                                           sb_padb[:, 8 * r:8 * r + 8]).then_inc(sems["vch"], 1)
                            eng.wait_ge(sems["vch"], vg + 1)
                            eng.tensor_add(sb_t8[:, :], Pps[:, :], sb_selb[:, :]).then_inc(sems["vch"], 1)
                            eng.wait_ge(sems["vch"], vg + 2)
                            eng.tensor_mul(sb_t8[:, :], sb_t8[:, :], Hs).then_inc(sems["vch"], 1)
                            eng.wait_ge(sems["vch"], vg + 3)
                            eng.reduce_sum(sb_t8[:, 0:1], sb_t8[:, :], axis=AX.X).then_inc(sems["vch"], 1)
                            eng.wait_ge(sems["vch"], vg + 4)
                            eng.tensor_copy(slti[:, :], sb_t8[:, 0:1]).then_inc(sems["vch"], 1)
                            eng.tensor_scalar(tkb[:, :], sb_iotaP[:, :], float(tk * 128),
                                              None, ALU.add)
                            i0 = eng.tensor_copy(cwb5[:, :], wsrc)
                            i0.then_inc(sems["cmpv"], 1)
                        add("vector", f)
                        cmv = inc("cmpv", 1)
                        ps_P.set_tok(pi5, ("cmpv", cmv))
                        ps_pb_tok = ("cmpv", cmv)
                        wait("gpsimd", "cmpv", cmv)
                        wait("gpsimd", "ropeld", pfl_t)
                        ssem = f"sct{si5}"
                        getsem(ssem)
                        def f(eng, r=r, slti=slti, tkb=tkb, cwb5=cwb5, ssem=ssem):
                            n0 = nc.n_instructions()
                            d0 = eng.indirect_dma_start(
                                out=d_idxf[r].ap(),
                                out_offset=bass.IndirectOffsetOnAxis(slti[:, 0:1], axis=0),
                                in_=tkb[:, :], in_offset=None,
                                bounds_check=c.NSLOT_R - 1, oob_is_err=False)
                            d0.then_inc(sems[ssem], 16)
                            d1 = eng.indirect_dma_start(
                                out=d_cwsf[r].ap(),
                                out_offset=bass.IndirectOffsetOnAxis(slti[:, 0:1], axis=0),
                                in_=cwb5[:, :], in_offset=None,
                                bounds_check=c.NSLOT_R - 1, oob_is_err=False)
                            d1.then_inc(sems[ssem], 16)
                            _free_val_regs(nc, eng, n0)
                        add("gpsimd", f)
                        sc5 = inc(ssem, 32)
                        slt.set_tok(si5, (ssem, sc5))
                        tkid.set_tok(ti5, (ssem, sc5))
                        cwv.set_tok(ci5, (ssem, sc5))

                NT_ALL = 2 * c.NTILE_R
                sb_chg = ph.enter_context(nc.sbuf_tensor("chg", [1, NT_ALL], F32))
                sb_wi1 = ph.enter_context(nc.sbuf_tensor("wi1", [128, NT_ALL], F32))
                ps_wi = ph.enter_context(nc.psum_tensor("pswi", [128, NT_ALL], F32))
                wait("vector", "cmpz", cur("cmpz"))
                vg = cur("vch")
                inc("vch", 3)
                def f(eng, vg=vg):
                    eng.memset(sb_chg[0:1, :], 1.0).then_inc(sems["vch"], 1)
                    eng.wait_ge(sems["vch"], vg + 1)
                    eng.tensor_tensor(sb_chg[0:1, 1:NT_ALL], sb_etf[0:1, 1:NT_ALL],
                                      sb_etf[0:1, 0:NT_ALL - 1],
                                      ALU.not_equal).then_inc(sems["vch"], 1)
                    eng.wait_ge(sems["vch"], vg + 2)
                    eng.tensor_scalar(sb_chg[0:1, :], sb_chg[0:1, :],
                                      float(-c.E * 128), None,
                                      ALU.mult).then_inc(sems["vch"], 1)
                    eng.wait_ge(sems["vch"], vg + 3)
                    i0 = eng.scalar_tensor_tensor(sb_chg[0:1, :], sb_etf[0:1, :],
                                                  128.0, sb_chg[0:1, :], ALU.mult, ALU.add)
                    i0.then_inc(sems["cmpz"], 1)
                add("vector", f)
                chg1 = inc("cmpz", 1)
                wait("vector", "cmpz", chg1)
                def f(eng):
                    i0 = eng.tensor_scalar(sb_chg[0:1, :], sb_chg[0:1, :],
                                           float(c.E * 128), None, ALU.add)
                    i0.then_inc(sems["cmpz"], 1)
                add("vector", f)
                chg2 = inc("cmpz", 1)
                wait("tensor", "cmpz", chg2)
                wait_tok("tensor", ps_pb_tok)
                def f(eng):
                    mm = eng.matmul(ps_wi[:, :], sb_ones_f[0:1, :], sb_chg[0:1, :],
                                    start=True, stop=True)
                    mm.then_inc(sems["rt_mm"], 1)
                add("tensor", f)
                wmm = inc("rt_mm", 1)
                wait("vector", "rt_mm", wmm)
                vg2 = cur("vch")
                inc("vch", 1)
                def f(eng, vg2=vg2):
                    eng.tensor_scalar(sb_wi1[:, :], ps_wi[:, :], sb_iotaP[:, 0:1],
                                      None, ALU.add).then_inc(sems["vch"], 1)
                    eng.wait_ge(sems["vch"], vg2 + 1)
                    i0 = eng.tensor_copy(sb_idxw_i[:, :], sb_wi1[:, :])
                    i0.then_inc(sems["cmpz"], 1)
                add("vector", f)
                inc("cmpz", 1)

            barrier()

        if c.SPARSE:
            ph_h2.close()
            with ExitStack() as ph:
                NT_ALL = 2 * c.NTILE_R
                FSL = [(fs * 512, min(512, c.F - fs * 512))
                       for fs in range((c.F + 511) // 512)]
                sb_idbf = ph.enter_context(nc.sbuf_tensor("idbfsb", [128, 128], BF16))
                sb_idf6 = ph.enter_context(nc.sbuf_tensor("idf6", [128, 128], F32))
                dma("sync", sb_idbf[:, :], d_idbf[:, :], "yixld")
                dma("sync", sb_idf6[:, :], d_idf[:, :], "yixld")
                ps_A = ph.enter_context(nc.psum_tensor("psA6", [128, c.F], F32))
                ps_B = ph.enter_context(nc.psum_tensor("psB6", [128, c.F], F32))
                ps_i = ps_A
                sb_ir = [ph.enter_context(nc.sbuf_tensor(f"idxraw{r}", [c.NTILE_R, 128], F32))
                         for r in range(2)]
                sb_cr = [ph.enter_context(nc.sbuf_tensor(f"cwsraw{r}", [c.NTILE_R, 128], F32))
                         for r in range(2)]
                sb_idx = ph.enter_context(nc.sbuf_tensor("sbidx", [128, NT_ALL], mybir.dt.int32))
                sb_cws = ph.enter_context(nc.sbuf_tensor("sbcws", [128, NT_ALL], F32))
                wait("sync", "sct0", cur("sct0"))
                wait("sync", "sct1", cur("sct1"))
                for r in range(2):
                    dma("sync", sb_ir[r][:, :],
                        d_idxf[r].ap().rearrange("(j p) one -> j (p one)", p=128), "yixld")
                    dma("sync", sb_cr[r][:, :],
                        d_cwsf[r].ap().rearrange("(j p) one -> j (p one)", p=128), "yixld")
                aux6 = cur("yixld")
                ps_x = Ring([ph.enter_context(nc.psum_tensor(f"psx{r}", [128, 128], BF16))
                             for r in range(2)])
                wait("tensor", "yixld", aux6)
                NTr = c.NTILE_R
                def f(eng):
                    eng.transpose(ps_i[:, 0:NTr], sb_ir[0][:, :], sb_idf6[0:NTr, 0:NTr])
                    eng.transpose(ps_i[:, NTr:NT_ALL], sb_ir[1][:, :],
                                  sb_idf6[0:NTr, 0:NTr]).then_inc(sems["m_tp"], 1)
                add("tensor", f)
                t1 = inc("m_tp", 1)
                wait("vector", "m_tp", t1)
                def f(eng):
                    i0 = eng.tensor_copy(sb_idx[:, :], ps_i[:, 0:NT_ALL])
                    i0.then_inc(sems["sidx"], 1)
                add("vector", f)
                inc("sidx", 1)
                wait("tensor", "sidx", cur("sidx"))
                def f(eng):
                    eng.transpose(ps_i[:, 0:NTr], sb_cr[0][:, :], sb_idf6[0:NTr, 0:NTr])
                    eng.transpose(ps_i[:, NTr:NT_ALL], sb_cr[1][:, :],
                                  sb_idf6[0:NTr, 0:NTr]).then_inc(sems["m_tp"], 1)
                add("tensor", f)
                t2 = inc("m_tp", 1)
                wait("vector", "m_tp", t2)
                def f(eng):
                    i0 = eng.tensor_copy(sb_cws[:, :], ps_i[:, 0:NT_ALL])
                    i0.then_inc(sems["sidx"], 1)
                add("vector", f)
                inc("sidx", 1)

                wgsb = ph.enter_context(nc.sbuf_tensor("wgsb6", [128, DC * c.F], BF16))
                wusb = ph.enter_context(nc.sbuf_tensor("wusb6", [128, DC * c.F], BF16))
                wdsb = ph.enter_context(nc.sbuf_tensor("wdsb6", [128, c.FC * c.D], BF16))
                xg = DmaRing("xg", [ph.enter_context(nc.sbuf_tensor(f"xg{r}", [128, c.D], BF16))
                                    for r in range(2)])
                XT = Ring([ph.enter_context(nc.sbuf_tensor(f"XT{r}", [128, DC * 128], BF16))
                           for r in range(2)])
                sgb = Ring([ph.enter_context(nc.sbuf_tensor(f"sgb6{r}", [128, c.F], BF16))
                            for r in range(1)])
                ub = Ring([ph.enter_context(nc.sbuf_tensor(f"ub6{r}", [128, c.F], BF16))
                           for r in range(1)])
                gq = Ring([ph.enter_context(nc.sbuf_tensor(f"gq6{r}", [128, c.F], BF16))
                           for r in range(1)])
                gT = Ring([ph.enter_context(nc.sbuf_tensor(f"gT6{r}", [128, c.FC * 128], BF16))
                           for r in range(2)])
                eob = Ring([ph.enter_context(nc.sbuf_tensor(f"eob{r}", [128, c.D], BF16))
                            for r in range(2)])
                pump_done_waits("sync")
                wait("sync", "cmpz", cur("cmpz"))
                wait("gpsimd", "sidx", cur("sidx"))
                wait("scalar", "sidx", cur("sidx"))
                wait("vector", "sidx", cur("sidx"))
                psA_tok = None
                psB_tok = None
                if True:
                    for j in range(NT_ALL):
                        rblk = 0 if j < c.NTILE_R else 1
                        d_yr = d_y1 if rblk == 0 else d_y2
                        gi6, xbuf6, xtok6 = xg.get()
                        wait_tok("gpsimd", xtok6)
                        def f(eng, xbuf6=xbuf6, j=j, gs=xg.sem(gi6)):
                            n0 = nc.n_instructions()
                            dd = eng.indirect_dma_start(
                                out=xbuf6[:, :], out_offset=None,
                                in_=d_h2tm.ap(),
                                in_offset=bass.IndirectOffsetOnAxis(sb_idx[:, j:j + 1], axis=0),
                                bounds_check=OWN - 1, oob_is_err=False)
                            dd.then_inc(sems[gs], 16)
                            _free_val_regs(nc, eng, n0)
                        add("gpsimd", f)
                        xld = inc(xg.sem(gi6), 16)
                        ga_pre, up_pre, dn_pre = cur("m_ga"), cur("m_up"), cur("m_dn")
                        wait("gpsimd", "m_ga", ga_pre)
                        wait("gpsimd", "m_up", up_pre)
                        wait("gpsimd", "m_dn", dn_pre)
                        def f(eng, j=j):
                            n0 = nc.n_instructions()
                            eng.indirect_dma_start(
                                out=wgsb[:, :], out_offset=None,
                                in_=d_wgbf.ap(),
                                in_offset=bass.IndirectOffsetOnAxis(sb_idxw_i[:, j:j + 1], axis=0),
                                bounds_check=c.E * 128 - 1,
                                oob_is_err=False).then_inc(sems["swg"], 16)
                            eng.indirect_dma_start(
                                out=wusb[:, :], out_offset=None,
                                in_=d_wubf.ap(),
                                in_offset=bass.IndirectOffsetOnAxis(sb_idxw_i[:, j:j + 1], axis=0),
                                bounds_check=c.E * 128 - 1,
                                oob_is_err=False).then_inc(sems["swu"], 16)
                            eng.indirect_dma_start(
                                out=wdsb[:, :], out_offset=None,
                                in_=d_wdbf.ap(),
                                in_offset=bass.IndirectOffsetOnAxis(sb_idxw_i[:, j:j + 1], axis=0),
                                bounds_check=c.E * 128 - 1,
                                oob_is_err=False).then_inc(sems["swd"], 16)
                            _free_val_regs(nc, eng, n0)
                        add("gpsimd", f)
                        inc("swg", 16); inc("swu", 16); inc("swd", 16)
                        xi6, XTb, XTtok = XT.get()
                        wait("tensor", xg.sem(gi6), xld)
                        wait_tok("tensor", XTtok)
                        for ic in range(DC):
                            pi6, psx, pxtok = ps_x.get()
                            wait_tok("tensor", pxtok)
                            def f(eng, psx=psx, xbuf6=xbuf6, ic=ic):
                                mm = eng.transpose(psx[:, :], xbuf6[:, ic * 128:(ic + 1) * 128],
                                                   sb_idbf[:, :])
                                mm.then_inc(sems["m_tp"], 1)
                            add("tensor", f)
                            tmm6 = inc("m_tp", 1)
                            wait("vector", "m_tp", tmm6)
                            def f(eng, psx=psx, XTb=XTb, ic=ic):
                                i0 = eng.tensor_copy(XTb[:, ic * 128:(ic + 1) * 128], psx[:, :])
                                i0.then_inc(sems["m_cb"], 1)
                            add("vector", f)
                            ps_x.set_tok(pi6, ("m_cb", inc("m_cb", 1)))
                        xev = cur("m_cb")
                        xg.set_tok(gi6, ("m_tp", cur("m_tp")))
                        if j == 0:
                            wait("tensor", "sidx", cur("sidx"))
                        wait("tensor", "swg", 16 * (j + 1))
                        wait("tensor", "m_cb", xev)
                        wait_tok("tensor", psA_tok)
                        def f(eng, XTb=XTb):
                            for ic in range(DC):
                                for fs0, fn in FSL:
                                    mm = eng.matmul(ps_A[:, fs0:fs0 + fn],
                                                    XTb[:, ic * 128:(ic + 1) * 128],
                                                    wgsb[:, ic * c.F + fs0:ic * c.F + fs0 + fn],
                                                    start=(ic == 0), stop=(ic == DC - 1))
                            mm.then_inc(sems["m_ga"], 1)
                        add("tensor", f)
                        gat = inc("m_ga", 1)
                        si6, sgb6, sgtok6 = sgst_ = sgb.get()
                        wait("scalar", "m_ga", gat)
                        wait_tok("scalar", sgtok6)
                        def f(eng, sgb6=sgb6):
                            a = eng.activation(out=sgb6[:, :], in_=ps_A[:, :], func=AF.Silu)
                            a.then_inc(sems["m_evS"], 1)
                        add("scalar", f)
                        evs6 = inc("m_evS", 1)
                        wait("tensor", "swu", 16 * (j + 1))
                        wait_tok("tensor", psB_tok)
                        def f(eng, XTb=XTb):
                            for ic in range(DC):
                                for fs0, fn in FSL:
                                    mm = eng.matmul(ps_B[:, fs0:fs0 + fn],
                                                    XTb[:, ic * 128:(ic + 1) * 128],
                                                    wusb[:, ic * c.F + fs0:ic * c.F + fs0 + fn],
                                                    start=(ic == 0), stop=(ic == DC - 1))
                            mm.then_inc(sems["m_up"], 1)
                        add("tensor", f)
                        upt = inc("m_up", 1)
                        XT.set_tok(xi6, ("m_up", upt))
                        ui6, ub6, ubtok6 = ub.get()
                        wait("vector", "m_up", upt)
                        wait_tok("vector", ubtok6)
                        def f(eng, ub6=ub6, j=j):
                            i0 = eng.tensor_scalar(ub6[:, :], ps_B[:, :],
                                                   sb_cws[:, j:j + 1], None, ALU.mult)
                            i0.then_inc(sems["m_evV"], 1)
                        add("vector", f)
                        evv6 = inc("m_evV", 1)
                        qi6, gq6, gqtok6 = gq.get()
                        wait("vector", "m_evS", evs6)
                        wait("vector", "m_evV", evv6)
                        wait_tok("vector", gqtok6)
                        def f(eng, gq6=gq6, sgb6=sgb6, ub6=ub6):
                            i0 = eng.tensor_mul(gq6[:, :], sgb6[:, :], ub6[:, :])
                            i0.then_inc(sems["m_gp"], 1)
                        add("vector", f)
                        gp6 = inc("m_gp", 1)
                        sgb.set_tok(si6, ("m_gp", gp6))
                        ub.set_tok(ui6, ("m_gp", gp6))
                        gi7, gTb, gTtok = gT.get()
                        wait("tensor", "m_gp", gp6)
                        wait_tok("tensor", gTtok)
                        for fc in range(c.FC):
                            pi7, psx, pxtok = ps_x.get()
                            wait_tok("tensor", pxtok)
                            def f(eng, psx=psx, gq6=gq6, fc=fc):
                                mm = eng.transpose(psx[:, :], gq6[:, fc * 128:(fc + 1) * 128],
                                                   sb_idbf[:, :])
                                mm.then_inc(sems["m_tp"], 1)
                            add("tensor", f)
                            tmm7 = inc("m_tp", 1)
                            wait("vector", "m_tp", tmm7)
                            def f(eng, psx=psx, gTb=gTb, fc=fc):
                                i0 = eng.tensor_copy(gTb[:, fc * 128:(fc + 1) * 128], psx[:, :])
                                i0.then_inc(sems["m_cbe"], 1)
                            add("vector", f)
                            ps_x.set_tok(pi7, ("m_cbe", inc("m_cbe", 1)))
                        gev = cur("m_cbe")
                        gq.set_tok(qi6, ("m_tp", cur("m_tp")))
                        ei6, eo6, eotok6 = eob.get()
                        DH = c.D // 2
                        DNS = [(nn * 512, min(512, DH - nn * 512))
                               for nn in range((DH + 511) // 512)]
                        for hhf, pst in ((0, ps_A), (1, ps_B)):
                            wait("tensor", "m_cbe", gev)
                            wait("tensor", "swd", 16 * (j + 1))
                            if hhf == 0:
                                wait("tensor", "m_evS", evs6)
                            else:
                                wait("tensor", "m_evV", evv6)
                            def f(eng, gTb=gTb, pst=pst, hhf=hhf, DH=DH, DNS=DNS):
                                for fc in range(c.FC):
                                    for n0, nw in DNS:
                                        mm = eng.matmul(
                                            pst[:, n0:n0 + nw],
                                            gTb[:, fc * 128:(fc + 1) * 128],
                                            wdsb[:, fc * c.D + hhf * DH + n0:
                                                 fc * c.D + hhf * DH + n0 + nw],
                                            start=(fc == 0), stop=(fc == c.FC - 1))
                                mm.then_inc(sems["m_dn"], 1)
                            add("tensor", f)
                            dnt = inc("m_dn", 1)
                            wait("vector", "m_dn", dnt)
                            if hhf == 0:
                                wait_tok("vector", eotok6)
                            def f(eng, eo6=eo6, pst=pst, hhf=hhf, DH=DH):
                                i0 = eng.tensor_copy(eo6[:, hhf * DH:(hhf + 1) * DH],
                                                     pst[:, 0:DH])
                                i0.then_inc(sems["m_acc"], 1)
                            add("vector", f)
                            acc6 = inc("m_acc", 1)
                            if hhf == 0:
                                psA_tok = ("m_acc", acc6)
                            else:
                                psB_tok = ("m_acc", acc6)
                        gT.set_tok(gi7, ("m_dn", cur("m_dn")))
                        wait("gpsimd", "m_acc", cur("m_acc"))
                        def f(eng, eo6=eo6, j=j, d_yr=d_yr, es=f"eos{ei6}"):
                            n0 = nc.n_instructions()
                            dd = eng.indirect_dma_start(
                                out=d_yr.ap(),
                                out_offset=bass.IndirectOffsetOnAxis(sb_idx[:, j:j + 1], axis=0),
                                in_=eo6[:, :], in_offset=None,
                                bounds_check=OWN - 1, oob_is_err=False)
                            dd.then_inc(sems[es], 16)
                            _free_val_regs(nc, eng, n0)
                        add("gpsimd", f)
                        eob.set_tok(ei6, (f"eos{ei6}", inc(f"eos{ei6}", 16)))
                barrier()

        with ExitStack() as ph:
            GMAX = 6
            MH, MQ, NMQ = c.MH, c.MQ, c.NMQ
            if c.SPARSE:
                sb_h2 = [ph.enter_context(nc.sbuf_tensor(f"h2Tb{i}", [128, OWN], BF16))
                         for i in range(DC)]
                for i in range(DC):
                    dma("sync", sb_h2[i][:, :], d_h2fm[i, :, :], "h2sp")
                wait("tensor", "h2sp", cur("h2sp"))
                wait("vector", "h2sp", cur("h2sp"))
            sb_idf = ph.enter_context(nc.sbuf_tensor("idfsb", [128, 128], F32))
            dma("sync", sb_idf[:, :], d_idf[:, :], "idfld")
            wait("tensor", "idfld", cur("idfld"))
            sb_yacc = [ph.enter_context(nc.sbuf_tensor(f"yac{i}", [128, MH], BF16))
                       for i in range(DC)]
            wg_r = DmaRing("wpj", [ph.enter_context(nc.sbuf_tensor(f"mwg{r}", [128, DC * 128], BF16))
                                   for r in range(2)])
            wu_r = DmaRing("mwu", [ph.enter_context(nc.sbuf_tensor(f"mwu{r}", [128, DC * 128], BF16))
                                   for r in range(2)])
            wd_r = DmaRing("mwd", [ph.enter_context(nc.sbuf_tensor(f"mwd{r}", [128, DC * 128], BF16))
                                   for r in range(GMAX)])
            gbuf = [ph.enter_context(nc.sbuf_tensor(f"gbf{j}", [128, MH], BF16))
                    for j in range(GMAX)]
            sgst = Ring([ph.enter_context(nc.sbuf_tensor(f"sgs{r}", [128, MH], BF16))
                         for r in range(2)])
            usst = Ring([ph.enter_context(nc.sbuf_tensor(f"uss{r}", [128, MH], BF16))
                         for r in range(2)])
            cwrow = ph.enter_context(nc.sbuf_tensor("cwrow", [1, MH], F32))
            cwb = ph.enter_context(nc.sbuf_tensor("cwbb", [128, MH], F32))
            xo2 = DmaRing("ox", [ph.enter_context(nc.sbuf_tensor(f"xo2{r}", [128, MH], F32))
                                 for r in range(2)])
            yrb = DmaRing("qsp", [ph.enter_context(nc.sbuf_tensor(f"yrb{r}", [128, MH], BF16))
                                  for r in range(4)])
            ost = Ring([ph.enter_context(nc.sbuf_tensor(f"ost{r}", [128, MH], F32))
                        for r in range(2)])
            getsem("ost0"); getsem("ost1")
            ps_m = Ring([ph.enter_context(nc.psum_tensor(f"psm{r}", [128, MH], F32))
                         for r in range(4)])

            def stream_w(ring, src_ap, three_d):
                i, buf, tok = ring.get()
                wait_tok("gpsimd", tok)
                if three_d:
                    dma("gpsimd", buf[:, :].rearrange("p (i f) -> p i f", f=128),
                        src_ap, ring.sem(i))
                else:
                    dma("gpsimd", buf[:, :], src_ap, ring.sem(i))
                return i, buf, (ring.sem(i), cur(ring.sem(i)))

            units = ([] if c.SPARSE else [("e", e) for e in range(c.E)]) + [("s", None)]
            gbuf_tok = [None] * GMAX
            cwb_tok = None
            cwrow_tok = None

            for th in range(c.TH):
                t0 = th * MH
                last_acc = [None] * DC
                first_dn = [True] * DC
                for ukind, ue in units:
                    nfc = c.FC if ukind == "e" else c.FSC
                    if ukind == "e":
                        w_g3 = d_wgT.ap()[ue].rearrange("(i p) f -> p i f", p=128)
                        w_u3 = d_wuT.ap()[ue].rearrange("(i p) f -> p i f", p=128)
                        d_wd2 = d_wdT.ap()[ue]
                    else:
                        w_g3 = d_swgT.ap().rearrange("(i p) f -> p i f", p=128)
                        w_u3 = d_swuT.ap().rearrange("(i p) f -> p i f", p=128)
                        d_wd2 = d_swdT.ap()

                    if ukind == "e":
                        pi, psb, ptok = ps_m.get()
                        wait_tok("tensor", ptok)
                        wait("tensor", "rt_cw", rt_done)
                        ntk_h = MH // 128
                        for jtk in range(ntk_h):
                            tk = th * ntk_h + jtk
                            def f(eng, tk=tk, jtk=jtk, psb=psb, ue=ue):
                                mm = eng.transpose(psb[0:1, jtk * 128:(jtk + 1) * 128],
                                                   sb_cwtm[:, tk * c.E + ue:tk * c.E + ue + 1],
                                                   sb_idf[:, :])
                                mm.then_inc(sems["m_tp"], 1)
                            add("tensor", f)
                            inc("m_tp", 1)
                        wait("vector", "m_tp", cur("m_tp"))
                        wait_tok("vector", cwrow_tok)
                        def f(eng, psb=psb):
                            i0 = eng.tensor_copy(cwrow[0:1, :], psb[0:1, 0:MH])
                            i0.then_inc(sems["m_cb"], 1)
                        add("vector", f)
                        cbr = inc("m_cb", 1)
                        wait("tensor", "m_cb", cbr)
                        def f(eng, psb=psb):
                            for mq in range(NMQ):
                                mm = eng.matmul(psb[:, mq * MQ:(mq + 1) * MQ],
                                                sb_ones_f[0:1, :],
                                                cwrow[0:1, mq * MQ:(mq + 1) * MQ],
                                                start=True, stop=True)
                            mm.then_inc(sems["m_cb"], 1)
                        add("tensor", f)
                        cbb = inc("m_cb", 1)
                        cwrow_tok = ("m_cb", cbb)
                        wait("vector", "m_cb", cbb)
                        wait_tok("vector", cwb_tok)
                        def f(eng, psb=psb):
                            i0 = eng.tensor_copy(cwb[:, :], psb[:, :])
                            i0.then_inc(sems["m_cbe"], 1)
                        add("vector", f)
                        cbe = inc("m_cbe", 1)
                        ps_m.set_tok(pi, ("m_cbe", cbe))

                    for grp in _fc_groups(nfc, GMAX):
                        grp_dw = []
                        for jj, fc in enumerate(grp):
                            gwi, gwb, gwt = stream_w(wg_r, w_g3[:, :, fc * 128:(fc + 1) * 128], True)
                            uwi, uwb, uwt = stream_w(wu_r, w_u3[:, :, fc * 128:(fc + 1) * 128], True)
                            dwi, dwb, dwt = stream_w(wd_r, d_wd2[fc * 128:(fc + 1) * 128, :], False)
                            grp_dw.append((dwi, dwb, dwt))
                            pi, psg, ptok = ps_m.get()
                            wait_tok("tensor", ptok)
                            wait_tok("tensor", gwt)
                            def f(eng, psg=psg, gwb=gwb, t0=t0, h2L=tuple(sb_h2)):
                                for i in range(DC):
                                    for mq in range(NMQ):
                                        mm = eng.matmul(psg[:, mq * MQ:(mq + 1) * MQ],
                                                        gwb[:, i * 128:(i + 1) * 128],
                                                        h2L[i][:, t0 + mq * MQ:t0 + (mq + 1) * MQ],
                                                        start=(i == 0), stop=(i == DC - 1))
                                mm.then_inc(sems["m_gu"], 1)
                            add("tensor", f)
                            gm = inc("m_gu", 1)
                            wg_r.set_tok(gwi, ("m_gu", gm))
                            sgi, sgb, sgtok = sgst.get()
                            wait("scalar", "m_gu", gm)
                            wait_tok("scalar", sgtok)
                            def f(eng, psg=psg, sgb=sgb):
                                a = eng.activation(out=sgb[:, :], in_=psg[:, :], func=AF.Silu)
                                a.then_inc(sems["m_evS"], 1)
                            add("scalar", f)
                            evs = inc("m_evS", 1)
                            ps_m.set_tok(pi, ("m_evS", evs))
                            pi2, psu, ptok2 = ps_m.get()
                            wait_tok("tensor", ptok2)
                            wait_tok("tensor", uwt)
                            def f(eng, psu=psu, uwb=uwb, t0=t0, h2L=tuple(sb_h2)):
                                for i in range(DC):
                                    for mq in range(NMQ):
                                        mm = eng.matmul(psu[:, mq * MQ:(mq + 1) * MQ],
                                                        uwb[:, i * 128:(i + 1) * 128],
                                                        h2L[i][:, t0 + mq * MQ:t0 + (mq + 1) * MQ],
                                                        start=(i == 0), stop=(i == DC - 1))
                                mm.then_inc(sems["m_gu"], 1)
                            add("tensor", f)
                            um = inc("m_gu", 1)
                            wu_r.set_tok(uwi, ("m_gu", um))
                            usi, usb, ustok = usst.get()
                            wait("vector", "m_gu", um)
                            wait_tok("vector", ustok)
                            if ukind == "e":
                                wait("vector", "m_cbe", cur("m_cbe"))
                                def f(eng, psu=psu, usb=usb):
                                    i0 = eng.tensor_mul(usb[:, :], psu[:, :], cwb[:, :])
                                    i0.then_inc(sems["m_evV"], 1)
                                add("vector", f)
                            else:
                                def f(eng, psu=psu, usb=usb):
                                    i0 = eng.tensor_copy(usb[:, :], psu[:, :])
                                    i0.then_inc(sems["m_evV"], 1)
                                add("vector", f)
                            evv = inc("m_evV", 1)
                            ps_m.set_tok(pi2, ("m_evV", evv))
                            wait("vector", "m_evS", evs)
                            wait("vector", "m_evV", evv)
                            wait_tok("vector", gbuf_tok[jj])
                            def f(eng, jj=jj, sgb=sgb, usb=usb):
                                i0 = eng.tensor_mul(gbuf[jj][:, :], sgb[:, :], usb[:, :])
                                i0.then_inc(sems["m_gp"], 1)
                            add("vector", f)
                            gp = inc("m_gp", 1)
                            sgst.set_tok(sgi, ("m_gp", gp))
                            usst.set_tok(usi, ("m_gp", gp))
                        gp_end = cur("m_gp")
                        for hc in range(DC):
                            pi, psd, ptok = ps_m.get()
                            wait_tok("tensor", ptok)
                            for jj, fc in enumerate(grp):
                                dwi, dwb, dwt = grp_dw[jj]
                                if hc == 0:
                                    wait_tok("tensor", dwt)
                                    wait("tensor", "m_gp", gp_end - len(grp) + jj + 1)
                                def f(eng, psd=psd, dwb=dwb, jj=jj, hc=hc,
                                      first=(jj == 0), last=(jj == len(grp) - 1)):
                                    for mq in range(NMQ):
                                        mm = eng.matmul(psd[:, mq * MQ:(mq + 1) * MQ],
                                                        dwb[:, hc * 128:(hc + 1) * 128],
                                                        gbuf[jj][:, mq * MQ:(mq + 1) * MQ],
                                                        start=(first and mq >= 0),
                                                        stop=last)
                                    if last:
                                        mm.then_inc(sems["m_dn"], 1)
                                add("tensor", f)
                            dn = inc("m_dn", 1)
                            wait("vector", "m_dn", dn)
                            wait_tok("vector", last_acc[hc])
                            if first_dn[hc]:
                                first_dn[hc] = False
                                def f(eng, psd=psd, hc=hc):
                                    i0 = eng.tensor_copy(sb_yacc[hc][:, :], psd[:, :])
                                    i0.then_inc(sems["m_acc"], 1)
                            else:
                                def f(eng, psd=psd, hc=hc):
                                    i0 = eng.scalar_tensor_tensor(sb_yacc[hc][:, :], psd[:, :],
                                                                  1.0, sb_yacc[hc][:, :],
                                                                  ALU.mult, ALU.add)
                                    i0.then_inc(sems["m_acc"], 1)
                            add("vector", f)
                            acc_t = inc("m_acc", 1)
                            last_acc[hc] = ("m_acc", acc_t)
                            ps_m.set_tok(pi, ("m_acc", acc_t))
                        for jj in range(len(grp)):
                            dwi, dwb, dwt = grp_dw[jj]
                            wd_r.set_tok(dwi, ("m_dn", cur("m_dn")))
                            gbuf_tok[jj] = ("m_dn", cur("m_dn"))
                    if ukind == "e":
                        cwb_tok = ("m_evV", cur("m_evV"))

                wait("vector", "m_acc", cur("m_acc"))
                if c.SPARSE and th == 0 and True:
                    wait("sync", "eos0", cur("eos0"))
                    wait("sync", "eos1", cur("eos1"))
                    wait("scalar", "eos0", cur("eos0"))
                    wait("scalar", "eos1", cur("eos1"))
                NBLK = MH // 128
                for hc in range(DC):
                    xi, xb, xtok = xo2.get()
                    wait_tok("sync", xtok)
                    xt = dma("sync", xb[:, :], d_x2[hc, :, t0:t0 + MH], xo2.sem(xi))
                    if c.SPARSE:
                        y1i, y1b, y1tok = yrb.get()
                        wait_tok("sync", y1tok)
                        y2i, y2b, y2tok = yrb.get()
                        wait_tok("sync", y2tok)
                        dma("sync", y1b[:, :],
                            d_y1[t0:t0 + MH, hc * 128:(hc + 1) * 128],
                            yrb.sem(y1i), transpose=True)
                        dma("sync", y2b[:, :],
                            d_y2[t0:t0 + MH, hc * 128:(hc + 1) * 128],
                            yrb.sem(y2i), transpose=True)
                        y1t, y2t = cur(yrb.sem(y1i)), cur(yrb.sem(y2i))
                    si, sb_, stok2 = ost.get()
                    wait("vector", xo2.sem(xi), xt)
                    wait_tok("vector", stok2)
                    if not c.SPARSE:
                        def f(eng, xb=xb, sb_=sb_, hc=hc):
                            i0 = eng.tensor_add(sb_[:, :], xb[:, :], sb_yacc[hc][:, :])
                            i0.then_inc(sems["o_ev"], 1)
                        add("vector", f)
                        ev = inc("o_ev", 1)
                    else:
                        wait("vector", yrb.sem(y1i), y1t)
                        wait("vector", yrb.sem(y2i), y2t)
                        vg9 = cur("vch")
                        inc("vch", 2)
                        def f(eng, xb=xb, sb_=sb_, hc=hc, y1b=y1b, y2b=y2b, vg9=vg9):
                            eng.tensor_add(sb_[:, :], xb[:, :], sb_yacc[hc][:, :]).then_inc(sems["vch"], 1)
                            eng.wait_ge(sems["vch"], vg9 + 1)
                            eng.tensor_add(sb_[:, :], sb_[:, :], y1b[:, :]).then_inc(sems["vch"], 1)
                            eng.wait_ge(sems["vch"], vg9 + 2)
                            i0 = eng.tensor_add(sb_[:, :], sb_[:, :], y2b[:, :])
                            i0.then_inc(sems["o_ev"], 1)
                        add("vector", f)
                        ev = inc("o_ev", 1)
                        yrb.set_tok(y1i, ("o_ev", ev))
                        yrb.set_tok(y2i, ("o_ev", ev))
                    xo2.set_tok(xi, ("o_ev", ev))
                    wait("sync", "o_ev", ev)
                    sp = dma("sync", d_out[hc * 128:(hc + 1) * 128, t0:t0 + MH], sb_[:, :],
                             f"ost{si}")
                    ost.set_tok(si, (f"ost{si}", sp))
            wait("sync", "ost0", cur("ost0"))
            wait("sync", "ost1", cur("ost1"))
        ph_b.close()

        with nc.Block() as block:
            def run(fns):
                def g(eng, fns=fns):
                    for f in fns:
                        f(eng)
                return g
            block.sync(run(sched["sync"]))
            block.tensor(run(sched["tensor"]))
            block.vector(run(sched["vector"]))
            block.scalar(run(sched["scalar"]))
            block.gpsimd(run(sched["gpsimd"]))

    return nc



def host_layout(c: CFG, x, Wqkv, Wproj, gw, wg, wu, wd, swg, swu, swd):
    f32 = np.float32
    D, SEQ, HC = c.D, c.SEQ, c.HC
    nheads = 2 * HC
    pi = np.concatenate([np.arange(0, 128, 2), np.arange(1, 128, 2)])

    Wqkv = np.asarray(Wqkv, dtype=f32)
    Wq = Wqkv[0 * D:1 * D].reshape(nheads, 128, D)[:, pi, :]
    Wk = Wqkv[1 * D:2 * D].reshape(nheads, 128, D)[:, pi, :]
    Wv = Wqkv[2 * D:3 * D].reshape(nheads, 128, D)

    pos = np.arange(SEQ, dtype=f32)
    invf = (1.0 / (10000.0 ** (np.arange(0, 128, 2, dtype=f32) / f32(128.0)))).astype(f32)
    ang = invf[:, None] * pos[None, :]
    C64 = np.cos(ang).astype(f32)
    S64 = np.sin(ang).astype(f32)
    ropeC = np.ascontiguousarray(np.concatenate([C64, C64], axis=0))
    ropeS = np.ascontiguousarray(np.concatenate([-S64, S64], axis=0))

    kk = np.arange(128)[:, None]
    qq = np.arange(c.QT)[None, :]
    mask = np.ascontiguousarray(np.concatenate(
        [np.where(kk > qq - 128 * v, f32(NEG_BIG), f32(0.0)) for v in range(c.VR)],
        axis=1).astype(f32))

    common = dict(
        wprojT=np.ascontiguousarray(np.asarray(Wproj, dtype=f32).T),
        ropeC=ropeC, ropeS=ropeS, maskdiag=mask,
        gwT=np.ascontiguousarray(np.asarray(gw, dtype=f32).T),
        wgT=np.ascontiguousarray(np.asarray(wg, dtype=f32).transpose(0, 2, 1)),
        wuT=np.ascontiguousarray(np.asarray(wu, dtype=f32).transpose(0, 2, 1)),
        wdT=np.ascontiguousarray(np.asarray(wd, dtype=f32).transpose(0, 2, 1)),
        swgT=np.ascontiguousarray(np.asarray(swg, dtype=f32).T),
        swuT=np.ascontiguousarray(np.asarray(swu, dtype=f32).T),
        swdT=np.ascontiguousarray(np.asarray(swd, dtype=f32).T),
        onesbf=np.ones((128, 128), dtype=ml_dtypes.bfloat16),
        onesf=np.ones((1, 128), dtype=f32),
        idf=np.ascontiguousarray(np.eye(128, dtype=f32)),
        idbf=np.ascontiguousarray(np.eye(128).astype(ml_dtypes.bfloat16)),
        ltriT=np.ascontiguousarray(
            np.triu(np.ones((128, 128)), k=1).astype(ml_dtypes.bfloat16)),
        iotaT=np.ascontiguousarray(
            (np.arange(c.NTILE_R, dtype=f32) * 128.0).reshape(1, -1)),
        iotaP=np.ascontiguousarray(np.arange(128, dtype=f32).reshape(128, 1)),
        constf=np.ascontiguousarray(np.tile(np.array(
            [[0.0, 1.0, 1e-5, 1.0 / 128.0, 1.0 / math.sqrt(128.0), 0.0, 0.0,
              0.0]], dtype=f32), (128, 1))),
    )

    in_maps = []
    for core in range(c.n_cores):
        b, hh = core // 2, core % 2
        heads = range(hh * HC, (hh + 1) * HC)
        wqkvT = np.concatenate(
            [np.concatenate([Wq[h] for h in heads], axis=0),
             np.concatenate([Wk[h] for h in heads], axis=0),
             np.concatenate([Wv[h] for h in heads], axis=0)], axis=0).T
        xb = np.asarray(x[b], dtype=f32).T
        m = dict(common)
        m["wqkvT"] = np.ascontiguousarray(wqkvT)
        m["xT"] = np.ascontiguousarray(xb)
        m["xownT"] = np.ascontiguousarray(xb[:, hh * c.OWN:(hh + 1) * c.OWN])
        if c.HC >= 8:
            hgroups = [list(range(0, c.HC - 4)),
                       [c.HC - 4, c.HC - 3], [c.HC - 2], [c.HC - 1]]
        elif c.HC >= 2:
            hgroups = [list(range(0, c.HC // 2)), list(range(c.HC // 2, c.HC))]
        else:
            hgroups = [list(range(c.HC))]
        yidx = np.zeros((128, c.DC), dtype=np.int32)
        for i in range(c.DC):
            p_i, lh = i // c.HC, i % c.HC
            g = next(gg for gg, hgg in enumerate(hgroups) if lh in hgg)
            jj = hgroups[g].index(lh)
            s = 2 * (core // 2) + p_i
            base = (s * len(hgroups[g]) + jj) * 128
            yidx[:, i] = base + np.arange(128)
        m["yidx"] = np.ascontiguousarray(yidx)
        in_maps.append(m)
    return in_maps


def gather_out(c: CFG, results, B):
    out = np.empty((B, c.SEQ, c.D), dtype=np.float32)
    for core in range(c.n_cores):
        b, hh = core // 2, core % 2
        out[b, hh * c.OWN:(hh + 1) * c.OWN, :] = np.asarray(results[core]["outT"]).T
    return out


def kernel(L0_attn_x, L0_attn_W_qkv, L0_attn_W_proj, L0_ffn_gate_weight,
           L0_ffn_w_gate, L0_ffn_w_up, L0_ffn_w_down, L0_ffn_sw_gate,
           L0_ffn_sw_up, L0_ffn_sw_down):
    c = CFG()
    x = np.asarray(L0_attn_x, dtype=np.float32)
    nc = build(c)
    in_maps = host_layout(c, x, L0_attn_W_qkv, L0_attn_W_proj, L0_ffn_gate_weight,
                          L0_ffn_w_gate, L0_ffn_w_up, L0_ffn_w_down,
                          L0_ffn_sw_gate, L0_ffn_sw_up, L0_ffn_sw_down)
    res = run_bass_kernel_spmd(nc, in_maps, core_ids=list(range(c.n_cores)))
    return gather_out(c, res.results, x.shape[0])
